# version 15 (final)
# speedup vs baseline: 1.0411x; 1.0263x over previous
import math
from contextlib import ExitStack

import numpy as np
import ml_dtypes
import concourse.bass as bass
import concourse.mybir as mybir
from concourse.bass_utils import run_bass_kernel_spmd

F32 = mybir.dt.float32
BF16 = mybir.dt.bfloat16
AF = mybir.ActivationFunctionType
ALU = mybir.AluOpType
AX = mybir.AxisListType

NCORES = 8
S = 4096
H = 1024
NT = S // 128

HEAD_AT_SLOT = [0, 4, 1, 5, 2, 6, 3, 7, 8, 12, 9, 13, 10, 14, 11, 15]

NEG = -1e30


class Track:

    def __init__(self, ctx, key, sem, step=1, name=None):
        self.ctx = ctx
        self.key = key
        self.name = name or key
        self.sem = sem
        self.step = step
        self.n = 0
        self.last_wait = {}

    def op(self, fn):
        self.n += self.step
        if self.ctx.cur == self.key:
            fn(self.ctx.raw).then_inc(self.sem, self.step)
        return self.n

    def wait(self, other, val):
        if val <= 0 or other is self and val <= self.n - self.step:
            pass
        if val <= 0:
            return
        lw = self.last_wait.get(other.name, -1)
        if val <= lw:
            return
        self.last_wait[other.name] = val
        if self.ctx.cur == self.key:
            self.ctx.raw.wait_ge(other.sem, val)


class Ctx:
    def __init__(self, nc, es):
        self.nc = nc
        self.es = es
        self.cur = None
        self.raw = None
        sem = lambda name: es.enter_context(nc.semaphore(name))
        self.T = Track(self, "t", sem("semT"))
        self.V = Track(self, "v", sem("semV"))
        self.S = Track(self, "s", sem("semS"))
        self.Q = Track(self, "q", sem("semQ"), step=16)
        self.G = Track(self, "g", sem("semG"), step=16)
        self.P = Track(self, "g", sem("semP"), name="p")
        self.tracks = [self.T, self.V, self.S, self.Q, self.G, self.P]
        self._mksem = sem

    def queue(self, name, engine="g"):
        t = Track(self, engine, self._mksem("sem_" + name), step=16, name=name)
        self.tracks.append(t)
        return t

    def barrier(self):
        ends = [(t, t.n) for t in self.tracks]
        for t in self.tracks:
            for o, n in ends:
                t.wait(o, n)

    def run_passes(self, blk, emit):
        saved = [(t.n, dict(t.last_wait)) for t in self.tracks]

        def mk(key):
            def body(raw):
                self.cur = key
                self.raw = raw
                for t, (n0, lw0) in zip(self.tracks, saved):
                    t.n = n0
                    t.last_wait = dict(lw0)
                emit(self)
            return body

        blk.tensor(mk("t"))
        blk.vector(mk("v"))
        blk.scalar(mk("s"))
        blk.sync(mk("q"))
        blk.gpsimd(mk("g"))
        self.cur = None
        self.raw = None


def ts(i, w=128):
    return slice(i * w, (i + 1) * w)



def emit_phase_a(ctx, io, buf, n_tiles=NT, fused=False):
    T, V, Sc, Q, G = ctx.T, ctx.V, ctx.S, ctx.Q, ctx.G
    trk = {"v": V, "s": Sc, "t": T}

    x_in, x1_out = io["x"], io["x1"]

    xin, x1t, ccen, sqscr, cols = (buf[k] for k in
                                   ("xin", "x1t", "ccen", "sqscr", "cols"))
    ln_bf, ln_fm, kfm, vtl, qfm = (buf[k] for k in
                                   ("ln_bf", "ln_fm", "kfm", "vtl", "qfm"))
    stsb = buf["ssb"]
    ptsb = buf["psb"]
    att, attbf, attfm = buf["att"], buf["attbf"], buf["attfm"]
    wq_bf, wk_bf, wv_bf, wo_bf, ident, maskbig = (
        buf[k] for k in ("wq_bf", "wk_bf", "wv_bf", "wo_bf", "ident",
                         "masks"))
    mbc_sb = buf["mbc_sb"]
    ones_col, negones = buf["ones_col"], buf["negones"]
    ksq, qsq = buf["ksq"], buf["qsq"]
    qn2_sb, lq_sb, crow_sb = buf["qn2_sb"], buf["lq_sb"], buf["crow_sb"]
    rsc = buf["rsc"]
    b0, b1, b2, b3, bpv, bmb, bt6, bt7 = (
        buf[k] for k in ("psA", "psB", "ptA", "ptB", "pv", "pk",
                         "pq1", "pq2"))

    VP = 260
    HSW = n_tiles * 128

    n_consts = Q.op(lambda q: q.dma_start(out=ident[:], in_=io["ident"]))
    n_consts = Q.op(lambda q: q.dma_start(out=maskbig[:], in_=io["masks"]))
    n_consts = Q.op(lambda q: q.dma_start(out=ones_col[:], in_=io["ones_c"]))
    n_consts = Q.op(lambda q: q.dma_start(out=negones[:], in_=io["negones"]))
    if fused:
        nW = ctx.nW_a
    else:
        for name, dst, kchunks in (("wq_t", wq_bf, 8), ("wk_t", wk_bf, 8),
                                   ("wv_t", wv_bf, 8), ("wo_t", wo_bf, 8)):
            nW = G.op(lambda g, name=name, dst=dst, kchunks=kchunks:
                      g.dma_start(
                out=dst[:].rearrange("p (k o) -> p k o", k=kchunks),
                in_=io[name].rearrange("(k p) o -> p k o", p=128)))
    V.wait(Q, n_consts)
    T.wait(Q, n_consts)
    T.wait(G, nW)
    n_eps = V.op(lambda v: v.memset(cols[:, 7:8], 1e-5))
    n_vset = V.op(lambda v: v.memset(vtl[:], 1.0))
    n_kn0 = V.op(lambda v: v.memset(rsc[:], 0.0))
    Sc.wait(V, n_eps)

    free = {i: ("v", 0) for i in range(8)}

    for i in range(n_tiles):
        ctx.barrier()
        slot = i % 2
        nload = Q.op(lambda q, i=i, slot=slot: q.dma_start(
            out=xin[:, ts(slot, 1024)], in_=x_in[ts(i), :]))

        V.wait(Q, nload)
        n1 = V.op(lambda v, slot=slot: v.reduce_sum(
            cols[:, 0:1], xin[:, ts(slot, 1024)], axis=AX.X, negate=True))
        V.wait(V, n1)
        n2 = V.op(lambda v: v.tensor_scalar_mul(cols[:, 0:1], cols[:, 0:1],
                                                1.0 / H))
        V.wait(V, n2)
        n3 = V.op(lambda v, slot=slot: v.tensor_scalar_add(
            ccen[:], xin[:, ts(slot, 1024)], cols[:, 0:1]))
        Sc.wait(V, n3)
        n4 = Sc.op(lambda s: s.activation(sqscr[:], ccen[:], AF.Square,
                                          accum_out=cols[:, 1:2]))
        V.wait(Sc, n4)
        n5 = V.op(lambda v: v.tensor_scalar_mul(cols[:, 2:3], cols[:, 1:2],
                                                1.0 / H))
        Sc.wait(V, n5)
        n6 = Sc.op(lambda s: s.activation(cols[:, 3:4], cols[:, 2:3], AF.Ln,
                                          bias=cols[:, 7:8]))
        Sc.wait(Sc, n6)
        n7 = Sc.op(lambda s: s.activation(cols[:, 2:3], cols[:, 3:4], AF.Exp,
                                          scale=-0.5))
        V.wait(Sc, n7)
        n8 = V.op(lambda v: v.tensor_scalar_mul(ln_bf[:], ccen[:],
                                                cols[:, 2:3]))

        T.wait(V, n8)
        for k in range(8):
            pt = (bt6, bt7)[k % 2]
            eng, cnt = free[6 + k % 2]
            T.wait(trk[eng], cnt)
            nt = T.op(lambda t, k=k, pt=pt: t.transpose(
                pt[:].bitcast(BF16)[:, 0:128], ln_bf[:, ts(k)], ident[:]))
            Sc.wait(T, nt)
            nsev = Sc.op(lambda s, k=k, pt=pt: s.copy(
                ln_fm[:, ts(k)], pt[:].bitcast(BF16)[:, 0:128]))
            free[6 + k % 2] = ("s", nsev)
        n_lnfm = nsev

        T.wait(Sc, n_lnfm)
        for pair in range(2):
            eng, cnt = free[0]
            T.wait(trk[eng], cnt)
            for k in range(8):
                nt = T.op(lambda t, k=k, pair=pair: t.matmul(
                    b0[:, 0:128],
                    wk_bf[:, k * 256 + pair * 128: k * 256 + pair * 128
                          + 128],
                    ln_fm[:, ts(k)], start=(k == 0), stop=(k == 7)))
            V.wait(T, nt)
            nv = V.op(lambda v, pair=pair, i=i: v.tensor_copy(
                kfm[pair][:, ts(i)], b0[:, 0:128]))
            free[0] = ("v", nv)
        n_kfm = nv

        eng, cnt = free[1]
        T.wait(trk[eng], cnt)
        for k in range(8):
            nt = T.op(lambda t, k=k: t.matmul(
                b1[:, 0:256], ln_fm[:, ts(k)], wv_bf[:, ts(k, 256)],
                start=(k == 0), stop=(k == 7)))
        V.wait(T, nt)
        V.wait(V, n_vset)
        nv = V.op(lambda v, i=i: v.tensor_copy(
            vtl[:, i * VP: i * VP + 260].rearrange(
                "p (kv c) -> p kv c", kv=4)[:, :, 0:64],
            b1[:, 0:256].rearrange("p (kv c) -> p kv c", kv=4)))
        free[1] = ("v", nv)

        for m in range(8):
            pq = (bt6, bt7)[m % 2]
            eng, cnt = free[6 + m % 2]
            T.wait(trk[eng], cnt)
            for k in range(8):
                nt = T.op(lambda t, k=k, m=m, pq=pq: t.matmul(
                    pq[:, 0:128],
                    wq_bf[:, k * 1024 + m * 128: k * 1024 + m * 128 + 128],
                    ln_fm[:, ts(k)], start=(k == 0), stop=(k == 7)))
            Sc.wait(T, nt)
            nsq = Sc.op(lambda s, m=m, pq=pq: s.activation(
                qfm[:, ts(m)], pq[:, 0:128], AF.Copy, scale=0.125))
            free[6 + m % 2] = ("s", nsq)
        n_qfm = nsq

        Sc.wait(V, n_kfm)
        nks = Sc.op(lambda s, i=i: s.activation(
            ksq[:, 0:128], kfm[0][:, ts(i)], AF.Square))
        nks = Sc.op(lambda s, i=i: s.activation(
            ksq[:, 128:256], kfm[1][:, ts(i)], AF.Square))
        Sc.wait(Sc, n_qfm)
        nqs = Sc.op(lambda s: s.activation(qsq[:], qfm[:], AF.Square))
        nkr = 0
        for kv in range(4):
            base = (kv % 2) * 64
            pb = (bmb, bpv)[kv % 2]
            eng, cnt = free[(5, 4)[kv % 2]]
            T.wait(trk[eng], cnt)
            T.wait(Sc, nks)
            ntk = T.op(lambda t, kv=kv, base=base, pb=pb: t.matmul(
                pb[:].bitcast(F32)[0:1, 0:128],
                ones_col[base:base + 64, 0:1],
                ksq[base:base + 64, (kv // 2) * 128:(kv // 2) * 128 + 128],
                start=True, stop=True))
            V.wait(T, ntk)
            V.wait(V, nkr)
            nkr = V.op(lambda v, kv=kv, pb=pb: v.reduce_max(
                rsc[0:1, 8 + kv: 9 + kv], pb[:].bitcast(F32)[0:1, 0:128],
                axis=AX.X))
            free[(5, 4)[kv % 2]] = ("v", nkr)
        V.wait(V, nkr)
        nkm = V.op(lambda v: v.tensor_max(rsc[0:1, 0:4], rsc[0:1, 0:4],
                                          rsc[0:1, 8:12]))
        Sc.wait(V, nkm)
        nlk = Sc.op(lambda s: s.activation(rsc[0:1, 12:16], rsc[0:1, 0:4],
                                           AF.Ln))
        V.wait(Sc, nlk)
        nbias = V.op(lambda v: v.tensor_scalar_mul(rsc[0:1, 4:8],
                                                   rsc[0:1, 12:16], 0.5))
        nqn = 0
        for j in range(16):
            base = (j % 2) * 64
            m = j // 2
            pb = (bmb, bpv)[j % 2]
            eng, cnt = free[(5, 4)[j % 2]]
            T.wait(trk[eng], cnt)
            T.wait(Sc, nqs)
            ntq = T.op(lambda t, j=j, base=base, m=m, pb=pb: t.matmul(
                pb[:].bitcast(F32)[0:1, 0:128],
                ones_col[base:base + 64, 0:1],
                qsq[base:base + 64, m * 128: m * 128 + 128],
                start=True, stop=True))
            Sc.wait(T, ntq)
            Sc.wait(Sc, nqn)
            nqn = Sc.op(lambda s, j=j, pb=pb: s.copy(
                qn2_sb[0:1, j * 128: j * 128 + 128],
                pb[:].bitcast(F32)[0:1, 0:128]))
            free[(5, 4)[j % 2]] = ("s", nqn)
        Sc.wait(Sc, nqn)
        nlq = Sc.op(lambda s: s.activation(lq_sb[:], qn2_sb[:], AF.Ln))

        klen = (i + 1) * 128
        pend = None
        nexp_hs = [0, 0]
        ntv_hs = [0, 0]
        prev_crow = 0

        def emit_pv(st):
            (j, kv, hs, n_evst) = st
            Sc.wait(V, n_evst)
            Sc.wait(T, ntv_hs[hs])
            nexp = Sc.op(lambda s, hs=hs: s.activation(
                ptsb[:, hs * HSW: hs * HSW + klen],
                stsb[:, hs * HSW: hs * HSW + klen], AF.Exp))
            nexp_hs[hs] = nexp
            eng, cnt = free[4]
            T.wait(trk[eng], cnt)
            T.wait(Sc, nexp)
            for kb in range(i + 1):
                ntv = T.op(lambda t, kb=kb, kv=kv, hs=hs: t.matmul(
                    bpv[:, 0:65],
                    ptsb[:, hs * HSW + kb * 128:
                         hs * HSW + kb * 128 + 128],
                    vtl[:, kb * VP + kv * 65: kb * VP + kv * 65 + 65],
                    start=(kb == 0), stop=(kb == i)))
            ntv_hs[hs] = ntv
            V.wait(T, ntv)
            nrl = V.op(lambda v: v.reciprocal(cols[:, 8:9], bpv[:, 64:65]))
            V.wait(V, nrl)
            nat = V.op(lambda v, j=j: v.tensor_scalar_mul(
                att[:, ts(j, 64)], bpv[:, 0:64], cols[:, 8:9]))
            free[4] = ("v", nat)
            return nat

        for j in range(16):
            h = HEAD_AT_SLOT[j]
            kv = h // 4
            qb = (j % 2) * 64
            m = j // 2
            pairt = kfm[kv // 2]
            kb_ = (kv % 2) * 64
            hs = j % 2
            Sc.wait(Sc, nlq)
            Sc.wait(V, nbias)
            Sc.wait(T, prev_crow)
            ncr = Sc.op(lambda s, j=j, kv=kv: s.activation(
                crow_sb[0:1, j * 128: j * 128 + 128],
                lq_sb[0:1, j * 128: j * 128 + 128], AF.Exp, scale=0.5,
                bias=rsc[0:1, 4 + kv: 5 + kv]))
            eng, cnt = free[5]
            T.wait(trk[eng], cnt)
            T.wait(Sc, ncr)
            ntb = T.op(lambda t, j=j: t.matmul(
                bmb[:, 0:512], negones[:],
                crow_sb[0:1, j * 128: j * 128 + 128].unsqueeze(1
                    ).broadcast_to([1, 4, 128]),
                start=True, stop=True))
            prev_crow = ntb
            V.wait(T, ntb)
            V.wait(V, V.n)
            nmc = V.op(lambda v: v.tensor_copy(mbc_sb[:, 0:512],
                                               bmb[:, 0:512]))
            free[5] = ("v", nmc)
            V.wait(V, nmc)
            nmd = V.op(lambda v: v.tensor_sub(
                mbc_sb[:, 512:640], mbc_sb[:, 0:128], maskbig[:]))
            n_evst = 0
            T.wait(Sc, n_qfm)
            T.wait(V, n_kfm)
            for g0 in range(0, i + 1, 4):
                gn = min(4, i + 1 - g0)
                gidx = (g0 // 4) % 2
                ps = (b2, b3)[gidx]
                eng, cnt = free[2 + gidx]
                T.wait(trk[eng], cnt)
                for kb in range(g0, g0 + gn):
                    nt = T.op(lambda t, kb=kb, g0=g0, ps=ps, qb=qb, m=m,
                              pairt=pairt, kb_=kb_: t.matmul(
                        ps[:, (kb - g0) * 128: (kb - g0) * 128 + 128],
                        pairt[kb_:kb_ + 64, ts(kb)],
                        qfm[qb:qb + 64, ts(m)],
                        start=True, stop=True))
                V.wait(T, nt)
                V.wait(V, nmd)
                V.wait(Sc, nexp_hs[hs])
                ndiag = 1 if g0 + gn - 1 == i else 0
                if gn - ndiag > 0:
                    n_evst = V.op(lambda v, g0=g0, gn=gn, nd=ndiag, ps=ps,
                                  hs=hs: v.scalar_tensor_tensor(
                        stsb[:, hs * HSW + g0 * 128:
                             hs * HSW + (g0 + gn - nd) * 128],
                        ps[:, 0:(gn - nd) * 128], 1.0,
                        mbc_sb[:, 0:(gn - nd) * 128], ALU.mult, ALU.add))
                if ndiag:
                    n_evst = V.op(lambda v, g0=g0, gn=gn, ps=ps, hs=hs, i=i:
                                  v.scalar_tensor_tensor(
                        stsb[:, hs * HSW + i * 128:
                             hs * HSW + i * 128 + 128],
                        ps[:, (gn - 1) * 128: gn * 128], 1.0,
                        mbc_sb[:, 512:640], ALU.mult, ALU.add))
                free[2 + gidx] = ("v", n_evst)
            cur = (j, kv, hs, n_evst)
            if pend is not None:
                emit_pv(pend)
            pend = cur
        nat = emit_pv(pend)
        n_att = nat

        V.wait(V, n_att)
        nbf = V.op(lambda v: v.tensor_copy(attbf[:], att[:]))
        T.wait(V, nbf)
        for k in range(8):
            pt = (bt6, bt7)[k % 2]
            eng, cnt = free[6 + k % 2]
            T.wait(trk[eng], cnt)
            nt = T.op(lambda t, k=k, pt=pt: t.transpose(
                pt[:].bitcast(BF16)[:, 0:128], attbf[:, ts(k)], ident[:]))
            Sc.wait(T, nt)
            nsev = Sc.op(lambda s, k=k, pt=pt: s.copy(
                attfm[:, ts(k)], pt[:].bitcast(BF16)[:, 0:128]))
            free[6 + k % 2] = ("s", nsev)
        n_attfm = nsev
        for half in range(2):
            ps = (b0, b1)[half]
            eng, cnt = free[half]
            T.wait(trk[eng], cnt)
            T.wait(Sc, n_attfm)
            for k in range(8):
                nt = T.op(lambda t, k=k, half=half, ps=ps: t.matmul(
                    ps[:, 0:512], attfm[:, ts(k)],
                    wo_bf[:, k * 1024 + half * 512: k * 1024 + half * 512
                          + 512],
                    start=(k == 0), stop=(k == 7)))
            V.wait(T, nt)
            nv = V.op(lambda v, half=half, slot=slot, ps=ps: v.tensor_tensor(
                x1t[:, half * 512: half * 512 + 512], ps[:, 0:512],
                xin[:, slot * 1024 + half * 512: slot * 1024 + half * 512
                    + 512],
                ALU.add))
            free[half] = ("v", nv)
        Q.wait(V, nv)
        nst = Q.op(lambda q, i=i: q.dma_start(
            out=x1_out[ts(i), :], in_=x1t[:]))

    Q.wait(Q, nst)


def build_phase_a(n_tiles=NT):
    nc = bass.Bass("TRN2", target_bir_lowering=False)
    Sx = n_tiles * 128
    io = {
        "x": nc.dram_tensor("x", [Sx, H], F32, kind="ExternalInput").ap(),
        "wq_t": nc.dram_tensor("wq_t", [H, 1024], F32, kind="ExternalInput").ap(),
        "wk_t": nc.dram_tensor("wk_t", [H, 256], F32, kind="ExternalInput").ap(),
        "wv_t": nc.dram_tensor("wv_t", [H, 256], F32, kind="ExternalInput").ap(),
        "wo_t": nc.dram_tensor("wo_t", [H, 1024], F32, kind="ExternalInput").ap(),
        "ident": nc.dram_tensor("ident", [128, 128], BF16, kind="ExternalInput").ap(),
        "masks": nc.dram_tensor("masks", [128, 128], F32, kind="ExternalInput").ap(),
        "ones_c": nc.dram_tensor("ones_c", [128, 1], BF16, kind="ExternalInput").ap(),
        "negones": nc.dram_tensor("negones", [1, 128], BF16, kind="ExternalInput").ap(),
        "x1": nc.dram_tensor("x1", [Sx, H], F32, kind="ExternalOutput").ap(),
    }
    es = ExitStack()
    with es:
        sb = lambda name, w, dt=F32, p=128: es.enter_context(
            nc.sbuf_tensor(name, [p, w], dt))
        psum = lambda name, w, dt=F32: es.enter_context(
            nc.psum_tensor(name, [128, w], dt))
        buf = {
            "xin": sb("xin", 2048),
            "x1t": sb("x1t", 1024),
            "ccen": sb("ccen", 1024),
            "sqscr": sb("sqscr", 1024),
            "cols": sb("cols", 24),
            "ln_bf": sb("ln_bf", 1024, BF16),
            "ln_fm": sb("ln_fm", 1024, BF16),
            "kfm": [sb("kfm0", n_tiles * 128, BF16),
                    sb("kfm1", n_tiles * 128, BF16)],
            "vtl": sb("vtl", n_tiles * 260, BF16),
            "qfm": sb("qfm", 1024, BF16),
            "ssb": sb("ssb", 2 * n_tiles * 128),
            "psb": sb("psb", 2 * n_tiles * 128, BF16),
            "mbc_sb": sb("mbc_sb", 640),
            "ones_col": sb("ones_colsb", 1, BF16),
            "negones": sb("negonessb", 128, BF16, 1),
            "ksq": sb("ksq", 256, BF16),
            "qsq": sb("qsq", 1024, BF16),
            "qn2_sb": sb("qn2_sb", 2048, F32, 1),
            "lq_sb": sb("lq_sb", 2048, F32, 1),
            "crow_sb": sb("crow_sb", 2048, BF16, 1),
            "rsc": sb("rsc", 16, F32, 1),
            "ones_r": sb("ones_r", 128, BF16, 1),
            "att": sb("att", 1024),
            "attbf": sb("attbf", 1024, BF16),
            "attfm": sb("attfm", 1024, BF16),
            "wq_bf": sb("wq_bf", 8192, BF16),
            "wk_bf": sb("wk_bf", 2048, BF16),
            "wv_bf": sb("wv_bf", 2048, BF16),
            "wo_bf": sb("wo_bf", 8192, BF16),
            "ident": sb("identsb", 128, BF16),
            "masks": sb("maskssb", 128),
            "psA": psum("psA", 512),
            "psB": psum("psB", 512),
            "ptA": psum("ptA", 512),
            "ptB": psum("ptB", 512),
            "pv": psum("pv", 512),
            "pk": psum("pk", 512),
            "pq1": psum("pq1", 512),
            "pq2": psum("pq2", 512),
        }
        ctx = Ctx(nc, es)
        blk = es.enter_context(nc.Block())
        ctx.run_passes(blk, lambda c: emit_phase_a(c, io, buf, n_tiles))
    return nc




def emit_phase_b(ctx, io, buf, n_tt=8, dbg=False, fused=False):
    T, V, Sc, Q, G = ctx.T, ctx.V, ctx.S, ctx.Q, ctx.G
    GWI, GWG, GWO = ctx.GWI, ctx.GWG, ctx.GWO
    trk = {"v": V, "s": Sc, "t": T}
    E = 8

    x1_in, x2_out = io["x1"], io["x2"]
    scr_wi, scr_wg, scr_wo = io["scr_wi"], io["scr_wg"], io["scr_wo"]

    (x1t, ln2_sb, ln2_fm, lnbf, g_buf, h_fm, sgt, wi_sb, acc,
     r_sb, gb_sb, bb_sb, ctile, cols, sqscr, bic, bgc, rowb, brf, brb,
     wr_bf, ident, ones) = (buf[k] for k in (
        "x1t", "ln2_sb", "ln2_fm", "lnbf", "g_buf", "h_fm", "sgt", "wi_sb",
        "acc", "r_sb", "gb_sb", "bb_sb", "ctile", "cols",
        "sqscr", "bic", "bgc", "rowb", "brf", "brb", "wr_bf",
        "ident", "ones"))
    wg_sb, wo_sb = buf["wg_sb"], buf["wo_sb"]
    pr = buf["pr"]

    free = {i: ("v", 0) for i in range(8)}

    if not fused:
        for e in range(E):
            G.op(lambda g, e=e: g.dma_start(out=scr_wi[e], in_=io["Wi"][e]))
            G.op(lambda g, e=e: g.dma_start(out=scr_wg[e], in_=io["Wg"][e]))
            G.op(lambda g, e=e: g.dma_start(out=scr_wo[e], in_=io["Wo"][e]))

    nc1 = Q.op(lambda q: q.dma_start(out=ident[:], in_=io["ident"]))
    nc1 = Q.op(lambda q: q.dma_start(out=ones[:], in_=io["ones1"]))
    nc1 = Q.op(lambda q: q.dma_start(out=bic[:], in_=io["bic"]))
    nc1 = Q.op(lambda q: q.dma_start(out=bgc[:], in_=io["bgc"]))
    nc1 = Q.op(lambda q: q.dma_start(out=brf[:], in_=io["br"]))
    nG = G.op(lambda g: g.dma_start(
        out=wr_bf[:].rearrange("p (k e) -> p k e", k=8),
        in_=io["Wr"].rearrange("(k p) e -> p k e", p=128)))
    n_g_setup = nG
    V.wait(Q, nc1)
    nbr = V.op(lambda v: v.tensor_copy(brb[:], brf[:]))
    neps = V.op(lambda v: v.memset(cols[:, 14:15], 1e-5))
    neps = V.op(lambda v: v.memset(cols[:, 15:16], 1e-12))
    Sc.wait(V, neps)
    T.wait(Q, nc1)
    T.wait(G, nG)
    T.wait(V, nbr)

    last_store = 0
    for tt in range(n_tt):
        ctx.barrier()
        nload = Q.op(lambda q, tt=tt: q.dma_start(
            out=x1t[:].rearrange("p (tc h) -> p tc h", tc=4),
            in_=x1_in[tt * 512:(tt + 1) * 512, :].rearrange(
                "(tc p) h -> p tc h", p=128)))
        V.wait(Q, nload)

        nse = 0
        for tc in range(4):
            xi = x1t[:, tc * 1024:(tc + 1) * 1024]
            lni = ln2_sb[:, tc * 1024:(tc + 1) * 1024]
            n1 = V.op(lambda v, xi=xi: v.reduce_sum(
                cols[:, 0:1], xi, axis=AX.X, negate=True))
            V.wait(V, n1)
            n2 = V.op(lambda v: v.tensor_scalar_mul(cols[:, 0:1],
                                                    cols[:, 0:1], 1.0 / H))
            V.wait(V, n2)
            V.wait(Q, n_lnst)
            n3 = V.op(lambda v, xi=xi, lni=lni: v.tensor_scalar_add(
                lni, xi, cols[:, 0:1]))
            Sc.wait(V, n3)
            n4 = Sc.op(lambda s, lni=lni: s.activation(
                sqscr[:], lni, AF.Square, accum_out=cols[:, 1:2]))
            V.wait(Sc, n4)
            n5 = V.op(lambda v: v.tensor_scalar_mul(cols[:, 2:3],
                                                    cols[:, 1:2], 1.0 / H))
            Sc.wait(V, n5)
            n6 = Sc.op(lambda s: s.activation(cols[:, 3:4], cols[:, 2:3],
                                              AF.Ln, bias=cols[:, 14:15]))
            Sc.wait(Sc, n6)
            n7 = Sc.op(lambda s: s.activation(cols[:, 2:3], cols[:, 3:4],
                                              AF.Exp, scale=-0.5))
            V.wait(Sc, n7)
            n8 = V.op(lambda v, lni=lni: v.tensor_scalar_mul(
                lnbf[:], lni, cols[:, 2:3]))
            V.wait(V, n8)
            n9 = V.op(lambda v, lni=lni: v.tensor_scalar_mul(
                lni, lni, cols[:, 2:3]))
            T.wait(V, n8)
            for k in range(8):
                p0 = pr[k % 2]
                eng, cnt = free[k % 2]
                T.wait(trk[eng], cnt)
                nt = T.op(lambda t, k=k, p0=p0: t.transpose(
                    p0[:].bitcast(BF16)[:, 0:128], lnbf[:, ts(k)], ident[:]))
                Sc.wait(T, nt)
                nse = Sc.op(lambda s, k=k, tc=tc, p0=p0: s.copy(
                    ln2_fm[:, k * 512 + tc * 128: k * 512 + tc * 128 + 128],
                    p0[:].bitcast(BF16)[:, 0:128]))
                free[k % 2] = ("s", nse)
        n_lnfm = nse
        if dbg and tt == 0:
            Q.wait(V, n9)
            Q.op(lambda q: q.dma_start(out=io["dbg_ln"], in_=ln2_sb[:]))
            Q.wait(Sc, n_lnfm)
            Q.op(lambda q: q.dma_start(out=io["dbg_lnfm"], in_=ln2_fm[:]))

        prev_nC = 0
        for tc in range(4):
            eng, cnt = free[4]
            T.wait(trk[eng], cnt)
            T.wait(Sc, n_lnfm)
            for k in range(8):
                nt = T.op(lambda t, k=k, tc=tc: t.matmul(
                    pr[4][:, 0:8],
                    ln2_fm[:, k * 512 + tc * 128: k * 512 + tc * 128 + 128],
                    wr_bf[:, k * 8: k * 8 + 8], start=(k == 0), stop=False))
            nt = T.op(lambda t: t.matmul(
                pr[4][:, 0:8], ones[:], brb[:], start=False, stop=True))
            V.wait(T, nt)
            V.wait(Sc, prev_nC)
            nA = V.op(lambda v: v.reduce_max(cols[:, 4:5], pr[4][:, 0:8],
                                             axis=AX.X, negate=True))
            V.wait(V, nA)
            nB = V.op(lambda v: v.tensor_scalar_add(ctile[:, 48:56],
                                                    pr[4][:, 0:8],
                                                    cols[:, 4:5]))
            free[4] = ("v", nB)
            Sc.wait(V, nB)
            nC = Sc.op(lambda s: s.activation(ctile[:, 40:48], ctile[:, 48:56],
                                              AF.Exp))
            prev_nC = nC
            V.wait(Sc, nC)
            nD = V.op(lambda v: v.reduce_max(cols[:, 5:6], ctile[:, 40:48],
                                             axis=AX.X))
            V.wait(V, nD)
            nE = V.op(lambda v: v.tensor_single_scalar(
                ctile[:, 32:40], ctile[:, 40:48], cols[:, 5:6], ALU.is_equal))
            V.wait(V, nE)
            nF = V.op(lambda v: v.scalar_tensor_tensor(
                ctile[:, 32:40], ctile[:, 32:40], NEG, ctile[:, 40:48],
                ALU.mult, ALU.add))
            V.wait(V, nF)
            nGm = V.op(lambda v: v.reduce_max(cols[:, 6:7], ctile[:, 32:40],
                                              axis=AX.X))
            V.wait(V, nGm)
            nH = V.op(lambda v: v.tensor_single_scalar(
                ctile[:, 32:40], ctile[:, 40:48], cols[:, 6:7], ALU.is_ge))
            nI = V.op(lambda v: v.tensor_add(cols[:, 7:8], cols[:, 5:6],
                                             cols[:, 6:7]))
            V.wait(V, nI)
            nJ = V.op(lambda v: v.reciprocal(cols[:, 7:8], cols[:, 7:8]))
            V.wait(V, nJ)
            nK = V.op(lambda v: v.tensor_mul(ctile[:, 32:40], ctile[:, 32:40],
                                             ctile[:, 40:48]))
            V.wait(V, nK)
            nL = V.op(lambda v, tc=tc: v.tensor_scalar_mul(
                ctile[:, tc * 8:(tc + 1) * 8], ctile[:, 32:40], cols[:, 7:8]))
            nM = V.op(lambda v, tc=tc: v.tensor_copy(
                acc[:, tc * 1024:(tc + 1) * 1024],
                x1t[:, tc * 1024:(tc + 1) * 1024]))
        n_accinit = nM

        if dbg and tt == 0:
            Q.wait(V, nM)
            Q.op(lambda q: q.dma_start(out=io["dbg_c"], in_=ctile[:, 0:32]))

        wlast_wi = [0, 0]
        wlast_wg = [0, 0]
        wlast_wo = [0, 0]
        sg_use = [0, 0]
        sq_use = 0
        rlast = [0, 0]
        n_rmm_prev = 0
        n_h_prev = 0
        for e in range(E):
            G.wait(G, n_g_setup)
            Sc.wait(V, n_h_prev)
            for fc in range(8):
                slot = fc % 2
                GWI[slot].wait(T, wlast_wi[slot])
                nwld = GWI[slot].op(lambda g, e=e, fc=fc, slot=slot: g.dma_start(
                    out=wi_sb[:, slot * 4096:(slot + 1) * 4096].rearrange(
                        "p (k f) -> p k f", k=8),
                    in_=scr_wi[e].rearrange("(k p) f -> p k f", p=128)[
                        :, :, fc * 512:(fc + 1) * 512]))
                T.wait(GWI[slot], nwld)
                for fb in range(4):
                    f = fc * 4 + fb
                    pp = pr[f % 4]
                    eng, cnt = free[f % 4]
                    T.wait(trk[eng], cnt)
                    for k in range(8):
                        nt = T.op(lambda t, k=k, fb=fb, slot=slot, pp=pp:
                                  t.matmul(
                            pp[:, 0:512],
                            wi_sb[:, slot * 4096 + k * 512 + fb * 128:
                                  slot * 4096 + k * 512 + fb * 128 + 128],
                            ln2_fm[:, ts(k, 512)],
                            start=(k == 0), stop=(k == 7)))
                    Sc.wait(T, nt)
                    nse = Sc.op(lambda s, f=f, e=e, pp=pp: s.activation(
                        g_buf[:, ts(f, 512)], pp[:, 0:512], AF.Gelu,
                        bias=bic[:, e * 32 + f: e * 32 + f + 1]))
                    free[f % 4] = ("s", nse)
                wlast_wi[slot] = nt

            V.wait(T, n_rmm_prev)
            nvh = 0
            for fc in range(8):
                slot = fc % 2
                GWG[slot].wait(T, wlast_wg[slot])
                nwld = GWG[slot].op(lambda g, e=e, fc=fc, slot=slot: g.dma_start(
                    out=wg_sb[:, slot * 4096:(slot + 1) * 4096].rearrange(
                        "p (k f) -> p k f", k=8),
                    in_=scr_wg[e].rearrange("(k p) f -> p k f", p=128)[
                        :, :, fc * 512:(fc + 1) * 512]))
                T.wait(GWG[slot], nwld)
                for fb in range(4):
                    f = fc * 4 + fb
                    fs = f % 2
                    pp = pr[f % 4]
                    eng, cnt = free[f % 4]
                    T.wait(trk[eng], cnt)
                    for k in range(8):
                        nt = T.op(lambda t, k=k, fb=fb, slot=slot, pp=pp:
                                  t.matmul(
                            pp[:, 0:512],
                            wg_sb[:, slot * 4096 + k * 512 + fb * 128:
                                  slot * 4096 + k * 512 + fb * 128 + 128],
                            ln2_fm[:, ts(k, 512)],
                            start=(k == 0), stop=(k == 7)))
                    Sc.wait(T, nt)
                    Sc.wait(V, sg_use[fs])
                    nse = Sc.op(lambda s, f=f, e=e, pp=pp, fs=fs:
                                s.activation(
                        sgt[:, fs * 512:(fs + 1) * 512], pp[:, 0:512],
                        AF.Sigmoid,
                        bias=bgc[:, e * 32 + f: e * 32 + f + 1]))
                    free[f % 4] = ("s", nse)
                    V.wait(Sc, nse)
                    nvh = V.op(lambda v, f=f, fs=fs: v.tensor_mul(
                        h_fm[:, ts(f, 512)], g_buf[:, ts(f, 512)],
                        sgt[:, fs * 512:(fs + 1) * 512]))
                    sg_use[fs] = nvh
                wlast_wg[slot] = nt
            n_h = nvh
            n_h_prev = nvh

            if dbg and tt == 0 and e == 0:
                Q.wait(V, n_h)
                Q.op(lambda q: q.dma_start(out=io["dbg_g"], in_=g_buf[:]))
                Q.op(lambda q: q.dma_start(out=io["dbg_h"], in_=h_fm[:]))

            G.wait(T, n_rmm_prev)
            G.wait(G, G.n)
            nrow = G.op(lambda g, e=e: g.dma_start(
                out=rowb[:].rearrange("o (a h) -> o a h", a=3),
                in_=io["rows"][:, e, :].unsqueeze(0)))
            T.wait(G, nrow)
            for which, dst in ((0, gb_sb), (1, bb_sb)):
                for half in range(2):
                    bank = 6 + which
                    eng, cnt = free[bank]
                    T.wait(trk[eng], cnt)
                    nt = T.op(lambda t, which=which, half=half, bank=bank:
                              t.matmul(pr[bank][:, 0:512], ones[:],
                                       rowb[0:1, 1024 + which * 1024
                                            + half * 512: 1024 + which * 1024
                                            + half * 512 + 512],
                                       start=True, stop=True))
                    V.wait(T, nt)
                    nv = V.op(lambda v, dst=dst, half=half, bank=bank:
                              v.tensor_copy(dst[:, half * 512:
                                                half * 512 + 512],
                                            pr[bank][:, 0:512]))
                    free[bank] = ("v", nv)

            if dbg and tt == 0 and e == 0:
                Q.wait(V, nv)
                Q.op(lambda q: q.dma_start(out=io["dbg_gb"], in_=gb_sb[:]))
                Q.op(lambda q: q.dma_start(out=io["dbg_bb"], in_=bb_sb[:]))

            T.wait(V, n_h)
            for b_ in range(8):
                eng, cnt = free[b_]
                T.wait(trk[eng], cnt)
            for fg in range(8):
                slot = fg % 2
                GWO[slot].wait(T, wlast_wo[slot])
                nwld = GWO[slot].op(lambda g, e=e, fg=fg, slot=slot: g.dma_start(
                    out=wo_sb[:, slot * 4096:(slot + 1) * 4096].rearrange(
                        "p (g o) -> p g o", g=4),
                    in_=scr_wo[e][fg * 512:(fg + 1) * 512, :].rearrange(
                        "(g p) o -> p g o", p=128)))
                T.wait(GWO[slot], nwld)
                for tc in range(4):
                    for fb in range(4):
                        f = fg * 4 + fb
                        for half in range(2):
                            nt = T.op(lambda t, f=f, tc=tc, fb=fb, half=half,
                                      slot=slot: t.matmul(
                                pr[tc * 2 + half][:, 0:512],
                                h_fm[:, f * 512 + tc * 128:
                                     f * 512 + tc * 128 + 128],
                                wo_sb[:, slot * 4096 + fb * 1024 + half * 512:
                                      slot * 4096 + fb * 1024 + half * 512
                                      + 512],
                                start=(fg == 0 and fb == 0), stop=False))
                wlast_wo[slot] = nt
            for tc in range(4):
                for half in range(2):
                    nt = T.op(lambda t, tc=tc, half=half, e=e: t.matmul(
                        pr[tc * 2 + half][:, 0:512], ones[:],
                        rowb[0:1, half * 512: half * 512 + 512],
                        start=False, stop=True))
            n_rmm = nt
            n_rmm_prev = nt
            for tc in range(4):
                V.wait(T, n_rmm)
                rsl = tc % 2
                rs = r_sb[:, rsl * 1024: rsl * 1024 + 1024]
                V.wait(V, rlast[rsl])
                for half in range(2):
                    nv1 = V.op(lambda v, tc=tc, half=half, rs=rs: v.tensor_add(
                        rs[:, half * 512: half * 512 + 512],
                        pr[tc * 2 + half][:, 0:512],
                        ln2_sb[:, tc * 1024 + half * 512:
                               tc * 1024 + half * 512 + 512]))
                    free[tc * 2 + half] = ("v", nv1)
                V.wait(V, nv1)
                if dbg and tt == 0 and e == 0 and tc == 0:
                    Q.wait(V, nv1)
                    ndbg = Q.op(lambda q, rs=rs: q.dma_start(
                        out=io["dbg_r"], in_=rs))
                    V.wait(Q, ndbg)
                n1 = V.op(lambda v, rs=rs: v.reduce_sum(
                    cols[:, 8:9], rs, axis=AX.X, negate=True))
                V.wait(V, n1)
                n2 = V.op(lambda v: v.tensor_scalar_mul(cols[:, 8:9],
                                                        cols[:, 8:9], 1.0 / H))
                V.wait(V, n2)
                Sc.wait(V, n2)
                Sc.wait(V, sq_use)
                n4 = Sc.op(lambda s, rs=rs: s.activation(
                    sqscr[:], rs, AF.Square, bias=cols[:, 8:9],
                    accum_out=cols[:, 9:10]))
                V.wait(Sc, n4)
                n5 = V.op(lambda v: v.tensor_scalar_mul(cols[:, 10:11],
                                                        cols[:, 9:10],
                                                        1.0 / H))
                Sc.wait(V, n5)
                n6 = Sc.op(lambda s: s.activation(cols[:, 11:12],
                                                  cols[:, 10:11], AF.Ln,
                                                  bias=cols[:, 15:16]))
                Sc.wait(Sc, n6)
                n7 = Sc.op(lambda s: s.activation(cols[:, 10:11],
                                                  cols[:, 11:12], AF.Exp,
                                                  scale=-0.5))
                V.wait(Sc, n7)
                n9 = V.op(lambda v, tc=tc, e=e: v.tensor_mul(
                    cols[:, 12:13], ctile[:, tc * 8 + e: tc * 8 + e + 1],
                    cols[:, 10:11]))
                V.wait(V, n9)
                nta = V.op(lambda v, rs=rs: v.tensor_scalar(
                    sqscr[:], rs, cols[:, 8:9], cols[:, 12:13],
                    ALU.add, ALU.mult))
                rlast[rsl] = nta
                V.wait(V, nta)
                ntb = V.op(lambda v: v.tensor_mul(sqscr[:], sqscr[:],
                                                  gb_sb[:]))
                V.wait(V, ntb)
                V.wait(V, n_accinit)
                ntc_ = V.op(lambda v, tc=tc: v.tensor_add(
                    acc[:, tc * 1024:(tc + 1) * 1024],
                    acc[:, tc * 1024:(tc + 1) * 1024], sqscr[:]))
                sq_use = ntc_
                V.wait(V, ntc_)
                ntd = V.op(lambda v, tc=tc, e=e: v.scalar_tensor_tensor(
                    acc[:, tc * 1024:(tc + 1) * 1024], bb_sb[:],
                    ctile[:, tc * 8 + e: tc * 8 + e + 1],
                    acc[:, tc * 1024:(tc + 1) * 1024], ALU.mult, ALU.add))

        ctx.barrier()
        last_store = Q.op(lambda q, tt=tt: q.dma_start(
            out=x2_out[tt * 512:(tt + 1) * 512, :].rearrange(
                "(tc p) h -> p tc h", p=128),
            in_=acc[:].rearrange("p (tc h) -> p tc h", tc=4)))

    Q.wait(Q, last_store)


def build_phase_b(n_tt=8, dbg=False):
    nc = bass.Bass("TRN2", target_bir_lowering=False)
    Sx = n_tt * 512
    E, F = 8, 4096
    io = {
        "x1": nc.dram_tensor("x1", [Sx, H], F32, kind="ExternalInput").ap(),
        "Wi": nc.dram_tensor("Wi", [E, H, F], F32, kind="ExternalInput").ap(),
        "Wg": nc.dram_tensor("Wg", [E, H, F], F32, kind="ExternalInput").ap(),
        "Wo": nc.dram_tensor("Wo", [E, F, H], F32, kind="ExternalInput").ap(),
        "Wr": nc.dram_tensor("Wr", [H, E], F32, kind="ExternalInput").ap(),
        "bic": nc.dram_tensor("bic", [128, 256], F32, kind="ExternalInput").ap(),
        "bgc": nc.dram_tensor("bgc", [128, 256], F32, kind="ExternalInput").ap(),
        "rows": nc.dram_tensor("rows", [3, E, 1024], F32,
                               kind="ExternalInput").ap(),
        "br": nc.dram_tensor("br", [1, 8], F32, kind="ExternalInput").ap(),
        "ident": nc.dram_tensor("ident", [128, 128], BF16,
                                kind="ExternalInput").ap(),
        "ones1": nc.dram_tensor("ones1", [1, 128], BF16,
                                kind="ExternalInput").ap(),
        "x2": nc.dram_tensor("x2", [Sx, H], F32, kind="ExternalOutput").ap(),
        "scr_wi": nc.dram_tensor("scr_wi", [E, H, F], BF16, kind="Internal").ap(),
        "scr_wg": nc.dram_tensor("scr_wg", [E, H, F], BF16, kind="Internal").ap(),
        "scr_wo": nc.dram_tensor("scr_wo", [E, F, H], BF16, kind="Internal").ap(),
    }
    if dbg:
        io["dbg_ln"] = nc.dram_tensor("dbg_ln", [128, 4096], F32, kind="ExternalOutput").ap()
        io["dbg_lnfm"] = nc.dram_tensor("dbg_lnfm", [128, 4096], BF16, kind="ExternalOutput").ap()
        io["dbg_c"] = nc.dram_tensor("dbg_c", [128, 32], F32, kind="ExternalOutput").ap()
        io["dbg_gb"] = nc.dram_tensor("dbg_gb", [128, 1024], F32, kind="ExternalOutput").ap()
        io["dbg_bb"] = nc.dram_tensor("dbg_bb", [128, 1024], F32, kind="ExternalOutput").ap()
        io["dbg_g"] = nc.dram_tensor("dbg_g", [128, 16384], BF16, kind="ExternalOutput").ap()
        io["dbg_h"] = nc.dram_tensor("dbg_h", [128, 16384], BF16, kind="ExternalOutput").ap()
        io["dbg_r"] = nc.dram_tensor("dbg_r", [128, 1024], F32, kind="ExternalOutput").ap()
    es = ExitStack()
    with es:
        sb = lambda name, w, dt=F32, p=128: es.enter_context(
            nc.sbuf_tensor(name, [p, w], dt))
        buf = {
            "x1t": sb("x1t", 4096),
            "ln2_sb": sb("ln2_sb", 4096),
            "ln2_fm": sb("ln2_fm", 4096, BF16),
            "lnbf": sb("lnbf", 1024, BF16),
            "g_buf": sb("g_buf", 16384, BF16),
            "h_fm": sb("h_fm", 16384, BF16),
            "sgt": sb("sgt", 1024, BF16),
            "wi_sb": sb("wi_sb", 8192, BF16),
            "wg_sb": sb("wg_sb", 8192, BF16),
            "wo_sb": sb("wo_sb", 8192, BF16),
            "acc": sb("acc", 4096),
            "r_sb": sb("r_sb", 2048),
            "gb_sb": sb("gb_sb", 1024),
            "bb_sb": sb("bb_sb", 1024),
            "ctile": sb("ctile", 64),
            "cols": sb("cols", 16),
            "sqscr": sb("sqscr", 1024),
            "bic": sb("bicsb", 256),
            "bgc": sb("bgcsb", 256),
            "rowb": sb("rowb", 3072, BF16, 1),
            "brf": sb("brf", 8, F32, 1),
            "brb": sb("brb", 8, BF16, 1),
            "wr_bf": sb("wr_bfsb", 64, BF16),
            "ident": sb("identsb", 128, BF16),
            "ones": sb("onessb", 128, BF16, 1),
            "pr": [es.enter_context(nc.psum_tensor(f"pr{i}", [128, 512], F32))
                   for i in range(8)],
        }
        ctx = Ctx(nc, es)
        ctx.GWI = (ctx.queue("gwi0", engine="q"), ctx.queue("gwi1", engine="q"))
        ctx.GWG = (ctx.queue("gwg0", engine="q"), ctx.queue("gwg1", engine="q"))
        ctx.GWO = (ctx.queue("gwo0", engine="q"), ctx.queue("gwo1", engine="q"))
        blk = es.enter_context(nc.Block())
        ctx.run_passes(blk, lambda c: emit_phase_b(c, io, buf, n_tt, dbg))
    return nc


def host_prep_b(inputs):
    bi = np.asarray(inputs["L0_ffn_bi"], np.float32)
    bg = np.asarray(inputs["L0_ffn_bg"], np.float32)
    bo = np.asarray(inputs["L0_ffn_bo"], np.float32)
    gam = np.asarray(inputs["L0_ffn_gamma"], np.float32)
    bet = np.asarray(inputs["L0_ffn_beta"], np.float32)
    br = np.asarray(inputs["L0_ffn_br"], np.float32)
    rows = np.stack([bo, gam, bet])
    return {
        "Wi": np.ascontiguousarray(inputs["L0_ffn_Wi"], np.float32),
        "Wg": np.ascontiguousarray(inputs["L0_ffn_Wg"], np.float32),
        "Wo": np.ascontiguousarray(inputs["L0_ffn_Wo"], np.float32),
        "Wr": np.ascontiguousarray(inputs["L0_ffn_Wr"], np.float32),
        "bic": np.ascontiguousarray(
            bi.reshape(8, 32, 128).transpose(2, 0, 1).reshape(128, 256)),
        "bgc": np.ascontiguousarray(
            bg.reshape(8, 32, 128).transpose(2, 0, 1).reshape(128, 256)),
        "rows": np.ascontiguousarray(rows),
        "br": np.ascontiguousarray(br.reshape(1, 8)),
        "ident": np.eye(128, dtype=ml_dtypes.bfloat16),
        "ones1": np.ones((1, 128), dtype=ml_dtypes.bfloat16),
    }





def emit_phase_b_sparse(ctx, io, buf, n_tt=8, fused=False):
    T, V, Sc, Q, G = ctx.T, ctx.V, ctx.S, ctx.Q, ctx.G
    GWI, GWG, GWO, GL = ctx.GWI, ctx.GWG, ctx.GWO, ctx.GL
    trk = {"v": V, "s": Sc, "t": T}
    E = 8
    TTOK = n_tt * 512
    NTC = n_tt * 4
    CAP = ((int(TTOK * 2 / 8 * 3.0) + 127) // 128) * 128
    NSC = CAP // 128
    SUBT = [min(512, CAP - s) for s in range(0, CAP, 512)]

    x1_in = io["x1"]
    x2_acc = io["scr_accx"]
    scr_wi, scr_wg, scr_wo = io["scr_wi"], io["scr_wg"], io["scr_wo"]
    scr_ln, scr_lnf, scr_rec = io["scr_ln"], io["scr_lnf"], io["scr_rec"]

    (x1t, ln2_sb, ln2_fm, lnbf, h_fm, sgt, wi_sb, acc,
     r_sb, gb_sb, bb_sb, ctile, cols, sqscr, bic, bgc, rowb, brf, brb,
     wr_bf, ident, ones) = (buf[k] for k in (
        "x1t", "ln2_sb", "ln2_fm", "lnbf", "h_fm", "sgt", "wi_sb",
        "acc", "r_sb", "gb_sb", "bb_sb", "ctile", "cols",
        "sqscr", "bic", "bgc", "rowb", "brf", "brb", "wr_bf",
        "ident", "ones"))
    wg_sb, wo_sb = buf["wg_sb"], buf["wo_sb"]
    pr = buf["pr"]
    rec_sb, islots, iota_c = buf["rec_sb"], buf["islots"], buf["iota_c"]
    onecol, ecol, idf32 = buf["onecol"], buf["ecol"], buf["idf32"]
    grec, gtok, zrec, y_out = (buf["grec"], buf["gtok"], buf["zrec"],
                               buf["y_out"])
    cfm = h_fm[:].bitcast(F32)[0:8, 0:TTOK]
    msk = h_fm[:].bitcast(F32)[0:8, TTOK:2 * TTOK]
    msk2 = wi_sb[:].bitcast(F32)[0:8, 0:TTOK]
    gbf = buf["gln"][:]
    free = {i: ("v", 0) for i in range(8)}
    ctx.n_gb_last = 0

    if not fused:
        for e in range(E):
            G.op(lambda g, e=e: g.dma_start(out=scr_wi[e], in_=io["Wi"][e]))
            G.op(lambda g, e=e: g.dma_start(out=scr_wg[e], in_=io["Wg"][e]))
            G.op(lambda g, e=e: g.dma_start(out=scr_wo[e], in_=io["Wo"][e]))

    nc1 = Q.op(lambda q: q.dma_start(out=ident[:], in_=io["ident"]))
    nc1 = Q.op(lambda q: q.dma_start(out=idf32[:], in_=io["idf32"]))
    nc1 = Q.op(lambda q: q.dma_start(out=ones[:], in_=io["ones1"]))
    nc1 = Q.op(lambda q: q.dma_start(out=bic[:], in_=io["bic"]))
    nc1 = Q.op(lambda q: q.dma_start(out=bgc[:], in_=io["bgc"]))
    nc1 = Q.op(lambda q: q.dma_start(out=brf[:], in_=io["br"]))
    nc1 = Q.op(lambda q: q.dma_start(out=iota_c[:], in_=io["iota_c"]))
    nc1 = Q.op(lambda q: q.dma_start(out=ecol[:], in_=io["ecolv"]))
    nc1 = Q.op(lambda q: q.dma_start(out=buf["dumpc"][:], in_=io["dumpc"]))
    nc1 = Q.op(lambda q: q.dma_start(out=zrec[:], in_=io["zrecv"]))
    nG = G.op(lambda g: g.dma_start(
        out=wr_bf[:].rearrange("p (k e) -> p k e", k=8),
        in_=io["Wr"].rearrange("(k p) e -> p k e", p=128)))
    n_g_setup = nG
    G.wait(G, nG)
    V.wait(Q, nc1)
    nbr = V.op(lambda v: v.tensor_copy(brb[:], brf[:]))
    neps = V.op(lambda v: v.memset(cols[:, 14:15], 1e-5))
    neps = V.op(lambda v: v.memset(cols[:, 15:16], 1e-12))
    noc = V.op(lambda v: v.memset(onecol[:], 1.0))
    noc = V.op(lambda v: v.memset(rec_sb[:], 0.0))
    Sc.wait(V, neps)
    T.wait(Q, nc1)
    T.wait(G, nG)
    T.wait(V, nbr)
    G.wait(Q, nc1)
    nzf = V.op(lambda v: v.memset(lnbf[:], 0.0))
    nzf = V.op(lambda v: v.memset(ln2_sb[:, 0:1024], 0.0))
    Q.wait(V, nzf)
    nzfst = Q.op(lambda q: q.dma_start(out=scr_ln[TTOK:TTOK + 128, :],
                                       in_=lnbf[:]))
    nzfst = Q.op(lambda q: q.dma_start(out=scr_lnf[TTOK:TTOK + 128, :],
                                       in_=ln2_sb[:, 0:1024]))
    Q.wait(Q, nc1)
    nzrec = 0
    for z in range(E * CAP // 128):
        nzrec = Q.op(lambda q, z=z: q.dma_start(
            out=scr_rec[z * 128:(z + 1) * 128, :], in_=zrec[:]))

    last_cfm = 0
    n_lnst = nzfst
    nR = 0
    for tt in range(n_tt):
        ctx.barrier()
        nload = Q.op(lambda q, tt=tt: q.dma_start(
            out=x1t[:].rearrange("p (tc h) -> p tc h", tc=4),
            in_=x1_in[tt * 512:(tt + 1) * 512, :].rearrange(
                "(tc p) h -> p tc h", p=128)))
        V.wait(Q, nload)
        Q.wait(Q, nload)
        nacc0 = Q.op(lambda q, tt=tt: q.dma_start(
            out=x2_acc[tt * 512:(tt + 1) * 512, :].rearrange(
                "(tc p) h -> p tc h", p=128),
            in_=x1t[:].rearrange("p (tc h) -> p tc h", tc=4)))
        ctx.nacc0 = nacc0

        for tc in range(4):
            gtc = tt * 4 + tc
            xi = x1t[:, tc * 1024:(tc + 1) * 1024]
            lni = ln2_sb[:, tc * 1024:(tc + 1) * 1024]
            n1 = V.op(lambda v, xi=xi: v.reduce_sum(
                cols[:, 0:1], xi, axis=AX.X, negate=True))
            V.wait(V, n1)
            n2 = V.op(lambda v: v.tensor_scalar_mul(cols[:, 0:1],
                                                    cols[:, 0:1], 1.0 / H))
            V.wait(V, n2)
            V.wait(Q, n_lnst)
            n3 = V.op(lambda v, xi=xi, lni=lni: v.tensor_scalar_add(
                lni, xi, cols[:, 0:1]))
            Sc.wait(V, n3)
            n4 = Sc.op(lambda s, lni=lni: s.activation(
                sqscr[:], lni, AF.Square, accum_out=cols[:, 1:2]))
            V.wait(Sc, n4)
            n5 = V.op(lambda v: v.tensor_scalar_mul(cols[:, 2:3],
                                                    cols[:, 1:2], 1.0 / H))
            Sc.wait(V, n5)
            n6 = Sc.op(lambda s: s.activation(cols[:, 3:4], cols[:, 2:3],
                                              AF.Ln, bias=cols[:, 14:15]))
            Sc.wait(Sc, n6)
            n7 = Sc.op(lambda s: s.activation(cols[:, 2:3], cols[:, 3:4],
                                              AF.Exp, scale=-0.5))
            V.wait(Sc, n7)
            V.wait(Q, n_lnst)
            n8 = V.op(lambda v, lni=lni: v.tensor_scalar_mul(
                lnbf[:], lni, cols[:, 2:3]))
            V.wait(V, n8)
            n9 = V.op(lambda v, lni=lni: v.tensor_scalar_mul(
                lni, lni, cols[:, 2:3]))
            Q.wait(V, n9)
            n_lnst = Q.op(lambda q, gtc=gtc: q.dma_start(
                out=scr_ln[gtc * 128:(gtc + 1) * 128, :], in_=lnbf[:]))
            n_lnst = Q.op(lambda q, gtc=gtc, lni=lni: q.dma_start(
                out=scr_lnf[gtc * 128:(gtc + 1) * 128, :], in_=lni))
            T.wait(V, n8)
            for k in range(8):
                p0 = pr[k % 2]
                eng, cnt = free[k % 2]
                T.wait(trk[eng], cnt)
                nt = T.op(lambda t, k=k, p0=p0: t.transpose(
                    p0[:].bitcast(BF16)[:, 0:128], lnbf[:, ts(k)], ident[:]))
                Sc.wait(T, nt)
                nse = Sc.op(lambda s, k=k, tc=tc, p0=p0: s.copy(
                    ln2_fm[:, k * 512 + tc * 128: k * 512 + tc * 128 + 128],
                    p0[:].bitcast(BF16)[:, 0:128]))
                free[k % 2] = ("s", nse)
            n_lnfm = nse
            eng, cnt = free[4]
            T.wait(trk[eng], cnt)
            T.wait(Sc, n_lnfm)
            for k in range(8):
                nt = T.op(lambda t, k=k, tc=tc: t.matmul(
                    pr[4][:, 0:8],
                    ln2_fm[:, k * 512 + tc * 128: k * 512 + tc * 128 + 128],
                    wr_bf[:, k * 8: k * 8 + 8], start=(k == 0), stop=False))
            nt = T.op(lambda t: t.matmul(
                pr[4][:, 0:8], ones[:], brb[:], start=False, stop=True))
            V.wait(T, nt)
            nA = V.op(lambda v: v.reduce_max(cols[:, 4:5], pr[4][:, 0:8],
                                             axis=AX.X, negate=True))
            V.wait(V, nA)
            nB = V.op(lambda v: v.tensor_scalar_add(ctile[:, 48:56],
                                                    pr[4][:, 0:8],
                                                    cols[:, 4:5]))
            free[4] = ("v", nB)
            Sc.wait(V, nB)
            nC = Sc.op(lambda s: s.activation(ctile[:, 40:48],
                                              ctile[:, 48:56], AF.Exp))
            V.wait(Sc, nC)
            nD = V.op(lambda v: v.reduce_max(cols[:, 5:6], ctile[:, 40:48],
                                             axis=AX.X))
            V.wait(V, nD)
            nE = V.op(lambda v: v.tensor_single_scalar(
                ctile[:, 32:40], ctile[:, 40:48], cols[:, 5:6],
                ALU.is_equal))
            V.wait(V, nE)
            nF = V.op(lambda v: v.scalar_tensor_tensor(
                ctile[:, 32:40], ctile[:, 32:40], NEG, ctile[:, 40:48],
                ALU.mult, ALU.add))
            V.wait(V, nF)
            nGm = V.op(lambda v: v.reduce_max(cols[:, 6:7], ctile[:, 32:40],
                                              axis=AX.X))
            V.wait(V, nGm)
            nH = V.op(lambda v: v.tensor_single_scalar(
                ctile[:, 32:40], ctile[:, 40:48], cols[:, 6:7], ALU.is_ge))
            nI = V.op(lambda v: v.tensor_add(cols[:, 7:8], cols[:, 5:6],
                                             cols[:, 6:7]))
            V.wait(V, nI)
            nJ = V.op(lambda v: v.reciprocal(cols[:, 7:8], cols[:, 7:8]))
            V.wait(V, nJ)
            nK = V.op(lambda v: v.tensor_mul(ctile[:, 32:40],
                                             ctile[:, 32:40],
                                             ctile[:, 40:48]))
            V.wait(V, nK)
            nL = V.op(lambda v: v.tensor_scalar_mul(
                ctile[:, 0:8], ctile[:, 32:40], cols[:, 7:8]))
            V.wait(V, nL)
            nR = V.op(lambda v, gtc=gtc: v.tensor_copy(
                rec_sb[:, gtc * 16: gtc * 16 + 8], ctile[:, 0:8]))
            nR = V.op(lambda v, gtc=gtc: v.tensor_copy(
                rec_sb[:, gtc * 16 + 8: gtc * 16 + 9],
                iota_c[:, gtc: gtc + 1]))
            nCb = V.op(lambda v: v.tensor_copy(
                ctile[:].bitcast(BF16)[:, 112:120], ctile[:, 0:8]))
            eng, cnt = free[5]
            T.wait(trk[eng], cnt)
            T.wait(V, nCb)
            ntc2 = T.op(lambda t: t.transpose(
                pr[5][:].bitcast(BF16)[0:8, 0:128],
                ctile[:].bitcast(BF16)[:, 112:120], ident[:]))
            V.wait(T, ntc2)
            V.wait(V, last_cfm)
            ncf = V.op(lambda v, gtc=gtc: v.tensor_copy(
                cfm[0:8, gtc * 128:(gtc + 1) * 128],
                pr[5][:].bitcast(BF16)[0:8, 0:128]))
            last_cfm = ncf
            free[5] = ("v", ncf)

    ctx.barrier()
    nm = V.op(lambda v: v.tensor_single_scalar(msk, cfm, 0.0, ALU.is_gt))
    V.wait(V, nm)
    nm = V.op(lambda v: v.tensor_tensor_scan(
        cfm, onecol[:].broadcast_to([8, TTOK]), msk, 0.0,
        ALU.mult, ALU.add))
    V.wait(V, nm)
    nm = V.op(lambda v: v.tensor_scalar_add(cfm, cfm, -1.0))
    V.wait(V, nm)
    nm = V.op(lambda v: v.tensor_single_scalar(msk2, cfm, float(CAP) - 0.5,
                                               ALU.is_lt))
    V.wait(V, nm)
    nm = V.op(lambda v: v.tensor_mul(msk, msk, msk2))
    V.wait(V, nm)
    nm = V.op(lambda v: v.tensor_scalar(cfm, cfm, ecol[:, 0:1], None,
                                        ALU.add))
    V.wait(V, nm)
    nm = V.op(lambda v: v.tensor_mul(cfm, cfm, msk))
    V.wait(V, nm)
    nm = V.op(lambda v: v.scalar_tensor_tensor(
        msk2, msk, -1e9, onecol[:].broadcast_to([8, TTOK]),
        ALU.mult, ALU.bypass))
    V.wait(V, nm)
    nm = V.op(lambda v: v.scalar_tensor_tensor(
        msk2, msk, -1.0, onecol[:].broadcast_to([8, TTOK]),
        ALU.mult, ALU.add))
    V.wait(V, nm)
    nm = V.op(lambda v: v.scalar_tensor_tensor(
        cfm, msk2, 1e6, cfm, ALU.mult, ALU.add))
    V.wait(V, nm)
    nslt = 0
    nsc = 0
    bcreg = [None]
    def _bc(g):
        if bcreg[0] is None:
            bcreg[0] = g.to_reg(E * CAP - 1)
        return bcreg[0]
    G.wait(Q, nzrec)
    G.wait(V, nR)
    T.wait(V, nm)
    for gtc in range(NTC):
        eng, cnt = free[gtc % 2]
        T.wait(trk[eng], cnt)
        ntx = T.op(lambda t, gtc=gtc: t.transpose(
            pr[gtc % 2][:, 0:8], cfm[0:8, gtc * 128:(gtc + 1) * 128],
            idf32[0:8, 0:8]))
        V.wait(T, ntx)
        V.wait(V, nslt)
        nslt = V.op(lambda v, gtc=gtc: v.tensor_copy(
            islots[:, gtc * 8:(gtc + 1) * 8], pr[gtc % 2][:, 0:8]))
        free[gtc % 2] = ("v", nslt)
        G.wait(V, nslt)
        for e in range(E):
            nsc = G.op(lambda g, gtc=gtc, e=e: g.indirect_dma_start(
                out=scr_rec,
                out_offset=bass.IndirectOffsetOnAxis(
                    ap=islots[:, gtc * 8 + e: gtc * 8 + e + 1], axis=0),
                in_=rec_sb[:, gtc * 16:(gtc + 1) * 16],
                in_offset=None,
                bounds_check=_bc(g), oob_is_err=False))

    ctx.barrier()
    wlast_wi = [0, 0, 0]
    wlast_wg = [0, 0, 0]
    wlast_wo = [0, 0]
    sg_use = [0, 0]
    sq_use = 0
    rlast = [0, 0]
    n_rmm_prev = 0
    n_h_prev = 0
    n_ysc_prev = 0
    n_ysc_slot = [0, 0, 0, 0]
    n_lnuse_q = [0, 0]
    n_gtrans = 0
    lnf_bufs = [ln2_sb, x1t]
    NTILES = len(SUBT)
    tiles = [(e_, ti_) for e_ in range(E) for ti_ in range(NTILES)]
    grec_q = [grec[:, 0:NSC * 16], grec[:, NSC * 16:NSC * 32]]
    gtok_q = [gtok[:, 0:NSC], gtok[:, NSC:2 * NSC]]
    ntk_q = [0, 0]
    n_grec_read_exp = {}
    n_ysc_exp_end = {}
    pend_ngb = {}

    def emit_grec(e2):
        p = e2 % 2
        GL[0].wait(GL[0], GL[0].n)
        GL[0].wait(G, nsc)
        GL[0].wait(V, n_grec_read_exp.get(e2 - 2, 0))
        ngrec = GL[0].op(lambda g, e2=e2, p=p: g.dma_start(
            out=grec_q[p].rearrange("p (s r) -> p s r", s=NSC),
            in_=scr_rec[e2 * CAP:(e2 + 1) * CAP, :].rearrange(
                "(s p) r -> p s r", p=128)))
        V.wait(GL[0], ngrec)
        V.wait(GL[1], n_ysc_exp_end.get(e2 - 2, 0))
        ntk_q[p] = V.op(lambda v, p=p: v.tensor_copy(
            gtok_q[p],
            grec_q[p].rearrange("p (s r) -> p s r", s=NSC)[:, :, 8]))

    def emit_gathers(gi2):
        e2, ti2 = tiles[gi2]
        p = e2 % 2
        q2 = gi2 % 2
        lnf2 = lnf_bufs[q2]
        GL[0].wait(GL[0], GL[0].n)
        GL[0].wait(V, ntk_q[p])
        GL[0].wait(V, n_lnuse_q[q2])
        GL[0].wait(T, n_gtrans)
        ngb = 0
        for s in range(SUBT[ti2] // 128):
            sc_ = ti2 * 4 + s
            ngb = GL[0].op(lambda g, sc_=sc_, s=s, p=p: g.indirect_dma_start(
                out=gbf[:, s * 1024:(s + 1) * 1024],
                out_offset=None,
                in_=scr_ln,
                in_offset=bass.IndirectOffsetOnAxis(
                    ap=gtok_q[p][:, sc_: sc_ + 1], axis=0)))
            ngb = GL[0].op(lambda g, sc_=sc_, s=s, p=p, lnf2=lnf2:
                           g.indirect_dma_start(
                out=lnf2[:, s * 1024:(s + 1) * 1024],
                out_offset=None,
                in_=scr_lnf,
                in_offset=bass.IndirectOffsetOnAxis(
                    ap=gtok_q[p][:, sc_: sc_ + 1], axis=0)))
        pend_ngb[gi2] = ngb

    G.wait(G, n_g_setup)
    for qq in (GWI[0], GWI[1], GWI[2], GWG[0], GWG[1], GWG[2],
               GWO[0], GWO[1]):
        qq.wait(G, n_g_setup)
    emit_grec(0)
    emit_gathers(0)
    for gi, (e, ti) in enumerate(tiles):
        wt = SUBT[ti]
        toff = ti * 512
        ntc_w = wt // 128
        q_ = gi % 2
        p_ = e % 2
        lnf = lnf_bufs[q_]
        if True:
            T.wait(GL[0], pend_ngb[gi])
            Sc.wait(T, n_rmm_prev)
            nse = 0
            for s in range(ntc_w):
                for k in range(8):
                    p0 = pr[k % 4]
                    eng, cnt = free[k % 4]
                    T.wait(trk[eng], cnt)
                    nt = T.op(lambda t, k=k, s=s, p0=p0: t.transpose(
                        p0[:].bitcast(BF16)[:, 0:128],
                        gbf[:, s * 1024 + k * 128: s * 1024 + k * 128
                            + 128],
                        ident[:]))
                    n_gtrans = nt
                    Sc.wait(T, nt)
                    nse = Sc.op(lambda s_, k=k, s=s, p0=p0: s_.copy(
                        ln2_fm[:, k * 512 + s * 128: k * 512 + s * 128
                               + 128],
                        p0[:].bitcast(BF16)[:, 0:128]))
                    free[k % 4] = ("s", nse)
            n_lnfm = nse
            if gi + 1 < len(tiles):
                if tiles[gi + 1][1] == 0:
                    emit_grec(e + 1)
                emit_gathers(gi + 1)

            Sc.wait(V, n_h_prev)
            for fc in range(8):
                slot = fc % 3
                GWI[slot].wait(T, wlast_wi[slot])
                nwld = GWI[slot].op(lambda g, e=e, fc=fc, slot=slot:
                                    g.dma_start(
                    out=wi_sb[:, slot * 4096:(slot + 1) * 4096].rearrange(
                        "p (k f) -> p k f", k=8),
                    in_=scr_wi[e].rearrange("(k p) f -> p k f", p=128)[
                        :, :, fc * 512:(fc + 1) * 512]))
                T.wait(GWI[slot], nwld)
                for fb in range(4):
                    f = fc * 4 + fb
                    pp = pr[f % 4]
                    eng, cnt = free[f % 4]
                    T.wait(trk[eng], cnt)
                    T.wait(Sc, n_lnfm)
                    for k in range(8):
                        nt = T.op(lambda t, k=k, fb=fb, slot=slot, pp=pp,
                                  wt=wt: t.matmul(
                            pp[:, 0:wt],
                            wi_sb[:, slot * 4096 + k * 512 + fb * 128:
                                  slot * 4096 + k * 512 + fb * 128 + 128],
                            ln2_fm[:, k * 512: k * 512 + wt],
                            start=(k == 0), stop=(k == 7)))
                    Sc.wait(T, nt)
                    nse = Sc.op(lambda s, f=f, e=e, pp=pp, wt=wt:
                                s.activation(
                        h_fm[:, f * 512: f * 512 + wt], pp[:, 0:wt],
                        AF.Gelu,
                        bias=bic[:, e * 32 + f: e * 32 + f + 1]))
                    free[f % 4] = ("s", nse)
                wlast_wi[slot] = nt

            V.wait(T, n_rmm_prev)
            nvh = 0
            for fc in range(8):
                slot = fc % 3
                GWG[slot].wait(T, wlast_wg[slot])
                nwld = GWG[slot].op(lambda g, e=e, fc=fc, slot=slot:
                                    g.dma_start(
                    out=wg_sb[:, slot * 4096:(slot + 1) * 4096].rearrange(
                        "p (k f) -> p k f", k=8),
                    in_=scr_wg[e].rearrange("(k p) f -> p k f", p=128)[
                        :, :, fc * 512:(fc + 1) * 512]))
                T.wait(GWG[slot], nwld)
                for fb in range(4):
                    f = fc * 4 + fb
                    fs = f % 2
                    pp = pr[f % 4]
                    eng, cnt = free[f % 4]
                    T.wait(trk[eng], cnt)
                    for k in range(8):
                        nt = T.op(lambda t, k=k, fb=fb, slot=slot, pp=pp,
                                  wt=wt: t.matmul(
                            pp[:, 0:wt],
                            wg_sb[:, slot * 4096 + k * 512 + fb * 128:
                                  slot * 4096 + k * 512 + fb * 128 + 128],
                            ln2_fm[:, k * 512: k * 512 + wt],
                            start=(k == 0), stop=(k == 7)))
                    Sc.wait(T, nt)
                    Sc.wait(V, sg_use[fs])
                    nse = Sc.op(lambda s, f=f, e=e, pp=pp, fs=fs, wt=wt:
                                s.activation(
                        sgt[:, fs * 512: fs * 512 + wt], pp[:, 0:wt],
                        AF.Sigmoid,
                        bias=bgc[:, e * 32 + f: e * 32 + f + 1]))
                    free[f % 4] = ("s", nse)
                    V.wait(Sc, nse)
                    nvh = V.op(lambda v, f=f, fs=fs, wt=wt: v.tensor_mul(
                        h_fm[:, f * 512: f * 512 + wt],
                        h_fm[:, f * 512: f * 512 + wt],
                        sgt[:, fs * 512: fs * 512 + wt]))
                    sg_use[fs] = nvh
                wlast_wg[slot] = nt
            n_h = nvh
            n_h_prev = nvh

            G.wait(T, n_rmm_prev)
            G.wait(G, G.n)
            nrow = G.op(lambda g, e=e: g.dma_start(
                out=rowb[:].rearrange("o (a h) -> o a h", a=3),
                in_=io["rows"][:, e, :].unsqueeze(0)))
            T.wait(G, nrow)
            for which, dst in ((0, gb_sb), (1, bb_sb)):
                for half in range(2):
                    bank = 6 + which
                    eng, cnt = free[bank]
                    T.wait(trk[eng], cnt)
                    nt = T.op(lambda t, which=which, half=half, bank=bank:
                              t.matmul(pr[bank][:, 0:512], ones[:],
                                       rowb[0:1, 1024 + which * 1024
                                            + half * 512: 1024
                                            + which * 1024 + half * 512
                                            + 512],
                                       start=True, stop=True))
                    V.wait(T, nt)
                    nv = V.op(lambda v, dst=dst, half=half, bank=bank:
                              v.tensor_copy(dst[:, half * 512:
                                                half * 512 + 512],
                                            pr[bank][:, 0:512]))
                    free[bank] = ("v", nv)

            T.wait(V, n_h)
            for b_ in range(8):
                eng, cnt = free[b_]
                T.wait(trk[eng], cnt)
            for fg in range(8):
                slot = fg % 2
                GWO[slot].wait(T, wlast_wo[slot])
                nwld = GWO[slot].op(lambda g, e=e, fg=fg, slot=slot:
                                    g.dma_start(
                    out=wo_sb[:, slot * 4096:(slot + 1) * 4096].rearrange(
                        "p (g o) -> p g o", g=4),
                    in_=scr_wo[e][fg * 512:(fg + 1) * 512, :].rearrange(
                        "(g p) o -> p g o", p=128)))
                T.wait(GWO[slot], nwld)
                for tc in range(ntc_w):
                    for fb in range(4):
                        f = fg * 4 + fb
                        for half in range(2):
                            nt = T.op(lambda t, f=f, tc=tc, fb=fb,
                                      half=half, slot=slot: t.matmul(
                                pr[tc * 2 + half][:, 0:512],
                                h_fm[:, f * 512 + tc * 128:
                                     f * 512 + tc * 128 + 128],
                                wo_sb[:, slot * 4096 + fb * 1024
                                      + half * 512: slot * 4096
                                      + fb * 1024 + half * 512 + 512],
                                start=(fg == 0 and fb == 0), stop=False))
                wlast_wo[slot] = nt
            for tc in range(ntc_w):
                for half in range(2):
                    nt = T.op(lambda t, tc=tc, half=half, e=e: t.matmul(
                        pr[tc * 2 + half][:, 0:512], ones[:],
                        rowb[0:1, half * 512: half * 512 + 512],
                        start=False, stop=True))
            n_rmm = nt
            n_rmm_prev = nt
            for tc in range(ntc_w):
                V.wait(T, n_rmm)
                rsl = tc % 2
                rs = r_sb[:, rsl * 1024: rsl * 1024 + 1024]
                V.wait(V, rlast[rsl])
                for half in range(2):
                    nv1 = V.op(lambda v, tc=tc, half=half, rs=rs, lnf=lnf:
                               v.tensor_add(
                        rs[:, half * 512: half * 512 + 512],
                        pr[tc * 2 + half][:, 0:512],
                        lnf[:, tc * 1024 + half * 512:
                            tc * 1024 + half * 512 + 512]))
                    free[tc * 2 + half] = ("v", nv1)
                n_lnuse_q[q_] = nv1
                V.wait(V, nv1)
                n1 = V.op(lambda v, rs=rs: v.reduce_sum(
                    cols[:, 8:9], rs, axis=AX.X, negate=True))
                V.wait(V, n1)
                n2 = V.op(lambda v: v.tensor_scalar_mul(cols[:, 8:9],
                                                        cols[:, 8:9],
                                                        1.0 / H))
                V.wait(V, n2)
                Sc.wait(V, n2)
                Sc.wait(V, sq_use)
                n4 = Sc.op(lambda s, rs=rs: s.activation(
                    sqscr[:], rs, AF.Square, bias=cols[:, 8:9],
                    accum_out=cols[:, 9:10]))
                V.wait(Sc, n4)
                n5 = V.op(lambda v: v.tensor_scalar_mul(cols[:, 10:11],
                                                        cols[:, 9:10],
                                                        1.0 / H))
                Sc.wait(V, n5)
                n6 = Sc.op(lambda s: s.activation(cols[:, 11:12],
                                                  cols[:, 10:11], AF.Ln,
                                                  bias=cols[:, 15:16]))
                Sc.wait(Sc, n6)
                n7 = Sc.op(lambda s: s.activation(cols[:, 10:11],
                                                  cols[:, 11:12], AF.Exp,
                                                  scale=-0.5))
                V.wait(Sc, n7)
                n9 = V.op(lambda v, tc=tc, e=e, p_=p_: v.tensor_mul(
                    cols[:, 12:13],
                    grec_q[p_][:, (toff // 128 + tc) * 16 + e:
                               (toff // 128 + tc) * 16 + e + 1],
                    cols[:, 10:11]))
                V.wait(V, n9)
                nta = V.op(lambda v, rs=rs: v.tensor_scalar(
                    sqscr[:], rs, cols[:, 8:9], cols[:, 12:13],
                    ALU.add, ALU.mult))
                rlast[rsl] = nta
                V.wait(V, nta)
                ntb = V.op(lambda v: v.tensor_mul(sqscr[:], sqscr[:],
                                                  gb_sb[:]))
                sq_use = ntb
                V.wait(V, ntb)
                ysl = y_out[:, tc * 1024: tc * 1024 + 1024]
                V.wait(GL[1], n_ysc_slot[tc])
                ntc2 = V.op(lambda v, ysl=ysl: v.tensor_copy(ysl, sqscr[:]))
                sq_use = ntc2
                V.wait(V, ntc2)
                ntd = V.op(lambda v, tc=tc, e=e, ysl=ysl, p_=p_:
                           v.scalar_tensor_tensor(
                    ysl, bb_sb[:],
                    grec_q[p_][:, (toff // 128 + tc) * 16 + e:
                               (toff // 128 + tc) * 16 + e + 1],
                    ysl, ALU.mult, ALU.add))
                n_grec_read_exp[e] = ntd
                GL[1].wait(V, ntd)
                GL[1].wait(Q, ctx.nacc0)
                GL[1].wait(GL[1], n_ysc_prev)
                n_ysc = GL[1].op(lambda g, tc=tc, e=e, ysl=ysl, toff=toff,
                                 p_=p_: g.indirect_dma_start(
                    out=x2_acc,
                    out_offset=bass.IndirectOffsetOnAxis(
                        ap=gtok_q[p_][:, toff // 128 + tc:
                                      toff // 128 + tc + 1], axis=0),
                    in_=ysl,
                    in_offset=None,
                    compute_op=ALU.add))
                n_ysc_prev = n_ysc
                n_ysc_slot[tc] = n_ysc
            n_ysc_exp_end[e] = n_ysc_prev
    GL[1].wait(GL[1], n_ysc_prev)
    Q.wait(GL[1], n_ysc_prev)
    ctx.barrier()
    if not fused:
        nfin = Q.op(lambda q: q.dma_start(out=io["x2"],
                                          in_=x2_acc[0:TTOK, :]))
        Q.wait(Q, nfin)


def build_phase_b_sparse(n_tt=8):
    nc = bass.Bass("TRN2", target_bir_lowering=False)
    Sx = n_tt * 512
    E, F = 8, 4096
    CAP = ((int(Sx * 2 / 8 * 3.0) + 127) // 128) * 128
    NSC = CAP // 128
    io = {
        "x1": nc.dram_tensor("x1", [Sx, H], F32, kind="ExternalInput").ap(),
        "Wi": nc.dram_tensor("Wi", [E, H, F], F32, kind="ExternalInput").ap(),
        "Wg": nc.dram_tensor("Wg", [E, H, F], F32, kind="ExternalInput").ap(),
        "Wo": nc.dram_tensor("Wo", [E, F, H], F32, kind="ExternalInput").ap(),
        "Wr": nc.dram_tensor("Wr", [H, E], F32, kind="ExternalInput").ap(),
        "bic": nc.dram_tensor("bic", [128, 256], F32, kind="ExternalInput").ap(),
        "bgc": nc.dram_tensor("bgc", [128, 256], F32, kind="ExternalInput").ap(),
        "rows": nc.dram_tensor("rows", [3, E, 1024], F32,
                               kind="ExternalInput").ap(),
        "br": nc.dram_tensor("br", [1, 8], F32, kind="ExternalInput").ap(),
        "ident": nc.dram_tensor("ident", [128, 128], BF16,
                                kind="ExternalInput").ap(),
        "idf32": nc.dram_tensor("idf32", [128, 128], F32,
                                kind="ExternalInput").ap(),
        "ones1": nc.dram_tensor("ones1", [1, 128], BF16,
                                kind="ExternalInput").ap(),
        "iota_c": nc.dram_tensor("iota_c", [128, n_tt * 4], F32,
                                 kind="ExternalInput").ap(),
        "ecolv": nc.dram_tensor("ecolv", [8, 1], F32,
                                kind="ExternalInput").ap(),
        "x2": nc.dram_tensor("x2", [Sx, H], F32, kind="ExternalOutput").ap(),
        "scr_wi": nc.dram_tensor("scr_wi", [E, H, F], BF16, kind="Internal").ap(),
        "scr_wg": nc.dram_tensor("scr_wg", [E, H, F], BF16, kind="Internal").ap(),
        "scr_wo": nc.dram_tensor("scr_wo", [E, F, H], BF16, kind="Internal").ap(),
        "scr_ln": nc.dram_tensor("scr_ln", [Sx + 128, H], BF16,
                                 kind="Internal").ap(),
        "scr_lnf": nc.dram_tensor("scr_lnf", [Sx + 128, H], F32,
                                  kind="Internal").ap(),
        "scr_rec": nc.dram_tensor("scr_rec", [E * CAP + 128, 16], F32,
                                  kind="Internal").ap(),
        "scr_accx": nc.dram_tensor("scr_accx", [Sx + 128, H], F32,
                                   kind="Internal").ap(),
        "zrecv": nc.dram_tensor("zrecv", [128, 16], F32,
                                kind="ExternalInput").ap(),
        "dumpc": nc.dram_tensor("dumpc", [128, 1], F32,
                                kind="ExternalInput").ap(),
    }
    es = ExitStack()
    with es:
        sb = lambda name, w, dt=F32, p=128: es.enter_context(
            nc.sbuf_tensor(name, [p, w], dt))
        buf = {
            "x1t": sb("x1t", 4096),
            "ln2_sb": sb("ln2_sb", 4096),
            "ln2_fm": sb("ln2_fm", 4096, BF16),
            "lnbf": sb("lnbf", 1024, BF16),
            "h_fm": sb("h_fm", 16384, BF16),
            "sgt": sb("sgt", 1024, BF16),
            "wi_sb": sb("wi_sb", 12288, BF16),
            "wg_sb": sb("wg_sb", 12288, BF16),
            "wo_sb": sb("wo_sb", 8192, BF16),
            "acc": sb("accb", 16),
            "r_sb": sb("r_sb", 2048),
            "gb_sb": sb("gb_sb", 1024),
            "bb_sb": sb("bb_sb", 1024),
            "ctile": sb("ctile", 64),
            "cols": sb("cols", 16),
            "sqscr": sb("sqscr", 1024),
            "bic": sb("bicsb", 256),
            "bgc": sb("bgcsb", 256),
            "rowb": sb("rowb", 3072, BF16, 1),
            "brf": sb("brf", 8, F32, 1),
            "brb": sb("brb", 8, BF16, 1),
            "wr_bf": sb("wr_bfsb", 64, BF16),
            "ident": sb("identsb", 128, BF16),
            "idf32": sb("idf32sb", 128),
            "ones": sb("onessb", 128, BF16, 1),
            "rec_sb": sb("rec_sb", n_tt * 64),
            "islots": sb("islots", n_tt * 32, mybir.dt.int32),
            "iota_c": sb("iota_csb", n_tt * 4),
            "onecol": sb("onecol", 1, F32, 8),
            "ecol": sb("ecolsb", 1, F32, 8),
            "grec": sb("grec", NSC * 32),
            "gtok": sb("gtok", NSC * 2, mybir.dt.int32),
            "zrec": sb("zrec", 16),
            "y_out": sb("y_out", 4096),
            "gln": sb("gln", 4096, BF16),
            "dumpc": sb("dumpcsb", 1),
            "dmsk": sb("dmsk", 8),
            "dslf": sb("dslf", 8),
            "pr": [es.enter_context(nc.psum_tensor(f"pr{i}", [128, 512], F32))
                   for i in range(8)],
        }
        ctx = Ctx(nc, es)
        ctx.GWI = (ctx.queue("gwi0", engine="q"), ctx.queue("gwi1", engine="q"),
                   ctx.queue("gwi2", engine="q"))
        ctx.GWG = (ctx.queue("gwg0", engine="q"), ctx.queue("gwg1", engine="q"),
                   ctx.queue("gwg2", engine="q"))
        ctx.GWO = (ctx.queue("gwo0", engine="q"), ctx.queue("gwo1", engine="q"))
        ctx.GL = (ctx.queue("gl0"), ctx.queue("gl1"))
        blk = es.enter_context(nc.Block())
        ctx.run_passes(blk, lambda c: emit_phase_b_sparse(c, io, buf, n_tt))
    return nc


def host_prep_b_sparse(inputs, n_tt=8):
    d = host_prep_b(inputs)
    Sx = n_tt * 512
    CAP = ((int(Sx * 2 / 8 * 3.0) + 127) // 128) * 128
    d["idf32"] = np.eye(128, dtype=np.float32)
    d["iota_c"] = np.ascontiguousarray(
        (np.arange(n_tt * 4)[None, :] * 128
         + np.arange(128)[:, None]).astype(np.float32))
    d["ecolv"] = (np.arange(8, dtype=np.float32) * CAP).reshape(8, 1)
    z = np.zeros((128, 16), np.float32)
    z[:, 8] = Sx + np.arange(128)
    d["zrecv"] = z
    d["dumpc"] = (8 * CAP + np.arange(128, dtype=np.float32)).reshape(128, 1)
    return d



def emit_phase_c(ctx, io, buf, n_ch=8, fused=False):
    T, V, Sc, Q, G, P = ctx.T, ctx.V, ctx.S, ctx.Q, ctx.G, ctx.P
    G0, G1 = ctx.GW
    GW = [G0, G1]
    trk = {"v": V, "s": Sc, "t": T, "p": P}

    x2_in, x3_out = io["x2"], io["x3"]
    scr_win = io["scr_win"]

    (x2t, lnbf, ln3_fm, win_sb, wout_sb, u_sb, zs_bf, yg_bf, upad, sigma,
     state, a_sb, dcol, wcv, cols, sqscr, x3t, ident) = (buf[k] for k in (
        "x2t", "lnbf", "ln3_fm", "win_sb", "wout_sb", "u_sb", "zs_bf",
        "yg_bf", "upad", "sigma", "state", "a_sb", "dcol", "wcv", "cols",
        "sqscr", "x3t", "ident"))
    pr = buf["pr"]
    free = {i: ("v", 0) for i in range(8)}

    if not fused:
        G.op(lambda g: g.dma_start(out=scr_win, in_=io["W_in"]))
    nc1 = Q.op(lambda q: q.dma_start(out=ident[:], in_=io["ident"]))
    nc1 = Q.op(lambda q: q.dma_start(
        out=a_sb[:].rearrange("p (c s) -> p c s", c=16),
        in_=io["A_logT"].rearrange("(c p) s -> p c s", p=128)))
    nc1 = Q.op(lambda q: q.dma_start(out=dcol[:], in_=io["D_param"]))
    nc1 = Q.op(lambda q: q.dma_start(out=wcv[:], in_=io["conv_w"]))
    nc2 = Q.op(lambda q: q.dma_start(out=buf["cbc"][:], in_=io["conv_b"]))
    nG = G.op(lambda g: g.dma_start(
        out=wout_sb[:].rearrange("p (k o) -> p k o", k=16),
        in_=io["W_out"].rearrange("(k p) o -> p k o", p=128)))
    n_g_setup = nG
    Sc.wait(Q, nc2)
    na1 = Sc.op(lambda s: s.activation(a_sb[:], a_sb[:], AF.Exp))
    V.wait(Sc, na1)
    na2 = V.op(lambda v: v.tensor_scalar_mul(a_sb[:], a_sb[:], -1.0))
    neps = V.op(lambda v: v.memset(cols[:, 14:15], 1e-5))
    nst0 = V.op(lambda v: v.memset(state[:], 0.0))
    nut0 = V.op(lambda v: v.memset(buf["utail"][:], 0.0))
    Sc.wait(V, neps)
    T.wait(Q, nc2)
    T.wait(G, nG)
    cbc = buf["cbc"]
    utail = buf["utail"]

    n_state_w = {}
    n_u_c = [0] * 16
    n_z_c = [0] * 16
    nacc_c = [0] * 16
    last_store = 0
    wlast = [0, 0]
    nsig_used = [0, 0, 0, 0]
    nsig_ext = [0, 0, 0, 0]
    nacc_p = 0
    n_conv_prev = 0
    n_gate_c = [0] * 16
    n_oproj_prev = 0
    n_oproj_mm_prev = 0
    n_xz_mm_prev = 0
    for ch in range(n_ch):
        Q.wait(V, n_oproj_prev)
        Q.wait(Q, Q.n)
        nload = Q.op(lambda q, ch=ch: q.dma_start(
            out=x2t[:].rearrange("p (tc h) -> p tc h", tc=4),
            in_=x2_in[ch * 512:(ch + 1) * 512, :].rearrange(
                "(tc p) h -> p tc h", p=128)))
        V.wait(Q, nload)
        V.wait(V, V.n)
        Sc.wait(T, n_xz_mm_prev)
        V.wait(T, n_oproj_mm_prev)

        nse = 0
        for tc in range(4):
            xi = x2t[:, tc * 1024:(tc + 1) * 1024]
            n1 = V.op(lambda v, xi=xi: v.reduce_sum(
                cols[:, 0:1], xi, axis=AX.X, negate=True))
            V.wait(V, n1)
            n2 = V.op(lambda v: v.tensor_scalar_mul(cols[:, 0:1],
                                                    cols[:, 0:1], 1.0 / H))
            V.wait(V, n2)
            n3 = V.op(lambda v, xi=xi: v.tensor_scalar_add(
                sqscr[:], xi, cols[:, 0:1]))
            Sc.wait(V, n3)
            n4 = Sc.op(lambda s: s.activation(
                buf["sq2"][:], sqscr[:], AF.Square, accum_out=cols[:, 1:2]))
            V.wait(Sc, n4)
            n5 = V.op(lambda v: v.tensor_scalar_mul(cols[:, 2:3],
                                                    cols[:, 1:2], 1.0 / H))
            Sc.wait(V, n5)
            n6 = Sc.op(lambda s: s.activation(cols[:, 3:4], cols[:, 2:3],
                                              AF.Ln, bias=cols[:, 14:15]))
            Sc.wait(Sc, n6)
            n7 = Sc.op(lambda s: s.activation(cols[:, 2:3], cols[:, 3:4],
                                              AF.Exp, scale=-0.5))
            V.wait(Sc, n7)
            n8 = V.op(lambda v: v.tensor_scalar_mul(lnbf[:], sqscr[:],
                                                    cols[:, 2:3]))
            T.wait(V, n8)
            for k in range(8):
                p0 = pr[k % 2]
                eng, cnt = free[k % 2]
                T.wait(trk[eng], cnt)
                nt = T.op(lambda t, k=k, p0=p0: t.transpose(
                    p0[:].bitcast(BF16)[:, 0:128], lnbf[:, ts(k)], ident[:]))
                Sc.wait(T, nt)
                nse = Sc.op(lambda s, k=k, tc=tc, p0=p0: s.copy(
                    ln3_fm[:, k * 512 + tc * 128: k * 512 + tc * 128 + 128],
                    p0[:].bitcast(BF16)[:, 0:128]))
                free[k % 2] = ("s", nse)
        n_lnfm = nse

        G.wait(G, n_g_setup)
        GW[0].wait(G, n_g_setup)
        GW[1].wait(G, n_g_setup)
        for mg in range(8):
            slot = mg % 2
            GW[slot].wait(T, wlast[slot])
            nwld = GW[slot].op(lambda g, mg=mg, slot=slot: g.dma_start(
                out=win_sb[:, slot * 4096:(slot + 1) * 4096].rearrange(
                    "p (k f) -> p k f", k=8),
                in_=scr_win.rearrange("(k p) f -> p k f", p=128)[
                    :, :, mg * 512:(mg + 1) * 512]))
            T.wait(GW[slot], nwld)
            for mb in range(4):
                m = mg * 4 + mb
                pp = pr[2 + m % 2]
                eng, cnt = free[2 + m % 2]
                T.wait(trk[eng], cnt)
                T.wait(Sc, n_lnfm)
                for k in range(8):
                    nt = T.op(lambda t, k=k, mb=mb, slot=slot, pp=pp:
                              t.matmul(
                        pp[:, 0:512],
                        win_sb[:, slot * 4096 + k * 512 + mb * 128:
                               slot * 4096 + k * 512 + mb * 128 + 128],
                        ln3_fm[:, ts(k, 512)],
                        start=(k == 0), stop=(k == 7)))
                if m < 16:
                    Sc.wait(T, nt)
                    Sc.wait(V, n_gate_c[m])
                    nse = Sc.op(lambda s, m=m, pp=pp: s.copy(
                        u_sb[:, ts(m, 512)], pp[:, 0:512]))
                    n_u_c[m] = nse
                    free[2 + m % 2] = ("s", nse)
                else:
                    Sc.wait(T, nt)
                    Sc.wait(V, n_gate_c[m - 16])
                    nse = Sc.op(lambda s, m=m, pp=pp: s.activation(
                        zs_bf[:, ts(m - 16, 512)], pp[:, 0:512], AF.Silu))
                    n_z_c[m - 16] = nse
                    free[2 + m % 2] = ("s", nse)
            wlast[slot] = nt
        n_xz = nse
        n_xz_mm_prev = nt

        for c in range(16):
            P.wait(Sc, n_u_c[c])
            P.wait(V, n_conv_prev)
            P.wait(P, P.n)
            pc1 = P.op(lambda p, c=c: p.tensor_copy(
                upad[:, 0:3], utail[:, c * 4: c * 4 + 3]))
            pc2 = P.op(lambda p, c=c: p.tensor_copy(
                upad[:, 3:515], u_sb[:, ts(c, 512)]))
            P.wait(P, pc2)
            nutw = P.op(lambda p, c=c: p.tensor_copy(
                utail[:, c * 4: c * 4 + 3], upad[:, 512:515]))
            V.wait(P, pc2)
            m1 = V.op(lambda v, c=c: v.tensor_scalar(
                u_sb[:, ts(c, 512)], upad[:, 0:512],
                wcv[:, c * 4: c * 4 + 1], cbc[:, c: c + 1],
                ALU.mult, ALU.add))
            V.wait(V, m1)
            for kk in range(1, 4):
                mk_ = V.op(lambda v, c=c, kk=kk: v.scalar_tensor_tensor(
                    u_sb[:, ts(c, 512)], upad[:, kk: kk + 512],
                    wcv[:, c * 4 + kk: c * 4 + kk + 1],
                    u_sb[:, ts(c, 512)], ALU.mult, ALU.add))
                V.wait(V, mk_)
            n_conv = mk_
            n_conv_prev = mk_
            for s_ in range(16):
                sl = s_ % 4
                ab = a_sb[:, c * 16 + s_: c * 16 + s_ + 1]
                V.wait(P, nsig_used[sl])
                V.wait(Sc, nsig_ext[sl])
                V.wait(Sc, n_state_w.get((c, s_), 0))
                nscan = V.op(lambda v, c=c, s_=s_, sl=sl, ab=ab:
                             v.tensor_tensor_scan(
                    sigma[:, sl * 512:(sl + 1) * 512],
                    ab.broadcast_to([128, 512]), u_sb[:, ts(c, 512)],
                    state[:, c * 16 + s_: c * 16 + s_ + 1],
                    ALU.mult, ALU.add))

                Sc.wait(V, nscan)
                next_ = Sc.op(lambda s, c=c, s_=s_, sl=sl: s.copy(
                    state[:, c * 16 + s_: c * 16 + s_ + 1],
                    sigma[:, sl * 512 + 511: sl * 512 + 512]))
                n_state_w[(c, s_)] = next_
                nsig_ext[sl] = next_
                P.wait(V, nscan)
                if s_ == 0:
                    P.wait(V, n_gate_c[c])
                    P.wait(P, nacc_p)
                    nacc_p = P.op(lambda p, c=c, sl=sl: p.tensor_copy(
                        buf["y_acc"][:, ts(c, 512)],
                        sigma[:, sl * 512:(sl + 1) * 512]))
                else:
                    P.wait(P, nacc_p)
                    nacc_p = P.op(lambda p, c=c, sl=sl: p.tensor_add(
                        buf["y_acc"][:, ts(c, 512)],
                        buf["y_acc"][:, ts(c, 512)],
                        sigma[:, sl * 512:(sl + 1) * 512]))
                nsig_used[sl] = nacc_p
            nacc_c[c] = nacc_p
        for c in range(16):
            V.wait(P, nacc_c[c])
            V.wait(Sc, n_z_c[c])
            ng1 = V.op(lambda v, c=c: v.scalar_tensor_tensor(
                buf["y_acc"][:, ts(c, 512)], u_sb[:, ts(c, 512)],
                dcol[:, c: c + 1], buf["y_acc"][:, ts(c, 512)],
                ALU.mult, ALU.add))
            V.wait(V, ng1)
            ng2 = V.op(lambda v, c=c: v.tensor_mul(
                yg_bf[:, ts(c, 512)], buf["y_acc"][:, ts(c, 512)],
                zs_bf[:, ts(c, 512)]))
            n_gate_c[c] = ng2
        n_gate = ng2

        T.wait(V, n_gate)
        for tc in range(4):
            for half in range(2):
                bank = 4 + (tc * 2 + half) % 4
                pp = pr[bank]
                eng, cnt = free[bank]
                T.wait(trk[eng], cnt)
                for k in range(16):
                    nt = T.op(lambda t, k=k, tc=tc, half=half, pp=pp:
                              t.matmul(
                        pp[:, 0:512],
                        yg_bf[:, k * 512 + tc * 128: k * 512 + tc * 128 + 128],
                        wout_sb[:, k * 1024 + half * 512:
                                k * 1024 + half * 512 + 512],
                        start=(k == 0), stop=(k == 15)))
                V.wait(T, nt)
                V.wait(Q, last_store)
                nv = V.op(lambda v, tc=tc, half=half, pp=pp: v.tensor_add(
                    x3t[:, tc * 1024 + half * 512:
                        tc * 1024 + half * 512 + 512],
                    pp[:, 0:512],
                    x2t[:, tc * 1024 + half * 512:
                        tc * 1024 + half * 512 + 512]))
                free[bank] = ("v", nv)
        n_oproj_prev = nv
        n_oproj_mm_prev = nt
        Q.wait(V, nv)
        last_store = Q.op(lambda q, ch=ch: q.dma_start(
            out=x3_out[ch * 512:(ch + 1) * 512, :].rearrange(
                "(tc p) h -> p tc h", p=128),
            in_=x3t[:].rearrange("p (tc h) -> p tc h", tc=4)))

    Q.wait(Q, last_store)


def build_phase_c(n_ch=8):
    nc = bass.Bass("TRN2", target_bir_lowering=False)
    Sx = n_ch * 512
    io = {
        "x2": nc.dram_tensor("x2", [Sx, H], F32, kind="ExternalInput").ap(),
        "W_in": nc.dram_tensor("W_in", [H, 4096], F32, kind="ExternalInput").ap(),
        "W_out": nc.dram_tensor("W_out", [2048, H], F32, kind="ExternalInput").ap(),
        "A_logT": nc.dram_tensor("A_logT", [2048, 16], F32, kind="ExternalInput").ap(),
        "D_param": nc.dram_tensor("D_param", [128, 16], F32, kind="ExternalInput").ap(),
        "conv_w": nc.dram_tensor("conv_w", [128, 64], F32, kind="ExternalInput").ap(),
        "conv_b": nc.dram_tensor("conv_b", [128, 16], F32, kind="ExternalInput").ap(),
        "ident": nc.dram_tensor("ident", [128, 128], BF16, kind="ExternalInput").ap(),
        "x3": nc.dram_tensor("x3", [Sx, H], F32, kind="ExternalOutput").ap(),
        "scr_win": nc.dram_tensor("scr_win", [H, 4096], BF16, kind="Internal").ap(),
    }
    es = ExitStack()
    with es:
        sb = lambda name, w, dt=F32, p=128: es.enter_context(
            nc.sbuf_tensor(name, [p, w], dt))
        buf = {
            "x2t": sb("x2t", 4096),
            "lnbf": sb("lnbf", 1024, BF16),
            "ln3_fm": sb("ln3_fm", 4096, BF16),
            "win_sb": sb("win_sb", 8192, BF16),
            "wout_sb": sb("wout_sb", 16384, BF16),
            "u_sb": sb("u_sb", 8192),
            "zs_bf": sb("zs_bf", 8192, BF16),
            "yg_bf": sb("yg_bf", 8192, BF16),
            "y_acc": sb("y_acc", 8192),
            "upad": sb("upad", 515),
            "sigma": sb("sigma", 2048),
            "state": sb("state", 256),
            "utail": sb("utail", 64),
            "a_sb": sb("a_sb", 256),
            "dcol": sb("dcol", 16),
            "wcv": sb("wcv", 64),
            "cbc": sb("cbc", 16),
            "cols": sb("colsc", 16),
            "sqscr": sb("sqscrc", 1024),
            "sq2": sb("sq2", 1024),
            "x3t": sb("x3t", 4096),
            "ident": sb("identsb", 128, BF16),
            "pr": [es.enter_context(nc.psum_tensor(f"pr{i}", [128, 512], F32))
                   for i in range(8)],
        }
        ctx = Ctx(nc, es)
        ctx.GW = (ctx.queue("gw0", engine="q"), ctx.queue("gw1", engine="q"))
        blk = es.enter_context(nc.Block())
        ctx.run_passes(blk, lambda c: emit_phase_c(c, io, buf, n_ch))
    return nc


def host_prep_c(inputs):
    return {
        "W_in": np.ascontiguousarray(inputs["L1_attn_W_in"], np.float32),
        "W_out": np.ascontiguousarray(inputs["L1_attn_W_out"], np.float32),
        "A_logT": np.ascontiguousarray(
            np.asarray(inputs["L1_attn_A_log"], np.float32).T),
        "D_param": np.ascontiguousarray(
            np.asarray(inputs["L1_attn_D_param"], np.float32).reshape(16, 128).T),
        "conv_w": np.ascontiguousarray(
            np.asarray(inputs["L1_attn_conv_w"], np.float32).reshape(
                16, 128, 4).transpose(1, 0, 2).reshape(128, 64)),
        "conv_b": np.ascontiguousarray(
            np.asarray(inputs["L1_attn_conv_b"], np.float32).reshape(16, 128).T),
        "ident": np.eye(128, dtype=ml_dtypes.bfloat16),
    }




def emit_phase_d(ctx, io, buf, n_tt=8, fused=False):
    T, V, Sc, Q, G = ctx.T, ctx.V, ctx.S, ctx.Q, ctx.G
    GW1, GW2 = ctx.GW1, ctx.GW2
    trk = {"v": V, "s": Sc, "t": T}
    E, D = 8, 2048

    x3_in, x4_out = io["x3"], io["x4"]
    scr_w1, scr_w2 = io["scr_w1"], io["scr_w2"]

    (x3t, lnbf, ln4_fm, w_sb, h_fm, acc, ctile, cols, sqscr, b1c, rowb,
     brf, brb, wr_bf, ident, ones) = (buf[k] for k in (
        "x3t", "lnbf", "ln4_fm", "w_sb", "h_fm", "acc", "ctile", "cols",
        "sqscr", "b1c", "rowb", "brf", "brb", "wr_bf", "ident",
        "ones"))
    w2_sb = buf["w2_sb"]
    pr = buf["pr"]
    free = {i: ("v", 0) for i in range(8)}

    if not fused:
        for e in range(E):
            G.op(lambda g, e=e: g.dma_start(out=scr_w1[e], in_=io["W1"][e]))
            G.op(lambda g, e=e: g.dma_start(out=scr_w2[e], in_=io["W2"][e]))

    nc1 = Q.op(lambda q: q.dma_start(out=ident[:], in_=io["ident"]))
    nc1 = Q.op(lambda q: q.dma_start(out=ones[:], in_=io["ones1"]))
    nc1 = Q.op(lambda q: q.dma_start(out=b1c[:], in_=io["b1c"]))
    nc1 = Q.op(lambda q: q.dma_start(out=brf[:], in_=io["br2"]))
    nG = G.op(lambda g: g.dma_start(
        out=wr_bf[:].rearrange("p (k e) -> p k e", k=8),
        in_=io["Wr2"].rearrange("(k p) e -> p k e", p=128)))
    n_g_setup = nG
    V.wait(Q, nc1)
    nbr = V.op(lambda v: v.tensor_copy(brb[:], brf[:]))
    neps = V.op(lambda v: v.memset(cols[:, 14:15], 1e-5))
    Sc.wait(V, neps)
    T.wait(Q, nc1)
    T.wait(G, nG)
    T.wait(V, nbr)

    last_store = 0
    for tt in range(n_tt):
        ctx.barrier()
        nload = Q.op(lambda q, tt=tt: q.dma_start(
            out=x3t[:].rearrange("p (tc h) -> p tc h", tc=4),
            in_=x3_in[tt * 512:(tt + 1) * 512, :].rearrange(
                "(tc p) h -> p tc h", p=128)))
        V.wait(Q, nload)

        nse = 0
        for tc in range(4):
            xi = x3t[:, tc * 1024:(tc + 1) * 1024]
            n1 = V.op(lambda v, xi=xi: v.reduce_sum(
                cols[:, 0:1], xi, axis=AX.X, negate=True))
            V.wait(V, n1)
            n2 = V.op(lambda v: v.tensor_scalar_mul(cols[:, 0:1],
                                                    cols[:, 0:1], 1.0 / H))
            V.wait(V, n2)
            n3 = V.op(lambda v, xi=xi: v.tensor_scalar_add(
                sqscr[:], xi, cols[:, 0:1]))
            Sc.wait(V, n3)
            n4 = Sc.op(lambda s: s.activation(
                buf["sq2"][:], sqscr[:], AF.Square, accum_out=cols[:, 1:2]))
            V.wait(Sc, n4)
            n5 = V.op(lambda v: v.tensor_scalar_mul(cols[:, 2:3],
                                                    cols[:, 1:2], 1.0 / H))
            Sc.wait(V, n5)
            n6 = Sc.op(lambda s: s.activation(cols[:, 3:4], cols[:, 2:3],
                                              AF.Ln, bias=cols[:, 14:15]))
            Sc.wait(Sc, n6)
            n7 = Sc.op(lambda s: s.activation(cols[:, 2:3], cols[:, 3:4],
                                              AF.Exp, scale=-0.5))
            V.wait(Sc, n7)
            n8 = V.op(lambda v: v.tensor_scalar_mul(lnbf[:], sqscr[:],
                                                    cols[:, 2:3]))
            T.wait(V, n8)
            for k in range(8):
                p0 = pr[k % 2]
                eng, cnt = free[k % 2]
                T.wait(trk[eng], cnt)
                nt = T.op(lambda t, k=k, p0=p0: t.transpose(
                    p0[:].bitcast(BF16)[:, 0:128], lnbf[:, ts(k)], ident[:]))
                Sc.wait(T, nt)
                nse = Sc.op(lambda s, k=k, tc=tc, p0=p0: s.copy(
                    ln4_fm[:, k * 512 + tc * 128: k * 512 + tc * 128 + 128],
                    p0[:].bitcast(BF16)[:, 0:128]))
                free[k % 2] = ("s", nse)
        n_lnfm = nse

        prev_nC = 0
        for tc in range(4):
            eng, cnt = free[4]
            T.wait(trk[eng], cnt)
            T.wait(Sc, n_lnfm)
            for k in range(8):
                nt = T.op(lambda t, k=k, tc=tc: t.matmul(
                    pr[4][:, 0:8],
                    ln4_fm[:, k * 512 + tc * 128: k * 512 + tc * 128 + 128],
                    wr_bf[:, k * 8: k * 8 + 8], start=(k == 0), stop=False))
            nt = T.op(lambda t: t.matmul(
                pr[4][:, 0:8], ones[:], brb[:], start=False, stop=True))
            V.wait(T, nt)
            V.wait(Sc, prev_nC)
            nA = V.op(lambda v: v.reduce_max(cols[:, 4:5], pr[4][:, 0:8],
                                             axis=AX.X, negate=True))
            V.wait(V, nA)
            nB = V.op(lambda v: v.tensor_scalar_add(ctile[:, 48:56],
                                                    pr[4][:, 0:8],
                                                    cols[:, 4:5]))
            free[4] = ("v", nB)
            Sc.wait(V, nB)
            nC = Sc.op(lambda s: s.activation(ctile[:, 40:48], ctile[:, 48:56],
                                              AF.Exp))
            prev_nC = nC
            V.wait(Sc, nC)
            nS = V.op(lambda v: v.reduce_sum(cols[:, 7:8], ctile[:, 40:48],
                                             axis=AX.X))
            V.wait(V, nS)
            nJ = V.op(lambda v: v.reciprocal(cols[:, 7:8], cols[:, 7:8]))
            nD = V.op(lambda v: v.reduce_max(cols[:, 5:6], ctile[:, 40:48],
                                             axis=AX.X))
            V.wait(V, nD)
            nE = V.op(lambda v: v.tensor_single_scalar(
                ctile[:, 32:40], ctile[:, 40:48], cols[:, 5:6], ALU.is_equal))
            V.wait(V, nE)
            nF = V.op(lambda v: v.scalar_tensor_tensor(
                ctile[:, 32:40], ctile[:, 32:40], NEG, ctile[:, 40:48],
                ALU.mult, ALU.add))
            V.wait(V, nF)
            nGm = V.op(lambda v: v.reduce_max(cols[:, 6:7], ctile[:, 32:40],
                                              axis=AX.X))
            V.wait(V, nGm)
            nH = V.op(lambda v: v.tensor_single_scalar(
                ctile[:, 32:40], ctile[:, 40:48], cols[:, 6:7], ALU.is_ge))
            V.wait(V, nH)
            nK = V.op(lambda v: v.tensor_mul(ctile[:, 32:40], ctile[:, 32:40],
                                             ctile[:, 40:48]))
            V.wait(V, nK)
            nL = V.op(lambda v, tc=tc: v.tensor_scalar_mul(
                ctile[:, tc * 8:(tc + 1) * 8], ctile[:, 32:40], cols[:, 7:8]))
            nM = V.op(lambda v, tc=tc: v.tensor_copy(
                acc[:, tc * 1024:(tc + 1) * 1024],
                x3t[:, tc * 1024:(tc + 1) * 1024]))
        n_accinit = nM

        wlast_w1 = [0, 0]
        wlast_w2 = [0, 0]
        n_ymm_prev = 0
        for e in range(E):
            G.wait(G, n_g_setup)
            G.wait(T, n_ymm_prev)
            G.wait(G, G.n)
            nrow = G.op(lambda g, e=e: g.dma_start(
                out=rowb[:], in_=io["rows2"][e: e + 1, :]))
            T.wait(G, nrow)

            Sc.wait(T, n_ymm_prev)
            for fc in range(4):
                slot = fc % 2
                GW1[slot].wait(T, wlast_w1[slot])
                nwld = GW1[slot].op(lambda g, e=e, fc=fc, slot=slot:
                                    g.dma_start(
                    out=w_sb[:, slot * 4096:(slot + 1) * 4096].rearrange(
                        "p (k f) -> p k f", k=8),
                    in_=scr_w1[e].rearrange("(k p) f -> p k f", p=128)[
                        :, :, fc * 512:(fc + 1) * 512]))
                T.wait(GW1[slot], nwld)
                for fb in range(4):
                    f = fc * 4 + fb
                    pp = pr[f % 4]
                    eng, cnt = free[f % 4]
                    T.wait(trk[eng], cnt)
                    for k in range(8):
                        nt = T.op(lambda t, k=k, fb=fb, slot=slot, pp=pp:
                                  t.matmul(
                            pp[:, 0:512],
                            w_sb[:, slot * 4096 + k * 512 + fb * 128:
                                 slot * 4096 + k * 512 + fb * 128 + 128],
                            ln4_fm[:, ts(k, 512)],
                            start=(k == 0), stop=(k == 7)))
                    Sc.wait(T, nt)
                    nse = Sc.op(lambda s, f=f, e=e, pp=pp: s.activation(
                        h_fm[:, ts(f, 512)], pp[:, 0:512], AF.Relu,
                        bias=b1c[:, e * 16 + f: e * 16 + f + 1]))
                    free[f % 4] = ("s", nse)
                wlast_w1[slot] = nt
            n_h = nse

            T.wait(Sc, n_h)
            for b_ in range(8):
                eng, cnt = free[b_]
                T.wait(trk[eng], cnt)
            for dg in range(4):
                slot = dg % 2
                GW2[slot].wait(T, wlast_w2[slot])
                nwld = GW2[slot].op(lambda g, e=e, dg=dg, slot=slot:
                                    g.dma_start(
                    out=w2_sb[:, slot * 4096:(slot + 1) * 4096].rearrange(
                        "p (g o) -> p g o", g=4),
                    in_=scr_w2[e][dg * 512:(dg + 1) * 512, :].rearrange(
                        "(g p) o -> p g o", p=128)))
                T.wait(GW2[slot], nwld)
                for tc in range(4):
                    for fb in range(4):
                        f = dg * 4 + fb
                        for half in range(2):
                            nt = T.op(lambda t, f=f, tc=tc, fb=fb, half=half,
                                      slot=slot: t.matmul(
                                pr[tc * 2 + half][:, 0:512],
                                h_fm[:, f * 512 + tc * 128:
                                     f * 512 + tc * 128 + 128],
                                w2_sb[:, slot * 4096 + fb * 1024 + half * 512:
                                      slot * 4096 + fb * 1024 + half * 512
                                      + 512],
                                start=(dg == 0 and fb == 0), stop=False))
                wlast_w2[slot] = nt
            for tc in range(4):
                for half in range(2):
                    nt = T.op(lambda t, tc=tc, half=half: t.matmul(
                        pr[tc * 2 + half][:, 0:512], ones[:],
                        rowb[0:1, half * 512: half * 512 + 512],
                        start=False, stop=True))
            n_ymm = nt
            n_ymm_prev = nt
            V.wait(V, n_accinit)
            for tc in range(4):
                for half in range(2):
                    V.wait(T, n_ymm)
                    nv = V.op(lambda v, tc=tc, half=half, e=e:
                              v.scalar_tensor_tensor(
                        acc[:, tc * 1024 + half * 512:
                            tc * 1024 + half * 512 + 512],
                        pr[tc * 2 + half][:, 0:512],
                        ctile[:, tc * 8 + e: tc * 8 + e + 1],
                        acc[:, tc * 1024 + half * 512:
                            tc * 1024 + half * 512 + 512],
                        ALU.mult, ALU.add))
                    free[tc * 2 + half] = ("v", nv)

        ctx.barrier()
        last_store = Q.op(lambda q, tt=tt: q.dma_start(
            out=x4_out[tt * 512:(tt + 1) * 512, :].rearrange(
                "(tc p) h -> p tc h", p=128),
            in_=acc[:].rearrange("p (tc h) -> p tc h", tc=4)))

    Q.wait(Q, last_store)


def build_phase_d(n_tt=8):
    nc = bass.Bass("TRN2", target_bir_lowering=False)
    Sx = n_tt * 512
    E, D = 8, 2048
    io = {
        "x3": nc.dram_tensor("x3", [Sx, H], F32, kind="ExternalInput").ap(),
        "W1": nc.dram_tensor("W1", [E, H, D], F32, kind="ExternalInput").ap(),
        "W2": nc.dram_tensor("W2", [E, D, H], F32, kind="ExternalInput").ap(),
        "Wr2": nc.dram_tensor("Wr2", [H, E], F32, kind="ExternalInput").ap(),
        "b1c": nc.dram_tensor("b1c", [128, 128], F32, kind="ExternalInput").ap(),
        "rows2": nc.dram_tensor("rows2", [E, 1024], F32, kind="ExternalInput").ap(),
        "br2": nc.dram_tensor("br2", [1, 8], F32, kind="ExternalInput").ap(),
        "ident": nc.dram_tensor("ident", [128, 128], BF16, kind="ExternalInput").ap(),
        "ones1": nc.dram_tensor("ones1", [1, 128], BF16, kind="ExternalInput").ap(),
        "x4": nc.dram_tensor("x4", [Sx, H], F32, kind="ExternalOutput").ap(),
        "scr_w1": nc.dram_tensor("scr_w1", [E, H, D], BF16, kind="Internal").ap(),
        "scr_w2": nc.dram_tensor("scr_w2", [E, D, H], BF16, kind="Internal").ap(),
    }
    es = ExitStack()
    with es:
        sb = lambda name, w, dt=F32, p=128: es.enter_context(
            nc.sbuf_tensor(name, [p, w], dt))
        buf = {
            "x3t": sb("x3t", 4096),
            "lnbf": sb("lnbf", 1024, BF16),
            "ln4_fm": sb("ln4_fm", 4096, BF16),
            "w_sb": sb("w_sb", 8192, BF16),
            "w2_sb": sb("w2_sb", 8192, BF16),
            "h_fm": sb("h_fm", 8192, BF16),
            "acc": sb("acc", 4096),
            "ctile": sb("ctile", 64),
            "cols": sb("colsd", 16),
            "sqscr": sb("sqscrd", 1024),
            "sq2": sb("sq2d", 1024),
            "b1c": sb("b1csb", 128),
            "rowb": sb("rowbd", 1024, BF16, 1),
            "brf": sb("brfd", 8, F32, 1),
            "brb": sb("brbd", 8, BF16, 1),
            "wr_bf": sb("wr_bfd", 64, BF16),
            "ident": sb("identsb", 128, BF16),
            "ones": sb("onessb", 128, BF16, 1),
            "pr": [es.enter_context(nc.psum_tensor(f"pr{i}", [128, 512], F32))
                   for i in range(8)],
        }
        ctx = Ctx(nc, es)
        ctx.GW1 = (ctx.queue("gd10", engine="q"), ctx.queue("gd11", engine="q"))
        ctx.GW2 = (ctx.queue("gd20", engine="q"), ctx.queue("gd21", engine="q"))
        blk = es.enter_context(nc.Block())
        ctx.run_passes(blk, lambda c: emit_phase_d(c, io, buf, n_tt))
    return nc


def host_prep_d(inputs):
    b1 = np.asarray(inputs["L1_ffn_b1"], np.float32)
    return {
        "W1": np.ascontiguousarray(inputs["L1_ffn_W1"], np.float32),
        "W2": np.ascontiguousarray(inputs["L1_ffn_W2"], np.float32),
        "Wr2": np.ascontiguousarray(inputs["L1_ffn_Wr"], np.float32),
        "b1c": np.ascontiguousarray(
            b1.reshape(8, 16, 128).transpose(2, 0, 1).reshape(128, 128)),
        "rows2": np.ascontiguousarray(inputs["L1_ffn_b2"], np.float32),
        "br2": np.ascontiguousarray(
            np.asarray(inputs["L1_ffn_br"], np.float32).reshape(1, 8)),
        "ident": np.eye(128, dtype=ml_dtypes.bfloat16),
        "ones1": np.ones((1, 128), dtype=ml_dtypes.bfloat16),
    }


def emit_phase_d_sparse(ctx, io, buf, n_tt=8, fused=False):
    T, V, Sc, Q, G = ctx.T, ctx.V, ctx.S, ctx.Q, ctx.G
    GW1, GW2, GL = ctx.GW1, ctx.GW2, ctx.GL
    trk = {"v": V, "s": Sc, "t": T}
    E, D = 8, 2048
    TTOK = n_tt * 512
    NTC = n_tt * 4
    CAP = ((int(TTOK * 2 / 8 * 3.0) + 127) // 128) * 128
    NSC = CAP // 128
    SUBT = [min(512, CAP - s) for s in range(0, CAP, 512)]

    x3_in = io["x3"]
    x4_acc = io["scr_accx2"]
    scr_w1, scr_w2 = io["scr_w1"], io["scr_w2"]
    scr_ln, scr_rec = io["scr_lnd"], io["scr_recd"]

    (x3t, lnbf, ln4_fm, w_sb, w2_sb, h_fm, ctile, cols, sqscr, b1c, rowb,
     brf, brb, wr_bf, ident, ones) = (buf[k] for k in (
        "x3t", "lnbf", "ln4_fm", "w_sb", "w2_sb", "h_fm", "ctile", "cols",
        "sqscr", "b1c", "rowb", "brf", "brb", "wr_bf", "ident", "ones"))
    pr = buf["pr"]
    rec_sb, islots, iota_c = buf["rec_sb"], buf["islots"], buf["iota_c"]
    onecol, ecol, idf32 = buf["onecol"], buf["ecol"], buf["idf32"]
    grec, gtok, zrec, y_out = (buf["grec"], buf["gtok"], buf["zrec"],
                               buf["y_out"])
    gbf = buf["gln"][:]
    ctx.n_gb_last = 0
    cfm = h_fm[:].bitcast(F32)[0:8, 0:TTOK]
    msk = w_sb[:].bitcast(F32)[0:8, 0:TTOK]
    msk2 = w2_sb[:].bitcast(F32)[0:8, 0:TTOK]
    free = {i: ("v", 0) for i in range(8)}

    if not fused:
        for e in range(E):
            G.op(lambda g, e=e: g.dma_start(out=scr_w1[e], in_=io["W1"][e]))
            G.op(lambda g, e=e: g.dma_start(out=scr_w2[e], in_=io["W2"][e]))

    nc1 = Q.op(lambda q: q.dma_start(out=ident[:], in_=io["ident"]))
    nc1 = Q.op(lambda q: q.dma_start(out=idf32[:], in_=io["idf32"]))
    nc1 = Q.op(lambda q: q.dma_start(out=ones[:], in_=io["ones1"]))
    nc1 = Q.op(lambda q: q.dma_start(out=b1c[:], in_=io["b1c"]))
    nc1 = Q.op(lambda q: q.dma_start(out=brf[:], in_=io["br2"]))
    nc1 = Q.op(lambda q: q.dma_start(out=iota_c[:], in_=io["iota_c"]))
    nc1 = Q.op(lambda q: q.dma_start(out=ecol[:], in_=io["ecolv"]))
    nc1 = Q.op(lambda q: q.dma_start(out=zrec[:], in_=io["zrecv"]))
    nG = G.op(lambda g: g.dma_start(
        out=wr_bf[:].rearrange("p (k e) -> p k e", k=8),
        in_=io["Wr2"].rearrange("(k p) e -> p k e", p=128)))
    n_g_setup = nG
    G.wait(G, nG)
    V.wait(Q, nc1)
    nbr = V.op(lambda v: v.tensor_copy(brb[:], brf[:]))
    neps = V.op(lambda v: v.memset(cols[:, 14:15], 1e-5))
    noc = V.op(lambda v: v.memset(onecol[:], 1.0))
    noc = V.op(lambda v: v.memset(rec_sb[:], 0.0))
    Sc.wait(V, neps)
    T.wait(Q, nc1)
    T.wait(G, nG)
    T.wait(V, nbr)
    G.wait(Q, nc1)
    nzf = V.op(lambda v: v.memset(lnbf[:], 0.0))
    Q.wait(V, nzf)
    nzfst = Q.op(lambda q: q.dma_start(out=scr_ln[TTOK:TTOK + 128, :],
                                       in_=lnbf[:]))
    Q.wait(Q, nc1)
    nzrec = 0
    for z in range(E * CAP // 128):
        nzrec = Q.op(lambda q, z=z: q.dma_start(
            out=scr_rec[z * 128:(z + 1) * 128, :], in_=zrec[:]))

    last_cfm = 0
    n_lnst = nzfst
    nR = 0
    for tt in range(n_tt):
        ctx.barrier()
        nload = Q.op(lambda q, tt=tt: q.dma_start(
            out=x3t[:].rearrange("p (tc h) -> p tc h", tc=4),
            in_=x3_in[tt * 512:(tt + 1) * 512, :].rearrange(
                "(tc p) h -> p tc h", p=128)))
        V.wait(Q, nload)
        Q.wait(Q, nload)
        nacc0 = Q.op(lambda q, tt=tt: q.dma_start(
            out=x4_acc[tt * 512:(tt + 1) * 512, :].rearrange(
                "(tc p) h -> p tc h", p=128),
            in_=x3t[:].rearrange("p (tc h) -> p tc h", tc=4)))
        ctx.nacc0d = nacc0

        for tc in range(4):
            gtc = tt * 4 + tc
            xi = x3t[:, tc * 1024:(tc + 1) * 1024]
            n1 = V.op(lambda v, xi=xi: v.reduce_sum(
                cols[:, 0:1], xi, axis=AX.X, negate=True))
            V.wait(V, n1)
            n2 = V.op(lambda v: v.tensor_scalar_mul(cols[:, 0:1],
                                                    cols[:, 0:1], 1.0 / H))
            V.wait(V, n2)
            n3 = V.op(lambda v, xi=xi: v.tensor_scalar_add(
                sqscr[:], xi, cols[:, 0:1]))
            Sc.wait(V, n3)
            n4 = Sc.op(lambda s: s.activation(
                buf["sq2"][:], sqscr[:], AF.Square, accum_out=cols[:, 1:2]))
            V.wait(Sc, n4)
            n5 = V.op(lambda v: v.tensor_scalar_mul(cols[:, 2:3],
                                                    cols[:, 1:2], 1.0 / H))
            Sc.wait(V, n5)
            n6 = Sc.op(lambda s: s.activation(cols[:, 3:4], cols[:, 2:3],
                                              AF.Ln, bias=cols[:, 14:15]))
            Sc.wait(Sc, n6)
            n7 = Sc.op(lambda s: s.activation(cols[:, 2:3], cols[:, 3:4],
                                              AF.Exp, scale=-0.5))
            V.wait(Sc, n7)
            V.wait(Q, n_lnst)
            n8 = V.op(lambda v: v.tensor_scalar_mul(lnbf[:], sqscr[:],
                                                    cols[:, 2:3]))
            Q.wait(V, n8)
            n_lnst = Q.op(lambda q, gtc=gtc: q.dma_start(
                out=scr_ln[gtc * 128:(gtc + 1) * 128, :], in_=lnbf[:]))
            T.wait(V, n8)
            for k in range(8):
                p0 = pr[k % 2]
                eng, cnt = free[k % 2]
                T.wait(trk[eng], cnt)
                nt = T.op(lambda t, k=k, p0=p0: t.transpose(
                    p0[:].bitcast(BF16)[:, 0:128], lnbf[:, ts(k)], ident[:]))
                Sc.wait(T, nt)
                nse = Sc.op(lambda s, k=k, tc=tc, p0=p0: s.copy(
                    ln4_fm[:, k * 512 + tc * 128: k * 512 + tc * 128 + 128],
                    p0[:].bitcast(BF16)[:, 0:128]))
                free[k % 2] = ("s", nse)
            n_lnfm = nse
            eng, cnt = free[4]
            T.wait(trk[eng], cnt)
            T.wait(Sc, n_lnfm)
            for k in range(8):
                nt = T.op(lambda t, k=k, tc=tc: t.matmul(
                    pr[4][:, 0:8],
                    ln4_fm[:, k * 512 + tc * 128: k * 512 + tc * 128 + 128],
                    wr_bf[:, k * 8: k * 8 + 8], start=(k == 0), stop=False))
            nt = T.op(lambda t: t.matmul(
                pr[4][:, 0:8], ones[:], brb[:], start=False, stop=True))
            V.wait(T, nt)
            nA = V.op(lambda v: v.reduce_max(cols[:, 4:5], pr[4][:, 0:8],
                                             axis=AX.X, negate=True))
            V.wait(V, nA)
            nB = V.op(lambda v: v.tensor_scalar_add(ctile[:, 48:56],
                                                    pr[4][:, 0:8],
                                                    cols[:, 4:5]))
            free[4] = ("v", nB)
            Sc.wait(V, nB)
            nC = Sc.op(lambda s: s.activation(ctile[:, 40:48],
                                              ctile[:, 48:56], AF.Exp))
            V.wait(Sc, nC)
            nS = V.op(lambda v: v.reduce_sum(cols[:, 7:8], ctile[:, 40:48],
                                             axis=AX.X))
            V.wait(V, nS)
            nJ = V.op(lambda v: v.reciprocal(cols[:, 7:8], cols[:, 7:8]))
            nD = V.op(lambda v: v.reduce_max(cols[:, 5:6], ctile[:, 40:48],
                                             axis=AX.X))
            V.wait(V, nD)
            nE = V.op(lambda v: v.tensor_single_scalar(
                ctile[:, 32:40], ctile[:, 40:48], cols[:, 5:6],
                ALU.is_equal))
            V.wait(V, nE)
            nF = V.op(lambda v: v.scalar_tensor_tensor(
                ctile[:, 32:40], ctile[:, 32:40], NEG, ctile[:, 40:48],
                ALU.mult, ALU.add))
            V.wait(V, nF)
            nGm = V.op(lambda v: v.reduce_max(cols[:, 6:7], ctile[:, 32:40],
                                              axis=AX.X))
            V.wait(V, nGm)
            nH = V.op(lambda v: v.tensor_single_scalar(
                ctile[:, 32:40], ctile[:, 40:48], cols[:, 6:7], ALU.is_ge))
            V.wait(V, nH)
            nK = V.op(lambda v: v.tensor_mul(ctile[:, 32:40],
                                             ctile[:, 32:40],
                                             ctile[:, 40:48]))
            V.wait(V, nK)
            nL = V.op(lambda v: v.tensor_scalar_mul(
                ctile[:, 0:8], ctile[:, 32:40], cols[:, 7:8]))
            V.wait(V, nL)
            nR = V.op(lambda v, gtc=gtc: v.tensor_copy(
                rec_sb[:, gtc * 16: gtc * 16 + 8], ctile[:, 0:8]))
            nR = V.op(lambda v, gtc=gtc: v.tensor_copy(
                rec_sb[:, gtc * 16 + 8: gtc * 16 + 9],
                iota_c[:, gtc: gtc + 1]))
            nCb = V.op(lambda v: v.tensor_copy(
                ctile[:].bitcast(BF16)[:, 112:120], ctile[:, 0:8]))
            eng, cnt = free[5]
            T.wait(trk[eng], cnt)
            T.wait(V, nCb)
            ntc2 = T.op(lambda t: t.transpose(
                pr[5][:].bitcast(BF16)[0:8, 0:128],
                ctile[:].bitcast(BF16)[:, 112:120], ident[:]))
            V.wait(T, ntc2)
            V.wait(V, last_cfm)
            ncf = V.op(lambda v, gtc=gtc: v.tensor_copy(
                cfm[0:8, gtc * 128:(gtc + 1) * 128],
                pr[5][:].bitcast(BF16)[0:8, 0:128]))
            last_cfm = ncf
            free[5] = ("v", ncf)

    ctx.barrier()
    nm = V.op(lambda v: v.tensor_single_scalar(msk, cfm, 0.0, ALU.is_gt))
    V.wait(V, nm)
    nm = V.op(lambda v: v.tensor_tensor_scan(
        cfm, onecol[:].broadcast_to([8, TTOK]), msk, 0.0,
        ALU.mult, ALU.add))
    V.wait(V, nm)
    nm = V.op(lambda v: v.tensor_scalar_add(cfm, cfm, -1.0))
    V.wait(V, nm)
    nm = V.op(lambda v: v.tensor_single_scalar(msk2, cfm, float(CAP) - 0.5,
                                               ALU.is_lt))
    V.wait(V, nm)
    nm = V.op(lambda v: v.tensor_mul(msk, msk, msk2))
    V.wait(V, nm)
    nm = V.op(lambda v: v.tensor_scalar(cfm, cfm, ecol[:, 0:1], None,
                                        ALU.add))
    V.wait(V, nm)
    nm = V.op(lambda v: v.tensor_mul(cfm, cfm, msk))
    V.wait(V, nm)
    nm = V.op(lambda v: v.scalar_tensor_tensor(
        msk2, msk, -1.0, onecol[:].broadcast_to([8, TTOK]),
        ALU.mult, ALU.add))
    V.wait(V, nm)
    nm = V.op(lambda v: v.scalar_tensor_tensor(
        cfm, msk2, 1e6, cfm, ALU.mult, ALU.add))
    V.wait(V, nm)
    nslt = 0
    nsc = 0
    bcreg = [None]
    def _bc(g):
        if bcreg[0] is None:
            bcreg[0] = g.to_reg(E * CAP - 1)
        return bcreg[0]
    G.wait(Q, nzrec)
    G.wait(V, nR)
    T.wait(V, nm)
    for gtc in range(NTC):
        eng, cnt = free[gtc % 2]
        T.wait(trk[eng], cnt)
        ntx = T.op(lambda t, gtc=gtc: t.transpose(
            pr[gtc % 2][:, 0:8], cfm[0:8, gtc * 128:(gtc + 1) * 128],
            idf32[0:8, 0:8]))
        V.wait(T, ntx)
        V.wait(V, nslt)
        nslt = V.op(lambda v, gtc=gtc: v.tensor_copy(
            islots[:, gtc * 8:(gtc + 1) * 8], pr[gtc % 2][:, 0:8]))
        free[gtc % 2] = ("v", nslt)
        G.wait(V, nslt)
        for e in range(E):
            nsc = G.op(lambda g, gtc=gtc, e=e: g.indirect_dma_start(
                out=scr_rec,
                out_offset=bass.IndirectOffsetOnAxis(
                    ap=islots[:, gtc * 8 + e: gtc * 8 + e + 1], axis=0),
                in_=rec_sb[:, gtc * 16:(gtc + 1) * 16],
                in_offset=None,
                bounds_check=_bc(g), oob_is_err=False))

    ctx.barrier()
    for qq in (GW1[0], GW1[1], GW2[0], GW2[1]):
        qq.wait(G, n_g_setup)
    wlast_w1 = [0, 0]
    wlast_w2 = [0, 0]
    n_ymm_prev = 0
    n_ysc_prev = 0
    n_ysc_slot = [0, 0, 0, 0]
    n_gtrans = 0
    NTILES = len(SUBT)
    tiles = [(e_, ti_) for e_ in range(E) for ti_ in range(NTILES)]
    grec_q = [grec[:, 0:NSC * 16], grec[:, NSC * 16:NSC * 32]]
    gtok_q = [gtok[:, 0:NSC], gtok[:, NSC:2 * NSC]]
    ntk_q = [0, 0]
    n_grec_read_exp = {}
    n_ysc_exp_end = {}
    pend_ngb = {}

    def emit_grec(e2):
        p = e2 % 2
        GL[0].wait(GL[0], GL[0].n)
        GL[0].wait(G, nsc)
        GL[0].wait(V, n_grec_read_exp.get(e2 - 2, 0))
        ngrec = GL[0].op(lambda g, e2=e2, p=p: g.dma_start(
            out=grec_q[p].rearrange("p (s r) -> p s r", s=NSC),
            in_=scr_rec[e2 * CAP:(e2 + 1) * CAP, :].rearrange(
                "(s p) r -> p s r", p=128)))
        V.wait(GL[0], ngrec)
        V.wait(GL[1], n_ysc_exp_end.get(e2 - 2, 0))
        ntk_q[p] = V.op(lambda v, p=p: v.tensor_copy(
            gtok_q[p],
            grec_q[p].rearrange("p (s r) -> p s r", s=NSC)[:, :, 8]))

    def emit_gathers(gi2):
        e2, ti2 = tiles[gi2]
        p = e2 % 2
        GL[0].wait(GL[0], GL[0].n)
        GL[0].wait(V, ntk_q[p])
        GL[0].wait(T, n_gtrans)
        ngb = 0
        for s in range(SUBT[ti2] // 128):
            sc_ = ti2 * 4 + s
            ngb = GL[0].op(lambda g, sc_=sc_, s=s, p=p: g.indirect_dma_start(
                out=gbf[:, s * 1024:(s + 1) * 1024],
                out_offset=None,
                in_=scr_ln,
                in_offset=bass.IndirectOffsetOnAxis(
                    ap=gtok_q[p][:, sc_: sc_ + 1], axis=0)))
        pend_ngb[gi2] = ngb

    G.wait(G, n_g_setup)
    emit_grec(0)
    emit_gathers(0)
    for gi, (e, ti) in enumerate(tiles):
        wt = SUBT[ti]
        toff = ti * 512
        ntc_w = wt // 128
        p_ = e % 2
        if ti == 0:
            G.wait(T, n_ymm_prev)
            G.wait(G, G.n)
            nrow = G.op(lambda g, e=e: g.dma_start(
                out=rowb[:], in_=io["rows2"][e: e + 1, :]))
            T.wait(G, nrow)
        if True:
            T.wait(GL[0], pend_ngb[gi])
            Sc.wait(T, n_ymm_prev)
            nse = 0
            for s in range(ntc_w):
                for k in range(8):
                    p0 = pr[k % 4]
                    eng, cnt = free[k % 4]
                    T.wait(trk[eng], cnt)
                    nt = T.op(lambda t, k=k, s=s, p0=p0: t.transpose(
                        p0[:].bitcast(BF16)[:, 0:128],
                        gbf[:, s * 1024 + k * 128: s * 1024 + k * 128
                            + 128],
                        ident[:]))
                    n_gtrans = nt
                    Sc.wait(T, nt)
                    nse = Sc.op(lambda s_, k=k, s=s, p0=p0: s_.copy(
                        ln4_fm[:, k * 512 + s * 128: k * 512 + s * 128
                               + 128],
                        p0[:].bitcast(BF16)[:, 0:128]))
                    free[k % 4] = ("s", nse)
            n_lnfm = nse
            if gi + 1 < len(tiles):
                if tiles[gi + 1][1] == 0:
                    emit_grec(e + 1)
                emit_gathers(gi + 1)

            for fc in range(4):
                slot = fc % 2
                GW1[slot].wait(T, wlast_w1[slot])
                nwld = GW1[slot].op(lambda g, e=e, fc=fc, slot=slot:
                                    g.dma_start(
                    out=w_sb[:, slot * 4096:(slot + 1) * 4096].rearrange(
                        "p (k f) -> p k f", k=8),
                    in_=scr_w1[e].rearrange("(k p) f -> p k f", p=128)[
                        :, :, fc * 512:(fc + 1) * 512]))
                T.wait(GW1[slot], nwld)
                for fb in range(4):
                    f = fc * 4 + fb
                    pp = pr[f % 4]
                    eng, cnt = free[f % 4]
                    T.wait(trk[eng], cnt)
                    T.wait(Sc, n_lnfm)
                    for k in range(8):
                        nt = T.op(lambda t, k=k, fb=fb, slot=slot, pp=pp,
                                  wt=wt: t.matmul(
                            pp[:, 0:wt],
                            w_sb[:, slot * 4096 + k * 512 + fb * 128:
                                 slot * 4096 + k * 512 + fb * 128 + 128],
                            ln4_fm[:, k * 512: k * 512 + wt],
                            start=(k == 0), stop=(k == 7)))
                    Sc.wait(T, nt)
                    nse = Sc.op(lambda s, f=f, e=e, pp=pp, wt=wt:
                                s.activation(
                        h_fm[:, f * 512: f * 512 + wt], pp[:, 0:wt],
                        AF.Relu,
                        bias=b1c[:, e * 16 + f: e * 16 + f + 1]))
                    free[f % 4] = ("s", nse)
                wlast_w1[slot] = nt
            n_h = nse

            T.wait(Sc, n_h)
            for b_ in range(8):
                eng, cnt = free[b_]
                T.wait(trk[eng], cnt)
            for dg in range(4):
                slot = dg % 2
                GW2[slot].wait(T, wlast_w2[slot])
                nwld = GW2[slot].op(lambda g, e=e, dg=dg, slot=slot:
                                    g.dma_start(
                    out=w2_sb[:, slot * 4096:(slot + 1) * 4096].rearrange(
                        "p (g o) -> p g o", g=4),
                    in_=scr_w2[e][dg * 512:(dg + 1) * 512, :].rearrange(
                        "(g p) o -> p g o", p=128)))
                T.wait(GW2[slot], nwld)
                for tc in range(ntc_w):
                    for fb in range(4):
                        f = dg * 4 + fb
                        for half in range(2):
                            nt = T.op(lambda t, f=f, tc=tc, fb=fb,
                                      half=half, slot=slot: t.matmul(
                                pr[tc * 2 + half][:, 0:512],
                                h_fm[:, f * 512 + tc * 128:
                                     f * 512 + tc * 128 + 128],
                                w2_sb[:, slot * 4096 + fb * 1024
                                      + half * 512: slot * 4096
                                      + fb * 1024 + half * 512 + 512],
                                start=(dg == 0 and fb == 0), stop=False))
                wlast_w2[slot] = nt
            for tc in range(ntc_w):
                for half in range(2):
                    nt = T.op(lambda t, tc=tc, half=half: t.matmul(
                        pr[tc * 2 + half][:, 0:512], ones[:],
                        rowb[0:1, half * 512: half * 512 + 512],
                        start=False, stop=True))
            n_ymm = nt
            n_ymm_prev = nt
            for tc in range(ntc_w):
                V.wait(T, n_ymm)
                ysl = y_out[:, tc * 1024: tc * 1024 + 1024]
                V.wait(GL[1], n_ysc_slot[tc])
                for half in range(2):
                    nv = V.op(lambda v, tc=tc, half=half, e=e, ysl=ysl,
                              toff=toff, p_=p_: v.tensor_scalar_mul(
                        ysl[:, half * 512: half * 512 + 512],
                        pr[tc * 2 + half][:, 0:512],
                        grec_q[p_][:, (toff // 128 + tc) * 16 + e:
                                   (toff // 128 + tc) * 16 + e + 1]))
                    free[tc * 2 + half] = ("v", nv)
                n_grec_read_exp[e] = nv
                GL[1].wait(V, nv)
                GL[1].wait(Q, ctx.nacc0d)
                GL[1].wait(GL[1], n_ysc_prev)
                n_ysc = GL[1].op(lambda g, tc=tc, e=e, ysl=ysl, toff=toff,
                                 p_=p_: g.indirect_dma_start(
                    out=x4_acc,
                    out_offset=bass.IndirectOffsetOnAxis(
                        ap=gtok_q[p_][:, toff // 128 + tc:
                                      toff // 128 + tc + 1], axis=0),
                    in_=ysl,
                    in_offset=None,
                    compute_op=ALU.add))
                n_ysc_prev = n_ysc
                n_ysc_slot[tc] = n_ysc
            if ti == NTILES - 1:
                n_ysc_exp_end[e] = n_ysc_prev
    GL[1].wait(GL[1], n_ysc_prev)
    Q.wait(GL[1], n_ysc_prev)
    ctx.barrier()
    nfin = Q.op(lambda q: q.dma_start(out=io["x4"],
                                      in_=x4_acc[0:TTOK, :]))
    Q.wait(Q, nfin)


def build_phase_d_sparse(n_tt=8):
    nc = bass.Bass("TRN2", target_bir_lowering=False)
    Sx = n_tt * 512
    E, D = 8, 2048
    CAP = ((int(Sx * 2 / 8 * 3.0) + 127) // 128) * 128
    NSC = CAP // 128
    io = {
        "x3": nc.dram_tensor("x3", [Sx, H], F32, kind="ExternalInput").ap(),
        "W1": nc.dram_tensor("W1", [E, H, D], F32, kind="ExternalInput").ap(),
        "W2": nc.dram_tensor("W2", [E, D, H], F32, kind="ExternalInput").ap(),
        "Wr2": nc.dram_tensor("Wr2", [H, E], F32, kind="ExternalInput").ap(),
        "b1c": nc.dram_tensor("b1c", [128, 128], F32,
                              kind="ExternalInput").ap(),
        "rows2": nc.dram_tensor("rows2", [E, 1024], F32,
                                kind="ExternalInput").ap(),
        "br2": nc.dram_tensor("br2", [1, 8], F32, kind="ExternalInput").ap(),
        "ident": nc.dram_tensor("ident", [128, 128], BF16,
                                kind="ExternalInput").ap(),
        "idf32": nc.dram_tensor("idf32", [128, 128], F32,
                                kind="ExternalInput").ap(),
        "ones1": nc.dram_tensor("ones1", [1, 128], BF16,
                                kind="ExternalInput").ap(),
        "iota_c": nc.dram_tensor("iota_c", [128, n_tt * 4], F32,
                                 kind="ExternalInput").ap(),
        "ecolv": nc.dram_tensor("ecolv", [8, 1], F32,
                                kind="ExternalInput").ap(),
        "zrecv": nc.dram_tensor("zrecv", [128, 16], F32,
                                kind="ExternalInput").ap(),
        "x4": nc.dram_tensor("x4", [Sx, H], F32, kind="ExternalOutput").ap(),
        "scr_w1": nc.dram_tensor("scr_w1", [E, H, D], BF16,
                                 kind="Internal").ap(),
        "scr_w2": nc.dram_tensor("scr_w2", [E, D, H], BF16,
                                 kind="Internal").ap(),
        "scr_lnd": nc.dram_tensor("scr_lnd", [Sx + 128, H], BF16,
                                  kind="Internal").ap(),
        "scr_recd": nc.dram_tensor("scr_recd", [E * CAP + 128, 16], F32,
                                   kind="Internal").ap(),
        "scr_accx2": nc.dram_tensor("scr_accx2", [Sx + 128, H], F32,
                                    kind="Internal").ap(),
    }
    es = ExitStack()
    with es:
        sb = lambda name, w, dt=F32, p=128: es.enter_context(
            nc.sbuf_tensor(name, [p, w], dt))
        buf = {
            "x3t": sb("x3t", 4096),
            "lnbf": sb("lnbf", 1024, BF16),
            "ln4_fm": sb("ln4_fm", 4096, BF16),
            "w_sb": sb("w_sb", 8192, BF16),
            "w2_sb": sb("w2_sb", 8192, BF16),
            "h_fm": sb("h_fm", 8192, BF16),
            "ctile": sb("ctile", 64),
            "cols": sb("colsd", 16),
            "sqscr": sb("sqscrd", 1024),
            "sq2": sb("sq2d", 1024),
            "b1c": sb("b1csb", 128),
            "rowb": sb("rowbd", 1024, BF16, 1),
            "brf": sb("brfd", 8, F32, 1),
            "brb": sb("brbd", 8, BF16, 1),
            "wr_bf": sb("wr_bfd", 64, BF16),
            "ident": sb("identsb", 128, BF16),
            "idf32": sb("idf32sb", 128),
            "ones": sb("onessb", 128, BF16, 1),
            "rec_sb": sb("rec_sbd", n_tt * 64),
            "islots": sb("islotsd", n_tt * 32, mybir.dt.int32),
            "iota_c": sb("iota_csbd", n_tt * 4),
            "onecol": sb("onecold", 1, F32, 8),
            "ecol": sb("ecolsbd", 1, F32, 8),
            "grec": sb("grecd", NSC * 32),
            "gtok": sb("gtokd", NSC * 2, mybir.dt.int32),
            "zrec": sb("zrecd", 16),
            "y_out": sb("y_outd", 4096),
            "gln": sb("glnd", 4096, BF16),
            "pr": [es.enter_context(nc.psum_tensor(f"pr{i}", [128, 512], F32))
                   for i in range(8)],
        }
        ctx = Ctx(nc, es)
        ctx.GW1 = (ctx.queue("gd10", engine="q"), ctx.queue("gd11", engine="q"))
        ctx.GW2 = (ctx.queue("gd20", engine="q"), ctx.queue("gd21", engine="q"))
        ctx.GL = (ctx.queue("gl0"), ctx.queue("gl1"))
        blk = es.enter_context(nc.Block())
        ctx.run_passes(blk, lambda c: emit_phase_d_sparse(c, io, buf, n_tt))
    return nc


def host_prep_d_sparse(inputs, n_tt=8):
    d = host_prep_d(inputs)
    Sx = n_tt * 512
    CAP = ((int(Sx * 2 / 8 * 3.0) + 127) // 128) * 128
    d["idf32"] = np.eye(128, dtype=np.float32)
    d["iota_c"] = np.ascontiguousarray(
        (np.arange(n_tt * 4)[None, :] * 128
         + np.arange(128)[:, None]).astype(np.float32))
    d["ecolv"] = (np.arange(8, dtype=np.float32) * CAP).reshape(8, 1)
    z = np.zeros((128, 16), np.float32)
    z[:, 8] = Sx + np.arange(128)
    d["zrecv"] = z
    return d


def _build_masks():
    kp = np.arange(128)[:, None]
    qp = np.arange(128)[None, :]
    return np.where(kp <= qp, 0.0, 1e30).astype(np.float32)


def host_prep_a(inputs):
    wq = np.asarray(inputs["L0_attn_Wq"], np.float32)
    wk = np.asarray(inputs["L0_attn_Wk"], np.float32)
    wv = np.asarray(inputs["L0_attn_Wv"], np.float32)
    wo = np.asarray(inputs["L0_attn_Wo"], np.float32)
    wq_t = np.empty((H, 1024), np.float32)
    wo_t = np.empty((1024, H), np.float32)
    for jslot, h in enumerate(HEAD_AT_SLOT):
        wq_t[:, jslot * 64:(jslot + 1) * 64] = wq[h * 64:(h + 1) * 64, :].T
        wo_t[jslot * 64:(jslot + 1) * 64, :] = wo[:, h * 64:(h + 1) * 64].T
    return {
        "wq_t": np.ascontiguousarray(wq_t),
        "wk_t": np.ascontiguousarray(wk.T),
        "wv_t": np.ascontiguousarray(wv.T),
        "wo_t": np.ascontiguousarray(wo_t),
        "ident": np.eye(128, dtype=ml_dtypes.bfloat16),
        "masks": _build_masks(),
        "ones_c": np.ones((128, 1), dtype=ml_dtypes.bfloat16),
        "negones": np.full((1, 128), -1.0, dtype=ml_dtypes.bfloat16),
    }




def build_fused():
    nc = bass.Bass("TRN2", target_bir_lowering=False)
    E, F, D = 8, 4096, 2048
    dram = lambda name, shape, dt=F32, kind="ExternalInput": nc.dram_tensor(
        name, shape, dt, kind=kind).ap()
    ident_ap = dram("ident", [128, 128], BF16)
    ones_ap = dram("ones1", [1, 128], BF16)
    x1s = dram("x1s", [S, H], F32, "Internal")
    x3s = dram("x3s", [S, H], F32, "Internal")
    CAP = ((int(S * 2 / 8 * 3.0) + 127) // 128) * 128
    idf32_ap = dram("idf32", [128, 128])
    iota_ap = dram("iota_c", [128, 32])
    ecol_ap = dram("ecolv", [8, 1])
    zrec_ap = dram("zrecv", [128, 16])
    accx_ap = dram("scr_accx", [S + 128, H], F32, "Internal")
    ioA = {
        "x": dram("x", [S, H]),
        "wq_t": dram("wq_t", [H, 1024]), "wk_t": dram("wk_t", [H, 256]),
        "wv_t": dram("wv_t", [H, 256]), "wo_t": dram("wo_t", [H, 1024]),
        "ident": ident_ap, "masks": dram("masks", [128, 128]),
        "ones_c": dram("ones_c", [128, 1], BF16),
        "negones": dram("negones", [1, 128], BF16),
        "x1": x1s,
    }
    ioB = {
        "x1": x1s,
        "Wi": dram("Wi", [E, H, F]), "Wg": dram("Wg", [E, H, F]),
        "Wo": dram("Wo", [E, F, H]), "Wr": dram("Wr", [H, E]),
        "bic": dram("bic", [128, 256]), "bgc": dram("bgc", [128, 256]),
        "rows": dram("rows", [3, E, 1024]), "br": dram("br", [1, 8]),
        "ident": ident_ap, "ones1": ones_ap,
        "scr_wi": dram("scr_wi", [E, H, F], BF16, "Internal"),
        "scr_wg": dram("scr_wg", [E, H, F], BF16, "Internal"),
        "scr_wo": dram("scr_wo", [E, F, H], BF16, "Internal"),
        "idf32": idf32_ap, "iota_c": iota_ap, "ecolv": ecol_ap,
        "zrecv": zrec_ap, "dumpc": dram("dumpc", [128, 1]),
        "scr_ln": dram("scr_ln", [S + 128, H], BF16, "Internal"),
        "scr_lnf": dram("scr_lnf", [S + 128, H], F32, "Internal"),
        "scr_rec": dram("scr_rec", [E * CAP + 128, 16], F32, "Internal"),
        "scr_accx": accx_ap,
    }
    ioC = {
        "x2": accx_ap, "x3": x3s,
        "W_in": dram("W_in", [H, 4096]), "W_out": dram("W_out", [2048, H]),
        "A_logT": dram("A_logT", [2048, 16]),
        "D_param": dram("D_param", [128, 16]),
        "conv_w": dram("conv_w", [128, 64]), "conv_b": dram("conv_b", [128, 16]),
        "ident": ident_ap,
        "scr_win": dram("scr_win", [H, 4096], BF16, "Internal"),
    }
    ioD = {
        "x3": x3s,
        "x4": dram("x4", [S, H], F32, "ExternalOutput"),
        "W1": dram("W1", [E, H, D]), "W2": dram("W2", [E, D, H]),
        "Wr2": dram("Wr2", [H, E]), "b1c": dram("b1c", [128, 128]),
        "rows2": dram("rows2", [E, 1024]), "br2": dram("br2", [1, 8]),
        "ident": ident_ap, "ones1": ones_ap,
        "idf32": idf32_ap, "iota_c": iota_ap, "ecolv": ecol_ap,
        "zrecv": zrec_ap,
        "scr_lnd": dram("scr_lnd", [S + 128, H], BF16, "Internal"),
        "scr_recd": dram("scr_recd", [E * CAP + 128, 16], F32, "Internal"),
        "scr_accx2": dram("scr_accx2", [S + 128, H], F32, "Internal"),
    }

    es = ExitStack()
    with es:
        ctx = Ctx(nc, es)
        ctx.GW = (ctx.queue("gw0", engine="q"), ctx.queue("gw1", engine="q"))
        ctx.GWI = (ctx.queue("gwi0", engine="q"), ctx.queue("gwi1", engine="q"),
                   ctx.queue("gwi2", engine="q"))
        ctx.GWG = (ctx.queue("gwg0", engine="q"), ctx.queue("gwg1", engine="q"),
                   ctx.queue("gwg2", engine="q"))
        ctx.GWO = (ctx.queue("gwo0", engine="q"), ctx.queue("gwo1", engine="q"))
        ctx.GW1 = (ctx.queue("gd10", engine="q"), ctx.queue("gd11", engine="q"))
        ctx.GW2 = (ctx.queue("gd20", engine="q"), ctx.queue("gd21", engine="q"))
        ctx.GL = (ctx.queue("gl0"), ctx.queue("gl1"))

        def mk_bufs(pfx, spec, esx):
            sb = lambda name, w, dt=F32, p=128: esx.enter_context(
                nc.sbuf_tensor(pfx + name, [p, w], dt))
            ps = lambda name, w, dt=F32: esx.enter_context(
                nc.psum_tensor(pfx + name, [128, w], dt))
            out = {}
            for key, kind, args in spec:
                if kind == "sb":
                    out[key] = sb(key, *args)
                elif kind == "ps":
                    out[key] = ps(key, *args)
                elif kind == "sblist":
                    out[key] = [sb(key + str(i), *args) for i in range(2)]
                elif kind == "pslist":
                    out[key] = [ps(key + str(i), *args) for i in range(args[0] and 8)]
            return out

        esA = ExitStack()
        bufA = {}
        sbA = lambda name, w, dt=F32, p=128: esA.enter_context(
            nc.sbuf_tensor("a_" + name, [p, w], dt))
        psA_ = lambda name, w, dt=F32: esA.enter_context(
            nc.psum_tensor("a_" + name, [128, w], dt))
        bufA = {
            "xin": sbA("xin", 2048), "x1t": sbA("x1t", 1024),
            "ccen": sbA("ccen", 1024), "sqscr": sbA("sqscr", 1024),
            "cols": sbA("cols", 24),
            "ln_bf": sbA("ln_bf", 1024, BF16), "ln_fm": sbA("ln_fm", 1024, BF16),
            "kfm": [sbA("kfm0", NT * 128, BF16), sbA("kfm1", NT * 128, BF16)],
            "vtl": sbA("vtl", NT * 260, BF16), "qfm": sbA("qfm", 1024, BF16),
            "ssb": sbA("ssb", 2 * NT * 128), "psb": sbA("psb", 2 * NT * 128, BF16),
            "mbc_sb": sbA("mbc_sb", 640),
            "ones_col": sbA("ones_colsb", 1, BF16),
            "negones": sbA("negonessb", 128, BF16, 1),
            "ksq": sbA("ksq", 256, BF16), "qsq": sbA("qsq", 1024, BF16),
            "qn2_sb": sbA("qn2_sb", 2048, F32, 1),
            "lq_sb": sbA("lq_sb", 2048, F32, 1),
            "crow_sb": sbA("crow_sb", 2048, BF16, 1),
            "rsc": sbA("rsc", 16, F32, 1),
            "ones_r": sbA("ones_r", 128, BF16, 1), "att": sbA("att", 1024),
            "attbf": sbA("attbf", 1024, BF16), "attfm": sbA("attfm", 1024, BF16),
            "wq_bf": sbA("wq_bf", 8192, BF16), "wk_bf": sbA("wk_bf", 2048, BF16),
            "wv_bf": sbA("wv_bf", 2048, BF16), "wo_bf": sbA("wo_bf", 8192, BF16),
            "ident": sbA("identsb", 128, BF16), "masks": sbA("maskssb", 128),
            "psA": psA_("psA", 512), "psB": psA_("psB", 512),
            "ptA": psA_("ptA", 512), "ptB": psA_("ptB", 512),
            "pv": psA_("pv", 512), "pk": psA_("pk", 512),
            "pq1": psA_("pq1", 512), "pq2": psA_("pq2", 512),
        }

        def emit_all(c):
            T, V, Sc, Q, G = c.T, c.V, c.S, c.Q, c.G
            for name, dst in (("wq_t", bufA["wq_bf"]), ("wk_t", bufA["wk_bf"]),
                              ("wv_t", bufA["wv_bf"]), ("wo_t", bufA["wo_bf"])):
                c.nW_a = G.op(lambda g, name=name, dst=dst: g.dma_start(
                    out=dst[:].rearrange("p (k o) -> p k o", k=8),
                    in_=ioA[name].rearrange("(k p) o -> p k o", p=128)))
            for e in range(E):
                G.op(lambda g, e=e: g.dma_start(out=ioB["scr_wi"][e],
                                                in_=ioB["Wi"][e]))
                G.op(lambda g, e=e: g.dma_start(out=ioB["scr_wg"][e],
                                                in_=ioB["Wg"][e]))
                G.op(lambda g, e=e: g.dma_start(out=ioB["scr_wo"][e],
                                                in_=ioB["Wo"][e]))
            G.op(lambda g: g.dma_start(out=ioC["scr_win"], in_=ioC["W_in"]))
            for e in range(E):
                G.op(lambda g, e=e: g.dma_start(out=ioD["scr_w1"][e],
                                                in_=ioD["W1"][e]))
                G.op(lambda g, e=e: g.dma_start(out=ioD["scr_w2"][e],
                                                in_=ioD["W2"][e]))
            emit_phase_a(c, ioA, bufA, NT, fused=True)
            c.barrier()
            emit_phase_b_sparse(c, ioB, bufB, 8, fused=True)
            c.barrier()
            emit_phase_c(c, ioC, bufC, 8, fused=True)
            c.barrier()
            emit_phase_d_sparse(c, ioD, bufD, 8, fused=True)

        esA.close()
        esB = ExitStack()
        sbB = lambda name, w, dt=F32, p=128: esB.enter_context(
            nc.sbuf_tensor("b_" + name, [p, w], dt))
        NSC = CAP // 128
        bufB = {
            "x1t": sbB("x1t", 4096), "ln2_sb": sbB("ln2_sb", 4096),
            "ln2_fm": sbB("ln2_fm", 4096, BF16), "lnbf": sbB("lnbf", 1024, BF16),
            "h_fm": sbB("h_fm", 16384, BF16),
            "sgt": sbB("sgt", 1024, BF16), "wi_sb": sbB("wi_sb", 12288, BF16),
            "wg_sb": sbB("wg_sb", 12288, BF16),
            "wo_sb": sbB("wo_sb", 8192, BF16),
            "acc": sbB("acc", 16), "r_sb": sbB("r_sb", 2048),
            "gb_sb": sbB("gb_sb", 1024), "bb_sb": sbB("bb_sb", 1024),
            "ctile": sbB("ctile", 64), "cols": sbB("cols", 16),
            "sqscr": sbB("sqscr", 1024), "bic": sbB("bicsb", 256),
            "bgc": sbB("bgcsb", 256),
            "rowb": sbB("rowb", 3072, BF16, 1), "brf": sbB("brf", 8, F32, 1),
            "brb": sbB("brb", 8, BF16, 1), "wr_bf": sbB("wr_bfsb", 64, BF16),
            "ident": sbB("identsb", 128, BF16), "ones": sbB("onessb", 128, BF16, 1),
            "idf32": sbB("idf32sb", 128),
            "rec_sb": sbB("rec_sb", 512),
            "islots": sbB("islots", 256, mybir.dt.int32),
            "iota_c": sbB("iota_csb", 32),
            "onecol": sbB("onecol", 1, F32, 8),
            "ecol": sbB("ecolsb", 1, F32, 8),
            "grec": sbB("grec", NSC * 32),
            "gtok": sbB("gtok", NSC * 2, mybir.dt.int32),
            "zrec": sbB("zrec", 16),
            "y_out": sbB("y_out", 4096),
            "gln": sbB("gln", 4096, BF16),
            "dumpc": sbB("dumpcsb", 1),
            "dmsk": sbB("dmsk", 8),
            "dslf": sbB("dslf", 8),
            "pr": [esB.enter_context(nc.psum_tensor(f"b_pr{i}", [128, 512], F32))
                   for i in range(8)],
        }
        esB.close()
        esC = ExitStack()
        sbC = lambda name, w, dt=F32, p=128: esC.enter_context(
            nc.sbuf_tensor("c_" + name, [p, w], dt))
        bufC = {
            "x2t": sbC("x2t", 4096), "lnbf": sbC("lnbf", 1024, BF16),
            "ln3_fm": sbC("ln3_fm", 4096, BF16), "win_sb": sbC("win_sb", 8192, BF16),
            "wout_sb": sbC("wout_sb", 16384, BF16), "u_sb": sbC("u_sb", 8192),
            "zs_bf": sbC("zs_bf", 8192, BF16), "yg_bf": sbC("yg_bf", 8192, BF16),
            "y_acc": sbC("y_acc", 8192), "upad": sbC("upad", 515),
            "sigma": sbC("sigma", 2048), "state": sbC("state", 256),
            "utail": sbC("utail", 64), "a_sb": sbC("a_sb", 256),
            "dcol": sbC("dcol", 16), "wcv": sbC("wcv", 64),
            "cbc": sbC("cbc", 16), "cols": sbC("colsc", 16),
            "sqscr": sbC("sqscrc", 1024), "sq2": sbC("sq2", 1024),
            "x3t": sbC("x3t", 4096), "ident": sbC("identsb", 128, BF16),
            "pr": [esC.enter_context(nc.psum_tensor(f"c_pr{i}", [128, 512], F32))
                   for i in range(8)],
        }
        esC.close()
        esD = ExitStack()
        sbD = lambda name, w, dt=F32, p=128: esD.enter_context(
            nc.sbuf_tensor("d_" + name, [p, w], dt))
        bufD = {
            "x3t": sbD("x3t", 4096), "lnbf": sbD("lnbf", 1024, BF16),
            "ln4_fm": sbD("ln4_fm", 4096, BF16), "w_sb": sbD("w_sb", 8192, BF16),
            "w2_sb": sbD("w2_sb", 8192, BF16),
            "h_fm": sbD("h_fm", 8192, BF16),
            "ctile": sbD("ctile", 64), "cols": sbD("colsd", 16),
            "sqscr": sbD("sqscrd", 1024), "sq2": sbD("sq2d", 1024),
            "b1c": sbD("b1csb", 128),
            "rowb": sbD("rowbd", 1024, BF16, 1), "brf": sbD("brfd", 8, F32, 1),
            "brb": sbD("brbd", 8, BF16, 1), "wr_bf": sbD("wr_bfd", 64, BF16),
            "ident": sbD("identsb", 128, BF16), "ones": sbD("onessb", 128, BF16, 1),
            "idf32": sbD("idf32sb", 128),
            "rec_sb": sbD("rec_sbd", 512),
            "islots": sbD("islotsd", 256, mybir.dt.int32),
            "iota_c": sbD("iota_csbd", 32),
            "onecol": sbD("onecold", 1, F32, 8),
            "ecol": sbD("ecolsbd", 1, F32, 8),
            "grec": sbD("grecd", NSC * 32),
            "gtok": sbD("gtokd", NSC * 2, mybir.dt.int32),
            "zrec": sbD("zrecd", 16),
            "y_out": sbD("y_outd", 4096),
            "gln": sbD("glnd", 4096, BF16),
            "pr": [esD.enter_context(nc.psum_tensor(f"d_pr{i}", [128, 512], F32))
                   for i in range(8)],
        }
        esD.close()
        ioD["scr_w1"] = dram("scr_w1", [E, H, D], BF16, "Internal")
        ioD["scr_w2"] = dram("scr_w2", [E, D, H], BF16, "Internal")

        blk = es.enter_context(nc.Block())
        ctx.run_passes(blk, emit_all)
    return nc


def kernel(**inputs):
    x = np.ascontiguousarray(np.asarray(inputs["L0_attn_hidden_states"],
                                        np.float32))
    cores = list(range(NCORES))
    shared = {}
    shared.update(host_prep_a(inputs))
    shared.update(host_prep_b_sparse(inputs, 8))
    shared.update(host_prep_c(inputs))
    shared.update(host_prep_d_sparse(inputs, 8))
    nc = build_fused()
    in_maps = [dict(shared, x=np.ascontiguousarray(x[c])) for c in cores]
    res = run_bass_kernel_spmd(nc, in_maps, core_ids=cores)
    return np.stack([r["x4"] for r in res.results])


def kernel_unfused(**inputs):

    x = np.ascontiguousarray(np.asarray(inputs["L0_attn_hidden_states"],
                                        np.float32))
    cores = list(range(NCORES))

    shared_a = host_prep_a(inputs)
    nc_a = build_phase_a()
    res = run_bass_kernel_spmd(
        nc_a, [dict(shared_a, x=np.ascontiguousarray(x[c])) for c in cores],
        core_ids=cores)
    xs = [r["x1"] for r in res.results]

    shared_b = host_prep_b(inputs)
    nc_b = build_phase_b()
    res = run_bass_kernel_spmd(
        nc_b, [dict(shared_b, x1=xs[c]) for c in cores], core_ids=cores)
    xs = [r["x2"] for r in res.results]

    shared_c = host_prep_c(inputs)
    nc_c = build_phase_c()
    res = run_bass_kernel_spmd(
        nc_c, [dict(shared_c, x2=xs[c]) for c in cores], core_ids=cores)
    xs = [r["x3"] for r in res.results]

    shared_d = host_prep_d(inputs)
    nc_d = build_phase_d()
    res = run_bass_kernel_spmd(
        nc_d, [dict(shared_d, x3=xs[c]) for c in cores], core_ids=cores)
    return np.stack([r["x4"] for r in res.results])


# version 16
# speedup vs baseline: 1.0521x; 1.0105x over previous
import math
from contextlib import ExitStack

import numpy as np
import ml_dtypes
import concourse.bass as bass
import concourse.mybir as mybir
from concourse.bass_utils import run_bass_kernel_spmd

F32 = mybir.dt.float32
BF16 = mybir.dt.bfloat16
AF = mybir.ActivationFunctionType
ALU = mybir.AluOpType
AX = mybir.AxisListType

NCORES = 8
S = 4096
H = 1024
NT = S // 128

HEAD_AT_SLOT = [0, 4, 1, 5, 2, 6, 3, 7, 8, 12, 9, 13, 10, 14, 11, 15]

NEG = -1e30


class Track:

    def __init__(self, ctx, key, sem, step=1, name=None):
        self.ctx = ctx
        self.key = key
        self.name = name or key
        self.sem = sem
        self.step = step
        self.n = 0
        self.last_wait = {}

    def op(self, fn):
        self.n += self.step
        if self.ctx.cur == self.key:
            fn(self.ctx.raw).then_inc(self.sem, self.step)
        return self.n

    def wait(self, other, val):
        if val <= 0 or other is self and val <= self.n - self.step:
            pass
        if val <= 0:
            return
        lw = self.last_wait.get(other.name, -1)
        if val <= lw:
            return
        self.last_wait[other.name] = val
        if self.ctx.cur == self.key:
            self.ctx.raw.wait_ge(other.sem, val)


class Ctx:
    def __init__(self, nc, es):
        self.nc = nc
        self.es = es
        self.cur = None
        self.raw = None
        sem = lambda name: es.enter_context(nc.semaphore(name))
        self.T = Track(self, "t", sem("semT"))
        self.V = Track(self, "v", sem("semV"))
        self.S = Track(self, "s", sem("semS"))
        self.Q = Track(self, "q", sem("semQ"), step=16)
        self.G = Track(self, "g", sem("semG"), step=16)
        self.P = Track(self, "g", sem("semP"), name="p")
        self.tracks = [self.T, self.V, self.S, self.Q, self.G, self.P]
        self._mksem = sem

    def queue(self, name, engine="g"):
        t = Track(self, engine, self._mksem("sem_" + name), step=16, name=name)
        self.tracks.append(t)
        return t

    def barrier(self):
        ends = [(t, t.n) for t in self.tracks]
        for t in self.tracks:
            for o, n in ends:
                t.wait(o, n)

    def run_passes(self, blk, emit):
        saved = [(t.n, dict(t.last_wait)) for t in self.tracks]

        def mk(key):
            def body(raw):
                self.cur = key
                self.raw = raw
                for t, (n0, lw0) in zip(self.tracks, saved):
                    t.n = n0
                    t.last_wait = dict(lw0)
                emit(self)
            return body

        blk.tensor(mk("t"))
        blk.vector(mk("v"))
        blk.scalar(mk("s"))
        blk.sync(mk("q"))
        blk.gpsimd(mk("g"))
        self.cur = None
        self.raw = None


def ts(i, w=128):
    return slice(i * w, (i + 1) * w)



def emit_phase_a(ctx, io, buf, n_tiles=NT, fused=False):
    T, V, Sc, Q, G = ctx.T, ctx.V, ctx.S, ctx.Q, ctx.G
    trk = {"v": V, "s": Sc, "t": T}

    x_in, x1_out = io["x"], io["x1"]

    xin, x1t, ccen, sqscr, cols = (buf[k] for k in
                                   ("xin", "x1t", "ccen", "sqscr", "cols"))
    ln_bf, ln_fm, kfm, vtl, qfm = (buf[k] for k in
                                   ("ln_bf", "ln_fm", "kfm", "vtl", "qfm"))
    stsb = buf["ssb"]
    ptsb = buf["psb"]
    att, attbf, attfm = buf["att"], buf["attbf"], buf["attfm"]
    wq_bf, wk_bf, wv_bf, wo_bf, ident, maskbig = (
        buf[k] for k in ("wq_bf", "wk_bf", "wv_bf", "wo_bf", "ident",
                         "masks"))
    mbc_sb = buf["mbc_sb"]
    ones_col, negones = buf["ones_col"], buf["negones"]
    ksq, qsq = buf["ksq"], buf["qsq"]
    qn2_sb, lq_sb, crow_sb = buf["qn2_sb"], buf["lq_sb"], buf["crow_sb"]
    rsc = buf["rsc"]
    b0, b1, b2, b3, bpv, bmb, bt6, bt7 = (
        buf[k] for k in ("psA", "psB", "ptA", "ptB", "pv", "pk",
                         "pq1", "pq2"))

    VP = 260
    HSW = n_tiles * 128

    n_consts = Q.op(lambda q: q.dma_start(out=ident[:], in_=io["ident"]))
    n_consts = Q.op(lambda q: q.dma_start(out=maskbig[:], in_=io["masks"]))
    n_consts = Q.op(lambda q: q.dma_start(out=ones_col[:], in_=io["ones_c"]))
    n_consts = Q.op(lambda q: q.dma_start(out=negones[:], in_=io["negones"]))
    if fused:
        nW = ctx.nW_a
    else:
        for name, dst, kchunks in (("wq_t", wq_bf, 8), ("wk_t", wk_bf, 8),
                                   ("wv_t", wv_bf, 8), ("wo_t", wo_bf, 8)):
            nW = G.op(lambda g, name=name, dst=dst, kchunks=kchunks:
                      g.dma_start(
                out=dst[:].rearrange("p (k o) -> p k o", k=kchunks),
                in_=io[name].rearrange("(k p) o -> p k o", p=128)))
    V.wait(Q, n_consts)
    T.wait(Q, n_consts)
    T.wait(G, nW)
    n_eps = V.op(lambda v: v.memset(cols[:, 7:8], 1e-5))
    n_vset = V.op(lambda v: v.memset(vtl[:], 1.0))
    n_kn0 = V.op(lambda v: v.memset(rsc[:], 0.0))
    Sc.wait(V, n_eps)

    free = {i: ("v", 0) for i in range(8)}

    for i in range(n_tiles):
        ctx.barrier()
        slot = i % 2
        nload = Q.op(lambda q, i=i, slot=slot: q.dma_start(
            out=xin[:, ts(slot, 1024)], in_=x_in[ts(i), :]))

        V.wait(Q, nload)
        n1 = V.op(lambda v, slot=slot: v.reduce_sum(
            cols[:, 0:1], xin[:, ts(slot, 1024)], axis=AX.X, negate=True))
        V.wait(V, n1)
        n2 = V.op(lambda v: v.tensor_scalar_mul(cols[:, 0:1], cols[:, 0:1],
                                                1.0 / H))
        V.wait(V, n2)
        n3 = V.op(lambda v, slot=slot: v.tensor_scalar_add(
            ccen[:], xin[:, ts(slot, 1024)], cols[:, 0:1]))
        Sc.wait(V, n3)
        n4 = Sc.op(lambda s: s.activation(sqscr[:], ccen[:], AF.Square,
                                          accum_out=cols[:, 1:2]))
        V.wait(Sc, n4)
        n5 = V.op(lambda v: v.tensor_scalar_mul(cols[:, 2:3], cols[:, 1:2],
                                                1.0 / H))
        Sc.wait(V, n5)
        n6 = Sc.op(lambda s: s.activation(cols[:, 3:4], cols[:, 2:3], AF.Ln,
                                          bias=cols[:, 7:8]))
        Sc.wait(Sc, n6)
        n7 = Sc.op(lambda s: s.activation(cols[:, 2:3], cols[:, 3:4], AF.Exp,
                                          scale=-0.5))
        V.wait(Sc, n7)
        n8 = V.op(lambda v: v.tensor_scalar_mul(ln_bf[:], ccen[:],
                                                cols[:, 2:3]))

        T.wait(V, n8)
        for k in range(8):
            pt = (bt6, bt7)[k % 2]
            eng, cnt = free[6 + k % 2]
            T.wait(trk[eng], cnt)
            nt = T.op(lambda t, k=k, pt=pt: t.transpose(
                pt[:].bitcast(BF16)[:, 0:128], ln_bf[:, ts(k)], ident[:]))
            Sc.wait(T, nt)
            nsev = Sc.op(lambda s, k=k, pt=pt: s.copy(
                ln_fm[:, ts(k)], pt[:].bitcast(BF16)[:, 0:128]))
            free[6 + k % 2] = ("s", nsev)
        n_lnfm = nsev

        T.wait(Sc, n_lnfm)
        for pair in range(2):
            eng, cnt = free[0]
            T.wait(trk[eng], cnt)
            for k in range(8):
                nt = T.op(lambda t, k=k, pair=pair: t.matmul(
                    b0[:, 0:128],
                    wk_bf[:, k * 256 + pair * 128: k * 256 + pair * 128
                          + 128],
                    ln_fm[:, ts(k)], start=(k == 0), stop=(k == 7)))
            V.wait(T, nt)
            nv = V.op(lambda v, pair=pair, i=i: v.tensor_copy(
                kfm[pair][:, ts(i)], b0[:, 0:128]))
            free[0] = ("v", nv)
        n_kfm = nv

        eng, cnt = free[1]
        T.wait(trk[eng], cnt)
        for k in range(8):
            nt = T.op(lambda t, k=k: t.matmul(
                b1[:, 0:256], ln_fm[:, ts(k)], wv_bf[:, ts(k, 256)],
                start=(k == 0), stop=(k == 7)))
        V.wait(T, nt)
        V.wait(V, n_vset)
        nv = V.op(lambda v, i=i: v.tensor_copy(
            vtl[:, i * VP: i * VP + 260].rearrange(
                "p (kv c) -> p kv c", kv=4)[:, :, 0:64],
            b1[:, 0:256].rearrange("p (kv c) -> p kv c", kv=4)))
        free[1] = ("v", nv)

        for m in range(8):
            pq = (bt6, bt7)[m % 2]
            eng, cnt = free[6 + m % 2]
            T.wait(trk[eng], cnt)
            for k in range(8):
                nt = T.op(lambda t, k=k, m=m, pq=pq: t.matmul(
                    pq[:, 0:128],
                    wq_bf[:, k * 1024 + m * 128: k * 1024 + m * 128 + 128],
                    ln_fm[:, ts(k)], start=(k == 0), stop=(k == 7)))
            Sc.wait(T, nt)
            nsq = Sc.op(lambda s, m=m, pq=pq: s.activation(
                qfm[:, ts(m)], pq[:, 0:128], AF.Copy, scale=0.125))
            free[6 + m % 2] = ("s", nsq)
        n_qfm = nsq

        Sc.wait(V, n_kfm)
        nks = Sc.op(lambda s, i=i: s.activation(
            ksq[:, 0:128], kfm[0][:, ts(i)], AF.Square))
        nks = Sc.op(lambda s, i=i: s.activation(
            ksq[:, 128:256], kfm[1][:, ts(i)], AF.Square))
        Sc.wait(Sc, n_qfm)
        nqs = Sc.op(lambda s: s.activation(qsq[:], qfm[:], AF.Square))
        nkr = 0
        for kv in range(4):
            base = (kv % 2) * 64
            pb = (bmb, bpv)[kv % 2]
            eng, cnt = free[(5, 4)[kv % 2]]
            T.wait(trk[eng], cnt)
            T.wait(Sc, nks)
            ntk = T.op(lambda t, kv=kv, base=base, pb=pb: t.matmul(
                pb[:].bitcast(F32)[0:1, 0:128],
                ones_col[base:base + 64, 0:1],
                ksq[base:base + 64, (kv // 2) * 128:(kv // 2) * 128 + 128],
                start=True, stop=True))
            V.wait(T, ntk)
            V.wait(V, nkr)
            nkr = V.op(lambda v, kv=kv, pb=pb: v.reduce_max(
                rsc[0:1, 8 + kv: 9 + kv], pb[:].bitcast(F32)[0:1, 0:128],
                axis=AX.X))
            free[(5, 4)[kv % 2]] = ("v", nkr)
        V.wait(V, nkr)
        nkm = V.op(lambda v: v.tensor_max(rsc[0:1, 0:4], rsc[0:1, 0:4],
                                          rsc[0:1, 8:12]))
        Sc.wait(V, nkm)
        nlk = Sc.op(lambda s: s.activation(rsc[0:1, 12:16], rsc[0:1, 0:4],
                                           AF.Ln))
        V.wait(Sc, nlk)
        nbias = V.op(lambda v: v.tensor_scalar_mul(rsc[0:1, 4:8],
                                                   rsc[0:1, 12:16], 0.5))
        nqn = 0
        for j in range(16):
            base = (j % 2) * 64
            m = j // 2
            pb = (bmb, bpv)[j % 2]
            eng, cnt = free[(5, 4)[j % 2]]
            T.wait(trk[eng], cnt)
            T.wait(Sc, nqs)
            ntq = T.op(lambda t, j=j, base=base, m=m, pb=pb: t.matmul(
                pb[:].bitcast(F32)[0:1, 0:128],
                ones_col[base:base + 64, 0:1],
                qsq[base:base + 64, m * 128: m * 128 + 128],
                start=True, stop=True))
            Sc.wait(T, ntq)
            Sc.wait(Sc, nqn)
            nqn = Sc.op(lambda s, j=j, pb=pb: s.copy(
                qn2_sb[0:1, j * 128: j * 128 + 128],
                pb[:].bitcast(F32)[0:1, 0:128]))
            free[(5, 4)[j % 2]] = ("s", nqn)
        Sc.wait(Sc, nqn)
        nlq = Sc.op(lambda s: s.activation(lq_sb[:], qn2_sb[:], AF.Ln))

        klen = (i + 1) * 128
        pend = None
        nexp_hs = [0, 0]
        ntv_hs = [0, 0]
        prev_crow = 0

        def emit_pv(st):
            (j, kv, hs, n_evst) = st
            Sc.wait(V, n_evst)
            Sc.wait(T, ntv_hs[hs])
            nexp = Sc.op(lambda s, hs=hs: s.activation(
                ptsb[:, hs * HSW: hs * HSW + klen],
                stsb[:, hs * HSW: hs * HSW + klen], AF.Exp))
            nexp_hs[hs] = nexp
            eng, cnt = free[4]
            T.wait(trk[eng], cnt)
            T.wait(Sc, nexp)
            for kb in range(i + 1):
                ntv = T.op(lambda t, kb=kb, kv=kv, hs=hs: t.matmul(
                    bpv[:, 0:65],
                    ptsb[:, hs * HSW + kb * 128:
                         hs * HSW + kb * 128 + 128],
                    vtl[:, kb * VP + kv * 65: kb * VP + kv * 65 + 65],
                    start=(kb == 0), stop=(kb == i)))
            ntv_hs[hs] = ntv
            V.wait(T, ntv)
            nrl = V.op(lambda v: v.reciprocal(cols[:, 8:9], bpv[:, 64:65]))
            V.wait(V, nrl)
            nat = V.op(lambda v, j=j: v.tensor_scalar_mul(
                att[:, ts(j, 64)], bpv[:, 0:64], cols[:, 8:9]))
            free[4] = ("v", nat)
            return nat

        for j in range(16):
            h = HEAD_AT_SLOT[j]
            kv = h // 4
            qb = (j % 2) * 64
            m = j // 2
            pairt = kfm[kv // 2]
            kb_ = (kv % 2) * 64
            hs = j % 2
            Sc.wait(Sc, nlq)
            Sc.wait(V, nbias)
            Sc.wait(T, prev_crow)
            ncr = Sc.op(lambda s, j=j, kv=kv: s.activation(
                crow_sb[0:1, j * 128: j * 128 + 128],
                lq_sb[0:1, j * 128: j * 128 + 128], AF.Exp, scale=0.5,
                bias=rsc[0:1, 4 + kv: 5 + kv]))
            eng, cnt = free[5]
            T.wait(trk[eng], cnt)
            T.wait(Sc, ncr)
            ntb = T.op(lambda t, j=j: t.matmul(
                bmb[:, 0:512], negones[:],
                crow_sb[0:1, j * 128: j * 128 + 128].unsqueeze(1
                    ).broadcast_to([1, 4, 128]),
                start=True, stop=True))
            prev_crow = ntb
            V.wait(T, ntb)
            V.wait(V, V.n)
            nmc = V.op(lambda v: v.tensor_copy(mbc_sb[:, 0:512],
                                               bmb[:, 0:512]))
            free[5] = ("v", nmc)
            V.wait(V, nmc)
            nmd = V.op(lambda v: v.tensor_sub(
                mbc_sb[:, 512:640], mbc_sb[:, 0:128], maskbig[:]))
            n_evst = 0
            T.wait(Sc, n_qfm)
            T.wait(V, n_kfm)
            for g0 in range(0, i + 1, 4):
                gn = min(4, i + 1 - g0)
                gidx = (g0 // 4) % 2
                ps = (b2, b3)[gidx]
                eng, cnt = free[2 + gidx]
                T.wait(trk[eng], cnt)
                for kb in range(g0, g0 + gn):
                    nt = T.op(lambda t, kb=kb, g0=g0, ps=ps, qb=qb, m=m,
                              pairt=pairt, kb_=kb_: t.matmul(
                        ps[:, (kb - g0) * 128: (kb - g0) * 128 + 128],
                        pairt[kb_:kb_ + 64, ts(kb)],
                        qfm[qb:qb + 64, ts(m)],
                        start=True, stop=True))
                V.wait(T, nt)
                V.wait(V, nmd)
                V.wait(Sc, nexp_hs[hs])
                ndiag = 1 if g0 + gn - 1 == i else 0
                if gn - ndiag > 0:
                    n_evst = V.op(lambda v, g0=g0, gn=gn, nd=ndiag, ps=ps,
                                  hs=hs: v.scalar_tensor_tensor(
                        stsb[:, hs * HSW + g0 * 128:
                             hs * HSW + (g0 + gn - nd) * 128],
                        ps[:, 0:(gn - nd) * 128], 1.0,
                        mbc_sb[:, 0:(gn - nd) * 128], ALU.mult, ALU.add))
                if ndiag:
                    n_evst = V.op(lambda v, g0=g0, gn=gn, ps=ps, hs=hs, i=i:
                                  v.scalar_tensor_tensor(
                        stsb[:, hs * HSW + i * 128:
                             hs * HSW + i * 128 + 128],
                        ps[:, (gn - 1) * 128: gn * 128], 1.0,
                        mbc_sb[:, 512:640], ALU.mult, ALU.add))
                free[2 + gidx] = ("v", n_evst)
            cur = (j, kv, hs, n_evst)
            if pend is not None:
                emit_pv(pend)
            pend = cur
        nat = emit_pv(pend)
        n_att = nat

        V.wait(V, n_att)
        nbf = V.op(lambda v: v.tensor_copy(attbf[:], att[:]))
        T.wait(V, nbf)
        for k in range(8):
            pt = (bt6, bt7)[k % 2]
            eng, cnt = free[6 + k % 2]
            T.wait(trk[eng], cnt)
            nt = T.op(lambda t, k=k, pt=pt: t.transpose(
                pt[:].bitcast(BF16)[:, 0:128], attbf[:, ts(k)], ident[:]))
            Sc.wait(T, nt)
            nsev = Sc.op(lambda s, k=k, pt=pt: s.copy(
                attfm[:, ts(k)], pt[:].bitcast(BF16)[:, 0:128]))
            free[6 + k % 2] = ("s", nsev)
        n_attfm = nsev
        for half in range(2):
            ps = (b0, b1)[half]
            eng, cnt = free[half]
            T.wait(trk[eng], cnt)
            T.wait(Sc, n_attfm)
            for k in range(8):
                nt = T.op(lambda t, k=k, half=half, ps=ps: t.matmul(
                    ps[:, 0:512], attfm[:, ts(k)],
                    wo_bf[:, k * 1024 + half * 512: k * 1024 + half * 512
                          + 512],
                    start=(k == 0), stop=(k == 7)))
            V.wait(T, nt)
            nv = V.op(lambda v, half=half, slot=slot, ps=ps: v.tensor_tensor(
                x1t[:, half * 512: half * 512 + 512], ps[:, 0:512],
                xin[:, slot * 1024 + half * 512: slot * 1024 + half * 512
                    + 512],
                ALU.add))
            free[half] = ("v", nv)
        Q.wait(V, nv)
        nst = Q.op(lambda q, i=i: q.dma_start(
            out=x1_out[ts(i), :], in_=x1t[:]))

    Q.wait(Q, nst)


def build_phase_a(n_tiles=NT):
    nc = bass.Bass("TRN2", target_bir_lowering=False)
    Sx = n_tiles * 128
    io = {
        "x": nc.dram_tensor("x", [Sx, H], F32, kind="ExternalInput").ap(),
        "wq_t": nc.dram_tensor("wq_t", [H, 1024], F32, kind="ExternalInput").ap(),
        "wk_t": nc.dram_tensor("wk_t", [H, 256], F32, kind="ExternalInput").ap(),
        "wv_t": nc.dram_tensor("wv_t", [H, 256], F32, kind="ExternalInput").ap(),
        "wo_t": nc.dram_tensor("wo_t", [H, 1024], F32, kind="ExternalInput").ap(),
        "ident": nc.dram_tensor("ident", [128, 128], BF16, kind="ExternalInput").ap(),
        "masks": nc.dram_tensor("masks", [128, 128], F32, kind="ExternalInput").ap(),
        "ones_c": nc.dram_tensor("ones_c", [128, 1], BF16, kind="ExternalInput").ap(),
        "negones": nc.dram_tensor("negones", [1, 128], BF16, kind="ExternalInput").ap(),
        "x1": nc.dram_tensor("x1", [Sx, H], F32, kind="ExternalOutput").ap(),
    }
    es = ExitStack()
    with es:
        sb = lambda name, w, dt=F32, p=128: es.enter_context(
            nc.sbuf_tensor(name, [p, w], dt))
        psum = lambda name, w, dt=F32: es.enter_context(
            nc.psum_tensor(name, [128, w], dt))
        buf = {
            "xin": sb("xin", 2048),
            "x1t": sb("x1t", 1024),
            "ccen": sb("ccen", 1024),
            "sqscr": sb("sqscr", 1024),
            "cols": sb("cols", 24),
            "ln_bf": sb("ln_bf", 1024, BF16),
            "ln_fm": sb("ln_fm", 1024, BF16),
            "kfm": [sb("kfm0", n_tiles * 128, BF16),
                    sb("kfm1", n_tiles * 128, BF16)],
            "vtl": sb("vtl", n_tiles * 260, BF16),
            "qfm": sb("qfm", 1024, BF16),
            "ssb": sb("ssb", 2 * n_tiles * 128),
            "psb": sb("psb", 2 * n_tiles * 128, BF16),
            "mbc_sb": sb("mbc_sb", 640),
            "ones_col": sb("ones_colsb", 1, BF16),
            "negones": sb("negonessb", 128, BF16, 1),
            "ksq": sb("ksq", 256, BF16),
            "qsq": sb("qsq", 1024, BF16),
            "qn2_sb": sb("qn2_sb", 2048, F32, 1),
            "lq_sb": sb("lq_sb", 2048, F32, 1),
            "crow_sb": sb("crow_sb", 2048, BF16, 1),
            "rsc": sb("rsc", 16, F32, 1),
            "ones_r": sb("ones_r", 128, BF16, 1),
            "att": sb("att", 1024),
            "attbf": sb("attbf", 1024, BF16),
            "attfm": sb("attfm", 1024, BF16),
            "wq_bf": sb("wq_bf", 8192, BF16),
            "wk_bf": sb("wk_bf", 2048, BF16),
            "wv_bf": sb("wv_bf", 2048, BF16),
            "wo_bf": sb("wo_bf", 8192, BF16),
            "ident": sb("identsb", 128, BF16),
            "masks": sb("maskssb", 128),
            "psA": psum("psA", 512),
            "psB": psum("psB", 512),
            "ptA": psum("ptA", 512),
            "ptB": psum("ptB", 512),
            "pv": psum("pv", 512),
            "pk": psum("pk", 512),
            "pq1": psum("pq1", 512),
            "pq2": psum("pq2", 512),
        }
        ctx = Ctx(nc, es)
        blk = es.enter_context(nc.Block())
        ctx.run_passes(blk, lambda c: emit_phase_a(c, io, buf, n_tiles))
    return nc




def emit_phase_b(ctx, io, buf, n_tt=8, dbg=False, fused=False):
    T, V, Sc, Q, G = ctx.T, ctx.V, ctx.S, ctx.Q, ctx.G
    GWI, GWG, GWO = ctx.GWI, ctx.GWG, ctx.GWO
    trk = {"v": V, "s": Sc, "t": T}
    E = 8

    x1_in, x2_out = io["x1"], io["x2"]
    scr_wi, scr_wg, scr_wo = io["scr_wi"], io["scr_wg"], io["scr_wo"]

    (x1t, ln2_sb, ln2_fm, lnbf, g_buf, h_fm, sgt, wi_sb, acc,
     r_sb, gb_sb, bb_sb, ctile, cols, sqscr, bic, bgc, rowb, brf, brb,
     wr_bf, ident, ones) = (buf[k] for k in (
        "x1t", "ln2_sb", "ln2_fm", "lnbf", "g_buf", "h_fm", "sgt", "wi_sb",
        "acc", "r_sb", "gb_sb", "bb_sb", "ctile", "cols",
        "sqscr", "bic", "bgc", "rowb", "brf", "brb", "wr_bf",
        "ident", "ones"))
    wg_sb, wo_sb = buf["wg_sb"], buf["wo_sb"]
    pr = buf["pr"]

    free = {i: ("v", 0) for i in range(8)}

    if not fused:
        for e in range(E):
            G.op(lambda g, e=e: g.dma_start(out=scr_wi[e], in_=io["Wi"][e]))
            G.op(lambda g, e=e: g.dma_start(out=scr_wg[e], in_=io["Wg"][e]))
            G.op(lambda g, e=e: g.dma_start(out=scr_wo[e], in_=io["Wo"][e]))

    nc1 = Q.op(lambda q: q.dma_start(out=ident[:], in_=io["ident"]))
    nc1 = Q.op(lambda q: q.dma_start(out=ones[:], in_=io["ones1"]))
    nc1 = Q.op(lambda q: q.dma_start(out=bic[:], in_=io["bic"]))
    nc1 = Q.op(lambda q: q.dma_start(out=bgc[:], in_=io["bgc"]))
    nc1 = Q.op(lambda q: q.dma_start(out=brf[:], in_=io["br"]))
    nG = G.op(lambda g: g.dma_start(
        out=wr_bf[:].rearrange("p (k e) -> p k e", k=8),
        in_=io["Wr"].rearrange("(k p) e -> p k e", p=128)))
    n_g_setup = nG
    V.wait(Q, nc1)
    nbr = V.op(lambda v: v.tensor_copy(brb[:], brf[:]))
    neps = V.op(lambda v: v.memset(cols[:, 14:15], 1e-5))
    neps = V.op(lambda v: v.memset(cols[:, 15:16], 1e-12))
    Sc.wait(V, neps)
    T.wait(Q, nc1)
    T.wait(G, nG)
    T.wait(V, nbr)

    last_store = 0
    for tt in range(n_tt):
        ctx.barrier()
        nload = Q.op(lambda q, tt=tt: q.dma_start(
            out=x1t[:].rearrange("p (tc h) -> p tc h", tc=4),
            in_=x1_in[tt * 512:(tt + 1) * 512, :].rearrange(
                "(tc p) h -> p tc h", p=128)))
        V.wait(Q, nload)

        nse = 0
        for tc in range(4):
            xi = x1t[:, tc * 1024:(tc + 1) * 1024]
            lni = ln2_sb[:, tc * 1024:(tc + 1) * 1024]
            n1 = V.op(lambda v, xi=xi: v.reduce_sum(
                cols[:, 0:1], xi, axis=AX.X, negate=True))
            V.wait(V, n1)
            n2 = V.op(lambda v: v.tensor_scalar_mul(cols[:, 0:1],
                                                    cols[:, 0:1], 1.0 / H))
            V.wait(V, n2)
            V.wait(Q, n_lnst)
            n3 = V.op(lambda v, xi=xi, lni=lni: v.tensor_scalar_add(
                lni, xi, cols[:, 0:1]))
            Sc.wait(V, n3)
            n4 = Sc.op(lambda s, lni=lni: s.activation(
                sqscr[:], lni, AF.Square, accum_out=cols[:, 1:2]))
            V.wait(Sc, n4)
            n5 = V.op(lambda v: v.tensor_scalar_mul(cols[:, 2:3],
                                                    cols[:, 1:2], 1.0 / H))
            Sc.wait(V, n5)
            n6 = Sc.op(lambda s: s.activation(cols[:, 3:4], cols[:, 2:3],
                                              AF.Ln, bias=cols[:, 14:15]))
            Sc.wait(Sc, n6)
            n7 = Sc.op(lambda s: s.activation(cols[:, 2:3], cols[:, 3:4],
                                              AF.Exp, scale=-0.5))
            V.wait(Sc, n7)
            n8 = V.op(lambda v, lni=lni: v.tensor_scalar_mul(
                lnbf[:], lni, cols[:, 2:3]))
            V.wait(V, n8)
            n9 = V.op(lambda v, lni=lni: v.tensor_scalar_mul(
                lni, lni, cols[:, 2:3]))
            T.wait(V, n8)
            for k in range(8):
                p0 = pr[k % 2]
                eng, cnt = free[k % 2]
                T.wait(trk[eng], cnt)
                nt = T.op(lambda t, k=k, p0=p0: t.transpose(
                    p0[:].bitcast(BF16)[:, 0:128], lnbf[:, ts(k)], ident[:]))
                Sc.wait(T, nt)
                nse = Sc.op(lambda s, k=k, tc=tc, p0=p0: s.copy(
                    ln2_fm[:, k * 512 + tc * 128: k * 512 + tc * 128 + 128],
                    p0[:].bitcast(BF16)[:, 0:128]))
                free[k % 2] = ("s", nse)
        n_lnfm = nse
        if dbg and tt == 0:
            Q.wait(V, n9)
            Q.op(lambda q: q.dma_start(out=io["dbg_ln"], in_=ln2_sb[:]))
            Q.wait(Sc, n_lnfm)
            Q.op(lambda q: q.dma_start(out=io["dbg_lnfm"], in_=ln2_fm[:]))

        prev_nC = 0
        for tc in range(4):
            eng, cnt = free[4]
            T.wait(trk[eng], cnt)
            T.wait(Sc, n_lnfm)
            for k in range(8):
                nt = T.op(lambda t, k=k, tc=tc: t.matmul(
                    pr[4][:, 0:8],
                    ln2_fm[:, k * 512 + tc * 128: k * 512 + tc * 128 + 128],
                    wr_bf[:, k * 8: k * 8 + 8], start=(k == 0), stop=False))
            nt = T.op(lambda t: t.matmul(
                pr[4][:, 0:8], ones[:], brb[:], start=False, stop=True))
            V.wait(T, nt)
            V.wait(Sc, prev_nC)
            nA = V.op(lambda v: v.reduce_max(cols[:, 4:5], pr[4][:, 0:8],
                                             axis=AX.X, negate=True))
            V.wait(V, nA)
            nB = V.op(lambda v: v.tensor_scalar_add(ctile[:, 48:56],
                                                    pr[4][:, 0:8],
                                                    cols[:, 4:5]))
            free[4] = ("v", nB)
            Sc.wait(V, nB)
            nC = Sc.op(lambda s: s.activation(ctile[:, 40:48], ctile[:, 48:56],
                                              AF.Exp))
            prev_nC = nC
            V.wait(Sc, nC)
            nD = V.op(lambda v: v.reduce_max(cols[:, 5:6], ctile[:, 40:48],
                                             axis=AX.X))
            V.wait(V, nD)
            nE = V.op(lambda v: v.tensor_single_scalar(
                ctile[:, 32:40], ctile[:, 40:48], cols[:, 5:6], ALU.is_equal))
            V.wait(V, nE)
            nF = V.op(lambda v: v.scalar_tensor_tensor(
                ctile[:, 32:40], ctile[:, 32:40], NEG, ctile[:, 40:48],
                ALU.mult, ALU.add))
            V.wait(V, nF)
            nGm = V.op(lambda v: v.reduce_max(cols[:, 6:7], ctile[:, 32:40],
                                              axis=AX.X))
            V.wait(V, nGm)
            nH = V.op(lambda v: v.tensor_single_scalar(
                ctile[:, 32:40], ctile[:, 40:48], cols[:, 6:7], ALU.is_ge))
            nI = V.op(lambda v: v.tensor_add(cols[:, 7:8], cols[:, 5:6],
                                             cols[:, 6:7]))
            V.wait(V, nI)
            nJ = V.op(lambda v: v.reciprocal(cols[:, 7:8], cols[:, 7:8]))
            V.wait(V, nJ)
            nK = V.op(lambda v: v.tensor_mul(ctile[:, 32:40], ctile[:, 32:40],
                                             ctile[:, 40:48]))
            V.wait(V, nK)
            nL = V.op(lambda v, tc=tc: v.tensor_scalar_mul(
                ctile[:, tc * 8:(tc + 1) * 8], ctile[:, 32:40], cols[:, 7:8]))
            nM = V.op(lambda v, tc=tc: v.tensor_copy(
                acc[:, tc * 1024:(tc + 1) * 1024],
                x1t[:, tc * 1024:(tc + 1) * 1024]))
        n_accinit = nM

        if dbg and tt == 0:
            Q.wait(V, nM)
            Q.op(lambda q: q.dma_start(out=io["dbg_c"], in_=ctile[:, 0:32]))

        wlast_wi = [0, 0]
        wlast_wg = [0, 0]
        wlast_wo = [0, 0]
        sg_use = [0, 0]
        sq_use = 0
        rlast = [0, 0]
        n_rmm_prev = 0
        n_h_prev = 0
        for e in range(E):
            G.wait(G, n_g_setup)
            Sc.wait(V, n_h_prev)
            for fc in range(8):
                slot = fc % 2
                GWI[slot].wait(T, wlast_wi[slot])
                nwld = GWI[slot].op(lambda g, e=e, fc=fc, slot=slot: g.dma_start(
                    out=wi_sb[:, slot * 4096:(slot + 1) * 4096].rearrange(
                        "p (k f) -> p k f", k=8),
                    in_=scr_wi[e].rearrange("(k p) f -> p k f", p=128)[
                        :, :, fc * 512:(fc + 1) * 512]))
                T.wait(GWI[slot], nwld)
                for fb in range(4):
                    f = fc * 4 + fb
                    pp = pr[f % 4]
                    eng, cnt = free[f % 4]
                    T.wait(trk[eng], cnt)
                    for k in range(8):
                        nt = T.op(lambda t, k=k, fb=fb, slot=slot, pp=pp:
                                  t.matmul(
                            pp[:, 0:512],
                            wi_sb[:, slot * 4096 + k * 512 + fb * 128:
                                  slot * 4096 + k * 512 + fb * 128 + 128],
                            ln2_fm[:, ts(k, 512)],
                            start=(k == 0), stop=(k == 7)))
                    Sc.wait(T, nt)
                    nse = Sc.op(lambda s, f=f, e=e, pp=pp: s.activation(
                        g_buf[:, ts(f, 512)], pp[:, 0:512], AF.Gelu,
                        bias=bic[:, e * 32 + f: e * 32 + f + 1]))
                    free[f % 4] = ("s", nse)
                wlast_wi[slot] = nt

            V.wait(T, n_rmm_prev)
            nvh = 0
            for fc in range(8):
                slot = fc % 2
                GWG[slot].wait(T, wlast_wg[slot])
                nwld = GWG[slot].op(lambda g, e=e, fc=fc, slot=slot: g.dma_start(
                    out=wg_sb[:, slot * 4096:(slot + 1) * 4096].rearrange(
                        "p (k f) -> p k f", k=8),
                    in_=scr_wg[e].rearrange("(k p) f -> p k f", p=128)[
                        :, :, fc * 512:(fc + 1) * 512]))
                T.wait(GWG[slot], nwld)
                for fb in range(4):
                    f = fc * 4 + fb
                    fs = f % 2
                    pp = pr[f % 4]
                    eng, cnt = free[f % 4]
                    T.wait(trk[eng], cnt)
                    for k in range(8):
                        nt = T.op(lambda t, k=k, fb=fb, slot=slot, pp=pp:
                                  t.matmul(
                            pp[:, 0:512],
                            wg_sb[:, slot * 4096 + k * 512 + fb * 128:
                                  slot * 4096 + k * 512 + fb * 128 + 128],
                            ln2_fm[:, ts(k, 512)],
                            start=(k == 0), stop=(k == 7)))
                    Sc.wait(T, nt)
                    Sc.wait(V, sg_use[fs])
                    nse = Sc.op(lambda s, f=f, e=e, pp=pp, fs=fs:
                                s.activation(
                        sgt[:, fs * 512:(fs + 1) * 512], pp[:, 0:512],
                        AF.Sigmoid,
                        bias=bgc[:, e * 32 + f: e * 32 + f + 1]))
                    free[f % 4] = ("s", nse)
                    V.wait(Sc, nse)
                    nvh = V.op(lambda v, f=f, fs=fs: v.tensor_mul(
                        h_fm[:, ts(f, 512)], g_buf[:, ts(f, 512)],
                        sgt[:, fs * 512:(fs + 1) * 512]))
                    sg_use[fs] = nvh
                wlast_wg[slot] = nt
            n_h = nvh
            n_h_prev = nvh

            if dbg and tt == 0 and e == 0:
                Q.wait(V, n_h)
                Q.op(lambda q: q.dma_start(out=io["dbg_g"], in_=g_buf[:]))
                Q.op(lambda q: q.dma_start(out=io["dbg_h"], in_=h_fm[:]))

            G.wait(T, n_rmm_prev)
            G.wait(G, G.n)
            nrow = G.op(lambda g, e=e: g.dma_start(
                out=rowb[:].rearrange("o (a h) -> o a h", a=3),
                in_=io["rows"][:, e, :].unsqueeze(0)))
            T.wait(G, nrow)
            for which, dst in ((0, gb_sb), (1, bb_sb)):
                for half in range(2):
                    bank = 6 + which
                    eng, cnt = free[bank]
                    T.wait(trk[eng], cnt)
                    nt = T.op(lambda t, which=which, half=half, bank=bank:
                              t.matmul(pr[bank][:, 0:512], ones[:],
                                       rowb[0:1, 1024 + which * 1024
                                            + half * 512: 1024 + which * 1024
                                            + half * 512 + 512],
                                       start=True, stop=True))
                    V.wait(T, nt)
                    nv = V.op(lambda v, dst=dst, half=half, bank=bank:
                              v.tensor_copy(dst[:, half * 512:
                                                half * 512 + 512],
                                            pr[bank][:, 0:512]))
                    free[bank] = ("v", nv)

            if dbg and tt == 0 and e == 0:
                Q.wait(V, nv)
                Q.op(lambda q: q.dma_start(out=io["dbg_gb"], in_=gb_sb[:]))
                Q.op(lambda q: q.dma_start(out=io["dbg_bb"], in_=bb_sb[:]))

            T.wait(V, n_h)
            for b_ in range(8):
                eng, cnt = free[b_]
                T.wait(trk[eng], cnt)
            for fg in range(8):
                slot = fg % 2
                GWO[slot].wait(T, wlast_wo[slot])
                nwld = GWO[slot].op(lambda g, e=e, fg=fg, slot=slot: g.dma_start(
                    out=wo_sb[:, slot * 4096:(slot + 1) * 4096].rearrange(
                        "p (g o) -> p g o", g=4),
                    in_=scr_wo[e][fg * 512:(fg + 1) * 512, :].rearrange(
                        "(g p) o -> p g o", p=128)))
                T.wait(GWO[slot], nwld)
                for tc in range(4):
                    for fb in range(4):
                        f = fg * 4 + fb
                        for half in range(2):
                            nt = T.op(lambda t, f=f, tc=tc, fb=fb, half=half,
                                      slot=slot: t.matmul(
                                pr[tc * 2 + half][:, 0:512],
                                h_fm[:, f * 512 + tc * 128:
                                     f * 512 + tc * 128 + 128],
                                wo_sb[:, slot * 4096 + fb * 1024 + half * 512:
                                      slot * 4096 + fb * 1024 + half * 512
                                      + 512],
                                start=(fg == 0 and fb == 0), stop=False))
                wlast_wo[slot] = nt
            for tc in range(4):
                for half in range(2):
                    nt = T.op(lambda t, tc=tc, half=half, e=e: t.matmul(
                        pr[tc * 2 + half][:, 0:512], ones[:],
                        rowb[0:1, half * 512: half * 512 + 512],
                        start=False, stop=True))
            n_rmm = nt
            n_rmm_prev = nt
            for tc in range(4):
                V.wait(T, n_rmm)
                rsl = tc % 2
                rs = r_sb[:, rsl * 1024: rsl * 1024 + 1024]
                V.wait(V, rlast[rsl])
                for half in range(2):
                    nv1 = V.op(lambda v, tc=tc, half=half, rs=rs: v.tensor_add(
                        rs[:, half * 512: half * 512 + 512],
                        pr[tc * 2 + half][:, 0:512],
                        ln2_sb[:, tc * 1024 + half * 512:
                               tc * 1024 + half * 512 + 512]))
                    free[tc * 2 + half] = ("v", nv1)
                V.wait(V, nv1)
                if dbg and tt == 0 and e == 0 and tc == 0:
                    Q.wait(V, nv1)
                    ndbg = Q.op(lambda q, rs=rs: q.dma_start(
                        out=io["dbg_r"], in_=rs))
                    V.wait(Q, ndbg)
                n1 = V.op(lambda v, rs=rs: v.reduce_sum(
                    cols[:, 8:9], rs, axis=AX.X, negate=True))
                V.wait(V, n1)
                n2 = V.op(lambda v: v.tensor_scalar_mul(cols[:, 8:9],
                                                        cols[:, 8:9], 1.0 / H))
                V.wait(V, n2)
                Sc.wait(V, n2)
                Sc.wait(V, sq_use)
                n4 = Sc.op(lambda s, rs=rs: s.activation(
                    sqscr[:], rs, AF.Square, bias=cols[:, 8:9],
                    accum_out=cols[:, 9:10]))
                V.wait(Sc, n4)
                n5 = V.op(lambda v: v.tensor_scalar_mul(cols[:, 10:11],
                                                        cols[:, 9:10],
                                                        1.0 / H))
                Sc.wait(V, n5)
                n6 = Sc.op(lambda s: s.activation(cols[:, 11:12],
                                                  cols[:, 10:11], AF.Ln,
                                                  bias=cols[:, 15:16]))
                Sc.wait(Sc, n6)
                n7 = Sc.op(lambda s: s.activation(cols[:, 10:11],
                                                  cols[:, 11:12], AF.Exp,
                                                  scale=-0.5))
                V.wait(Sc, n7)
                n9 = V.op(lambda v, tc=tc, e=e: v.tensor_mul(
                    cols[:, 12:13], ctile[:, tc * 8 + e: tc * 8 + e + 1],
                    cols[:, 10:11]))
                V.wait(V, n9)
                nta = V.op(lambda v, rs=rs: v.tensor_scalar(
                    sqscr[:], rs, cols[:, 8:9], cols[:, 12:13],
                    ALU.add, ALU.mult))
                rlast[rsl] = nta
                V.wait(V, nta)
                ntb = V.op(lambda v: v.tensor_mul(sqscr[:], sqscr[:],
                                                  gb_sb[:]))
                V.wait(V, ntb)
                V.wait(V, n_accinit)
                ntc_ = V.op(lambda v, tc=tc: v.tensor_add(
                    acc[:, tc * 1024:(tc + 1) * 1024],
                    acc[:, tc * 1024:(tc + 1) * 1024], sqscr[:]))
                sq_use = ntc_
                V.wait(V, ntc_)
                ntd = V.op(lambda v, tc=tc, e=e: v.scalar_tensor_tensor(
                    acc[:, tc * 1024:(tc + 1) * 1024], bb_sb[:],
                    ctile[:, tc * 8 + e: tc * 8 + e + 1],
                    acc[:, tc * 1024:(tc + 1) * 1024], ALU.mult, ALU.add))

        ctx.barrier()
        last_store = Q.op(lambda q, tt=tt: q.dma_start(
            out=x2_out[tt * 512:(tt + 1) * 512, :].rearrange(
                "(tc p) h -> p tc h", p=128),
            in_=acc[:].rearrange("p (tc h) -> p tc h", tc=4)))

    Q.wait(Q, last_store)


def build_phase_b(n_tt=8, dbg=False):
    nc = bass.Bass("TRN2", target_bir_lowering=False)
    Sx = n_tt * 512
    E, F = 8, 4096
    io = {
        "x1": nc.dram_tensor("x1", [Sx, H], F32, kind="ExternalInput").ap(),
        "Wi": nc.dram_tensor("Wi", [E, H, F], F32, kind="ExternalInput").ap(),
        "Wg": nc.dram_tensor("Wg", [E, H, F], F32, kind="ExternalInput").ap(),
        "Wo": nc.dram_tensor("Wo", [E, F, H], F32, kind="ExternalInput").ap(),
        "Wr": nc.dram_tensor("Wr", [H, E], F32, kind="ExternalInput").ap(),
        "bic": nc.dram_tensor("bic", [128, 256], F32, kind="ExternalInput").ap(),
        "bgc": nc.dram_tensor("bgc", [128, 256], F32, kind="ExternalInput").ap(),
        "rows": nc.dram_tensor("rows", [3, E, 1024], F32,
                               kind="ExternalInput").ap(),
        "br": nc.dram_tensor("br", [1, 8], F32, kind="ExternalInput").ap(),
        "ident": nc.dram_tensor("ident", [128, 128], BF16,
                                kind="ExternalInput").ap(),
        "ones1": nc.dram_tensor("ones1", [1, 128], BF16,
                                kind="ExternalInput").ap(),
        "x2": nc.dram_tensor("x2", [Sx, H], F32, kind="ExternalOutput").ap(),
        "scr_wi": nc.dram_tensor("scr_wi", [E, H, F], BF16, kind="Internal").ap(),
        "scr_wg": nc.dram_tensor("scr_wg", [E, H, F], BF16, kind="Internal").ap(),
        "scr_wo": nc.dram_tensor("scr_wo", [E, F, H], BF16, kind="Internal").ap(),
    }
    if dbg:
        io["dbg_ln"] = nc.dram_tensor("dbg_ln", [128, 4096], F32, kind="ExternalOutput").ap()
        io["dbg_lnfm"] = nc.dram_tensor("dbg_lnfm", [128, 4096], BF16, kind="ExternalOutput").ap()
        io["dbg_c"] = nc.dram_tensor("dbg_c", [128, 32], F32, kind="ExternalOutput").ap()
        io["dbg_gb"] = nc.dram_tensor("dbg_gb", [128, 1024], F32, kind="ExternalOutput").ap()
        io["dbg_bb"] = nc.dram_tensor("dbg_bb", [128, 1024], F32, kind="ExternalOutput").ap()
        io["dbg_g"] = nc.dram_tensor("dbg_g", [128, 16384], BF16, kind="ExternalOutput").ap()
        io["dbg_h"] = nc.dram_tensor("dbg_h", [128, 16384], BF16, kind="ExternalOutput").ap()
        io["dbg_r"] = nc.dram_tensor("dbg_r", [128, 1024], F32, kind="ExternalOutput").ap()
    es = ExitStack()
    with es:
        sb = lambda name, w, dt=F32, p=128: es.enter_context(
            nc.sbuf_tensor(name, [p, w], dt))
        buf = {
            "x1t": sb("x1t", 4096),
            "ln2_sb": sb("ln2_sb", 4096),
            "ln2_fm": sb("ln2_fm", 4096, BF16),
            "lnbf": sb("lnbf", 1024, BF16),
            "g_buf": sb("g_buf", 16384, BF16),
            "h_fm": sb("h_fm", 16384, BF16),
            "sgt": sb("sgt", 1024, BF16),
            "wi_sb": sb("wi_sb", 8192, BF16),
            "wg_sb": sb("wg_sb", 8192, BF16),
            "wo_sb": sb("wo_sb", 8192, BF16),
            "acc": sb("acc", 4096),
            "r_sb": sb("r_sb", 2048),
            "gb_sb": sb("gb_sb", 1024),
            "bb_sb": sb("bb_sb", 1024),
            "ctile": sb("ctile", 64),
            "cols": sb("cols", 16),
            "sqscr": sb("sqscr", 1024),
            "bic": sb("bicsb", 256),
            "bgc": sb("bgcsb", 256),
            "rowb": sb("rowb", 3072, BF16, 1),
            "brf": sb("brf", 8, F32, 1),
            "brb": sb("brb", 8, BF16, 1),
            "wr_bf": sb("wr_bfsb", 64, BF16),
            "ident": sb("identsb", 128, BF16),
            "ones": sb("onessb", 128, BF16, 1),
            "pr": [es.enter_context(nc.psum_tensor(f"pr{i}", [128, 512], F32))
                   for i in range(8)],
        }
        ctx = Ctx(nc, es)
        ctx.GWI = (ctx.queue("gwi0", engine="q"), ctx.queue("gwi1", engine="q"))
        ctx.GWG = (ctx.queue("gwg0", engine="q"), ctx.queue("gwg1", engine="q"))
        ctx.GWO = (ctx.queue("gwo0", engine="q"), ctx.queue("gwo1", engine="q"))
        blk = es.enter_context(nc.Block())
        ctx.run_passes(blk, lambda c: emit_phase_b(c, io, buf, n_tt, dbg))
    return nc


def host_prep_b(inputs):
    bi = np.asarray(inputs["L0_ffn_bi"], np.float32)
    bg = np.asarray(inputs["L0_ffn_bg"], np.float32)
    bo = np.asarray(inputs["L0_ffn_bo"], np.float32)
    gam = np.asarray(inputs["L0_ffn_gamma"], np.float32)
    bet = np.asarray(inputs["L0_ffn_beta"], np.float32)
    br = np.asarray(inputs["L0_ffn_br"], np.float32)
    rows = np.stack([bo, gam, bet])
    return {
        "Wi": np.ascontiguousarray(inputs["L0_ffn_Wi"], np.float32),
        "Wg": np.ascontiguousarray(inputs["L0_ffn_Wg"], np.float32),
        "Wo": np.ascontiguousarray(inputs["L0_ffn_Wo"], np.float32),
        "Wr": np.ascontiguousarray(inputs["L0_ffn_Wr"], np.float32),
        "bic": np.ascontiguousarray(
            bi.reshape(8, 32, 128).transpose(2, 0, 1).reshape(128, 256)),
        "bgc": np.ascontiguousarray(
            bg.reshape(8, 32, 128).transpose(2, 0, 1).reshape(128, 256)),
        "rows": np.ascontiguousarray(rows),
        "br": np.ascontiguousarray(br.reshape(1, 8)),
        "ident": np.eye(128, dtype=ml_dtypes.bfloat16),
        "ones1": np.ones((1, 128), dtype=ml_dtypes.bfloat16),
    }





def emit_phase_b_sparse(ctx, io, buf, n_tt=8, fused=False):
    T, V, Sc, Q, G = ctx.T, ctx.V, ctx.S, ctx.Q, ctx.G
    GWI, GWG, GWO, GL = ctx.GWI, ctx.GWG, ctx.GWO, ctx.GL
    trk = {"v": V, "s": Sc, "t": T}
    E = 8
    TTOK = n_tt * 512
    NTC = n_tt * 4
    CAP = ((int(TTOK * 2 / 8 * 3.0) + 127) // 128) * 128
    NSC = CAP // 128
    SUBT = [min(512, CAP - s) for s in range(0, CAP, 512)]

    x1_in = io["x1"]
    x2_acc = io["scr_accx"]
    scr_wi, scr_wg, scr_wo = io["scr_wi"], io["scr_wg"], io["scr_wo"]
    scr_ln, scr_lnf, scr_rec = io["scr_ln"], io["scr_lnf"], io["scr_rec"]

    (x1t, ln2_sb, ln2_fm, lnbf, h_fm, sgt, wi_sb, acc,
     r_sb, gb_sb, bb_sb, ctile, cols, sqscr, bic, bgc, rowb, brf, brb,
     wr_bf, ident, ones) = (buf[k] for k in (
        "x1t", "ln2_sb", "ln2_fm", "lnbf", "h_fm", "sgt", "wi_sb",
        "acc", "r_sb", "gb_sb", "bb_sb", "ctile", "cols",
        "sqscr", "bic", "bgc", "rowb", "brf", "brb", "wr_bf",
        "ident", "ones"))
    wg_sb, wo_sb = buf["wg_sb"], buf["wo_sb"]
    pr = buf["pr"]
    rec_sb, islots, iota_c = buf["rec_sb"], buf["islots"], buf["iota_c"]
    onecol, ecol, idf32 = buf["onecol"], buf["ecol"], buf["idf32"]
    grec, gtok, zrec, y_out = (buf["grec"], buf["gtok"], buf["zrec"],
                               buf["y_out"])
    cfm = h_fm[:].bitcast(F32)[0:8, 0:TTOK]
    msk = h_fm[:].bitcast(F32)[0:8, TTOK:2 * TTOK]
    msk2 = wi_sb[:].bitcast(F32)[0:8, 0:TTOK]
    gbf = buf["gln"][:]
    free = {i: ("v", 0) for i in range(8)}
    ctx.n_gb_last = 0

    if not fused:
        for e in range(E):
            G.op(lambda g, e=e: g.dma_start(out=scr_wi[e], in_=io["Wi"][e]))
            G.op(lambda g, e=e: g.dma_start(out=scr_wg[e], in_=io["Wg"][e]))
            G.op(lambda g, e=e: g.dma_start(out=scr_wo[e], in_=io["Wo"][e]))

    nc1 = Q.op(lambda q: q.dma_start(out=ident[:], in_=io["ident"]))
    nc1 = Q.op(lambda q: q.dma_start(out=idf32[:], in_=io["idf32"]))
    nc1 = Q.op(lambda q: q.dma_start(out=ones[:], in_=io["ones1"]))
    nc1 = Q.op(lambda q: q.dma_start(out=bic[:], in_=io["bic"]))
    nc1 = Q.op(lambda q: q.dma_start(out=bgc[:], in_=io["bgc"]))
    nc1 = Q.op(lambda q: q.dma_start(out=brf[:], in_=io["br"]))
    nc1 = Q.op(lambda q: q.dma_start(out=iota_c[:], in_=io["iota_c"]))
    nc1 = Q.op(lambda q: q.dma_start(out=ecol[:], in_=io["ecolv"]))
    nc1 = Q.op(lambda q: q.dma_start(out=buf["dumpc"][:], in_=io["dumpc"]))
    nc1 = Q.op(lambda q: q.dma_start(out=zrec[:], in_=io["zrecv"]))
    nG = G.op(lambda g: g.dma_start(
        out=wr_bf[:].rearrange("p (k e) -> p k e", k=8),
        in_=io["Wr"].rearrange("(k p) e -> p k e", p=128)))
    n_g_setup = nG
    G.wait(G, nG)
    V.wait(Q, nc1)
    nbr = V.op(lambda v: v.tensor_copy(brb[:], brf[:]))
    neps = V.op(lambda v: v.memset(cols[:, 14:15], 1e-5))
    neps = V.op(lambda v: v.memset(cols[:, 15:16], 1e-12))
    noc = V.op(lambda v: v.memset(onecol[:], 1.0))
    noc = V.op(lambda v: v.memset(rec_sb[:], 0.0))
    Sc.wait(V, neps)
    T.wait(Q, nc1)
    T.wait(G, nG)
    T.wait(V, nbr)
    G.wait(Q, nc1)
    nzf = V.op(lambda v: v.memset(lnbf[:], 0.0))
    nzf = V.op(lambda v: v.memset(ln2_sb[:, 0:1024], 0.0))
    Q.wait(V, nzf)
    nzfst = Q.op(lambda q: q.dma_start(out=scr_ln[TTOK:TTOK + 128, :],
                                       in_=lnbf[:]))
    nzfst = Q.op(lambda q: q.dma_start(out=scr_lnf[TTOK:TTOK + 128, :],
                                       in_=ln2_sb[:, 0:1024]))
    Q.wait(Q, nc1)
    nzrec = 0
    for z in range(E * CAP // 128):
        nzrec = Q.op(lambda q, z=z: q.dma_start(
            out=scr_rec[z * 128:(z + 1) * 128, :], in_=zrec[:]))

    last_cfm = 0
    n_lnst = nzfst
    nR = 0
    for tt in range(n_tt):
        ctx.barrier()
        nload = Q.op(lambda q, tt=tt: q.dma_start(
            out=x1t[:].rearrange("p (tc h) -> p tc h", tc=4),
            in_=x1_in[tt * 512:(tt + 1) * 512, :].rearrange(
                "(tc p) h -> p tc h", p=128)))
        V.wait(Q, nload)
        Q.wait(Q, nload)
        nacc0 = Q.op(lambda q, tt=tt: q.dma_start(
            out=x2_acc[tt * 512:(tt + 1) * 512, :].rearrange(
                "(tc p) h -> p tc h", p=128),
            in_=x1t[:].rearrange("p (tc h) -> p tc h", tc=4)))
        ctx.nacc0 = nacc0

        for tc in range(4):
            gtc = tt * 4 + tc
            xi = x1t[:, tc * 1024:(tc + 1) * 1024]
            lni = ln2_sb[:, tc * 1024:(tc + 1) * 1024]
            n1 = V.op(lambda v, xi=xi: v.reduce_sum(
                cols[:, 0:1], xi, axis=AX.X, negate=True))
            V.wait(V, n1)
            n2 = V.op(lambda v: v.tensor_scalar_mul(cols[:, 0:1],
                                                    cols[:, 0:1], 1.0 / H))
            V.wait(V, n2)
            V.wait(Q, n_lnst)
            n3 = V.op(lambda v, xi=xi, lni=lni: v.tensor_scalar_add(
                lni, xi, cols[:, 0:1]))
            Sc.wait(V, n3)
            n4 = Sc.op(lambda s, lni=lni: s.activation(
                sqscr[:], lni, AF.Square, accum_out=cols[:, 1:2]))
            V.wait(Sc, n4)
            n5 = V.op(lambda v: v.tensor_scalar_mul(cols[:, 2:3],
                                                    cols[:, 1:2], 1.0 / H))
            Sc.wait(V, n5)
            n6 = Sc.op(lambda s: s.activation(cols[:, 3:4], cols[:, 2:3],
                                              AF.Ln, bias=cols[:, 14:15]))
            Sc.wait(Sc, n6)
            n7 = Sc.op(lambda s: s.activation(cols[:, 2:3], cols[:, 3:4],
                                              AF.Exp, scale=-0.5))
            V.wait(Sc, n7)
            V.wait(Q, n_lnst)
            n8 = V.op(lambda v, lni=lni: v.tensor_scalar_mul(
                lnbf[:], lni, cols[:, 2:3]))
            V.wait(V, n8)
            n9 = V.op(lambda v, lni=lni: v.tensor_scalar_mul(
                lni, lni, cols[:, 2:3]))
            Q.wait(V, n9)
            n_lnst = Q.op(lambda q, gtc=gtc: q.dma_start(
                out=scr_ln[gtc * 128:(gtc + 1) * 128, :], in_=lnbf[:]))
            n_lnst = Q.op(lambda q, gtc=gtc, lni=lni: q.dma_start(
                out=scr_lnf[gtc * 128:(gtc + 1) * 128, :], in_=lni))
            T.wait(V, n8)
            for k in range(8):
                p0 = pr[k % 2]
                eng, cnt = free[k % 2]
                T.wait(trk[eng], cnt)
                nt = T.op(lambda t, k=k, p0=p0: t.transpose(
                    p0[:].bitcast(BF16)[:, 0:128], lnbf[:, ts(k)], ident[:]))
                Sc.wait(T, nt)
                nse = Sc.op(lambda s, k=k, tc=tc, p0=p0: s.copy(
                    ln2_fm[:, k * 512 + tc * 128: k * 512 + tc * 128 + 128],
                    p0[:].bitcast(BF16)[:, 0:128]))
                free[k % 2] = ("s", nse)
            n_lnfm = nse
            eng, cnt = free[4]
            T.wait(trk[eng], cnt)
            T.wait(Sc, n_lnfm)
            for k in range(8):
                nt = T.op(lambda t, k=k, tc=tc: t.matmul(
                    pr[4][:, 0:8],
                    ln2_fm[:, k * 512 + tc * 128: k * 512 + tc * 128 + 128],
                    wr_bf[:, k * 8: k * 8 + 8], start=(k == 0), stop=False))
            nt = T.op(lambda t: t.matmul(
                pr[4][:, 0:8], ones[:], brb[:], start=False, stop=True))
            V.wait(T, nt)
            nA = V.op(lambda v: v.reduce_max(cols[:, 4:5], pr[4][:, 0:8],
                                             axis=AX.X, negate=True))
            V.wait(V, nA)
            nB = V.op(lambda v: v.tensor_scalar_add(ctile[:, 48:56],
                                                    pr[4][:, 0:8],
                                                    cols[:, 4:5]))
            free[4] = ("v", nB)
            Sc.wait(V, nB)
            nC = Sc.op(lambda s: s.activation(ctile[:, 40:48],
                                              ctile[:, 48:56], AF.Exp))
            V.wait(Sc, nC)
            nD = V.op(lambda v: v.reduce_max(cols[:, 5:6], ctile[:, 40:48],
                                             axis=AX.X))
            V.wait(V, nD)
            nE = V.op(lambda v: v.tensor_single_scalar(
                ctile[:, 32:40], ctile[:, 40:48], cols[:, 5:6],
                ALU.is_equal))
            V.wait(V, nE)
            nF = V.op(lambda v: v.scalar_tensor_tensor(
                ctile[:, 32:40], ctile[:, 32:40], NEG, ctile[:, 40:48],
                ALU.mult, ALU.add))
            V.wait(V, nF)
            nGm = V.op(lambda v: v.reduce_max(cols[:, 6:7], ctile[:, 32:40],
                                              axis=AX.X))
            V.wait(V, nGm)
            nH = V.op(lambda v: v.tensor_single_scalar(
                ctile[:, 32:40], ctile[:, 40:48], cols[:, 6:7], ALU.is_ge))
            nI = V.op(lambda v: v.tensor_add(cols[:, 7:8], cols[:, 5:6],
                                             cols[:, 6:7]))
            V.wait(V, nI)
            nJ = V.op(lambda v: v.reciprocal(cols[:, 7:8], cols[:, 7:8]))
            V.wait(V, nJ)
            nK = V.op(lambda v: v.tensor_mul(ctile[:, 32:40],
                                             ctile[:, 32:40],
                                             ctile[:, 40:48]))
            V.wait(V, nK)
            nL = V.op(lambda v: v.tensor_scalar_mul(
                ctile[:, 0:8], ctile[:, 32:40], cols[:, 7:8]))
            V.wait(V, nL)
            nR = V.op(lambda v, gtc=gtc: v.tensor_copy(
                rec_sb[:, gtc * 16: gtc * 16 + 8], ctile[:, 0:8]))
            nR = V.op(lambda v, gtc=gtc: v.tensor_copy(
                rec_sb[:, gtc * 16 + 8: gtc * 16 + 9],
                iota_c[:, gtc: gtc + 1]))
            nCb = V.op(lambda v: v.tensor_copy(
                ctile[:].bitcast(BF16)[:, 112:120], ctile[:, 0:8]))
            eng, cnt = free[5]
            T.wait(trk[eng], cnt)
            T.wait(V, nCb)
            ntc2 = T.op(lambda t: t.transpose(
                pr[5][:].bitcast(BF16)[0:8, 0:128],
                ctile[:].bitcast(BF16)[:, 112:120], ident[:]))
            V.wait(T, ntc2)
            V.wait(V, last_cfm)
            ncf = V.op(lambda v, gtc=gtc: v.tensor_copy(
                cfm[0:8, gtc * 128:(gtc + 1) * 128],
                pr[5][:].bitcast(BF16)[0:8, 0:128]))
            last_cfm = ncf
            free[5] = ("v", ncf)

    ctx.barrier()
    nm = V.op(lambda v: v.tensor_single_scalar(msk, cfm, 0.0, ALU.is_gt))
    V.wait(V, nm)
    nm = V.op(lambda v: v.tensor_tensor_scan(
        cfm, onecol[:].broadcast_to([8, TTOK]), msk, 0.0,
        ALU.mult, ALU.add))
    V.wait(V, nm)
    nm = V.op(lambda v: v.tensor_scalar_add(cfm, cfm, -1.0))
    V.wait(V, nm)
    nm = V.op(lambda v: v.tensor_single_scalar(msk2, cfm, float(CAP) - 0.5,
                                               ALU.is_lt))
    V.wait(V, nm)
    nm = V.op(lambda v: v.tensor_mul(msk, msk, msk2))
    V.wait(V, nm)
    nm = V.op(lambda v: v.tensor_scalar(cfm, cfm, ecol[:, 0:1], None,
                                        ALU.add))
    V.wait(V, nm)
    nm = V.op(lambda v: v.tensor_mul(cfm, cfm, msk))
    V.wait(V, nm)
    nm = V.op(lambda v: v.scalar_tensor_tensor(
        msk2, msk, -1e9, onecol[:].broadcast_to([8, TTOK]),
        ALU.mult, ALU.bypass))
    V.wait(V, nm)
    nm = V.op(lambda v: v.scalar_tensor_tensor(
        msk2, msk, -1.0, onecol[:].broadcast_to([8, TTOK]),
        ALU.mult, ALU.add))
    V.wait(V, nm)
    nm = V.op(lambda v: v.scalar_tensor_tensor(
        cfm, msk2, 1e6, cfm, ALU.mult, ALU.add))
    V.wait(V, nm)
    nslt = 0
    nsc = 0
    bcreg = [None]
    def _bc(g):
        if bcreg[0] is None:
            bcreg[0] = g.to_reg(E * CAP - 1)
        return bcreg[0]
    G.wait(Q, nzrec)
    G.wait(V, nR)
    T.wait(V, nm)
    for gtc in range(NTC):
        eng, cnt = free[gtc % 2]
        T.wait(trk[eng], cnt)
        ntx = T.op(lambda t, gtc=gtc: t.transpose(
            pr[gtc % 2][:, 0:8], cfm[0:8, gtc * 128:(gtc + 1) * 128],
            idf32[0:8, 0:8]))
        V.wait(T, ntx)
        V.wait(V, nslt)
        nslt = V.op(lambda v, gtc=gtc: v.tensor_copy(
            islots[:, gtc * 8:(gtc + 1) * 8], pr[gtc % 2][:, 0:8]))
        free[gtc % 2] = ("v", nslt)
        G.wait(V, nslt)
        for e in range(E):
            nsc = G.op(lambda g, gtc=gtc, e=e: g.indirect_dma_start(
                out=scr_rec,
                out_offset=bass.IndirectOffsetOnAxis(
                    ap=islots[:, gtc * 8 + e: gtc * 8 + e + 1], axis=0),
                in_=rec_sb[:, gtc * 16:(gtc + 1) * 16],
                in_offset=None,
                bounds_check=_bc(g), oob_is_err=False))

    ctx.barrier()
    wlast_wi = [0, 0, 0]
    wlast_wg = [0, 0, 0]
    wlast_wo = [0, 0]
    sg_use = [0, 0]
    sq_use = 0
    rlast = [0, 0]
    n_rmm_prev = 0
    n_h_prev = 0
    n_ysc_prev = 0
    n_ysc_slot = [0, 0, 0, 0]
    n_lnuse_q = [0, 0]
    n_gtrans = 0
    lnf_bufs = [ln2_sb, x1t]
    NTILES = len(SUBT)
    tiles = [(e_, ti_) for e_ in range(E) for ti_ in range(NTILES)]
    grec_q = [grec[:, 0:NSC * 16], grec[:, NSC * 16:NSC * 32]]
    gtok_q = [gtok[:, 0:NSC], gtok[:, NSC:2 * NSC]]
    ntk_q = [0, 0]
    n_grec_read_exp = {}
    n_ysc_exp_end = {}
    pend_ngb = {}

    def emit_grec(e2):
        p = e2 % 2
        GL[0].wait(GL[0], GL[0].n)
        GL[0].wait(G, nsc)
        GL[0].wait(V, n_grec_read_exp.get(e2 - 2, 0))
        ngrec = GL[0].op(lambda g, e2=e2, p=p: g.dma_start(
            out=grec_q[p].rearrange("p (s r) -> p s r", s=NSC),
            in_=scr_rec[e2 * CAP:(e2 + 1) * CAP, :].rearrange(
                "(s p) r -> p s r", p=128)))
        V.wait(GL[0], ngrec)
        V.wait(GL[1], n_ysc_exp_end.get(e2 - 2, 0))
        ntk_q[p] = V.op(lambda v, p=p: v.tensor_copy(
            gtok_q[p],
            grec_q[p].rearrange("p (s r) -> p s r", s=NSC)[:, :, 8]))

    def emit_gathers(gi2):
        e2, ti2 = tiles[gi2]
        p = e2 % 2
        q2 = gi2 % 2
        lnf2 = lnf_bufs[q2]
        GL[0].wait(GL[0], GL[0].n)
        GL[0].wait(V, ntk_q[p])
        GL[0].wait(V, n_lnuse_q[q2])
        GL[0].wait(T, n_gtrans)
        ngb = 0
        for s in range(SUBT[ti2] // 128):
            sc_ = ti2 * 4 + s
            ngb = GL[0].op(lambda g, sc_=sc_, s=s, p=p: g.indirect_dma_start(
                out=gbf[:, s * 1024:(s + 1) * 1024],
                out_offset=None,
                in_=scr_ln,
                in_offset=bass.IndirectOffsetOnAxis(
                    ap=gtok_q[p][:, sc_: sc_ + 1], axis=0)))
            ngb = GL[0].op(lambda g, sc_=sc_, s=s, p=p, lnf2=lnf2:
                           g.indirect_dma_start(
                out=lnf2[:, s * 1024:(s + 1) * 1024],
                out_offset=None,
                in_=scr_lnf,
                in_offset=bass.IndirectOffsetOnAxis(
                    ap=gtok_q[p][:, sc_: sc_ + 1], axis=0)))
        pend_ngb[gi2] = ngb

    G.wait(G, n_g_setup)
    for qq in (GWI[0], GWI[1], GWI[2], GWG[0], GWG[1], GWG[2],
               GWO[0], GWO[1]):
        qq.wait(G, n_g_setup)
    emit_grec(0)
    emit_gathers(0)
    for gi, (e, ti) in enumerate(tiles):
        wt = SUBT[ti]
        toff = ti * 512
        ntc_w = wt // 128
        q_ = gi % 2
        p_ = e % 2
        lnf = lnf_bufs[q_]
        if True:
            T.wait(GL[0], pend_ngb[gi])
            Sc.wait(T, n_rmm_prev)
            nse = 0
            for s in range(ntc_w):
                for k in range(8):
                    p0 = pr[k % 4]
                    eng, cnt = free[k % 4]
                    T.wait(trk[eng], cnt)
                    nt = T.op(lambda t, k=k, s=s, p0=p0: t.transpose(
                        p0[:].bitcast(BF16)[:, 0:128],
                        gbf[:, s * 1024 + k * 128: s * 1024 + k * 128
                            + 128],
                        ident[:]))
                    n_gtrans = nt
                    Sc.wait(T, nt)
                    nse = Sc.op(lambda s_, k=k, s=s, p0=p0: s_.copy(
                        ln2_fm[:, k * 512 + s * 128: k * 512 + s * 128
                               + 128],
                        p0[:].bitcast(BF16)[:, 0:128]))
                    free[k % 4] = ("s", nse)
            n_lnfm = nse
            if gi + 1 < len(tiles):
                if tiles[gi + 1][1] == 0:
                    emit_grec(e + 1)
                emit_gathers(gi + 1)

            Sc.wait(V, n_h_prev)
            for fc in range(8):
                slot = fc % 3
                GWI[slot].wait(T, wlast_wi[slot])
                nwld = GWI[slot].op(lambda g, e=e, fc=fc, slot=slot:
                                    g.dma_start(
                    out=wi_sb[:, slot * 4096:(slot + 1) * 4096].rearrange(
                        "p (k f) -> p k f", k=8),
                    in_=scr_wi[e].rearrange("(k p) f -> p k f", p=128)[
                        :, :, fc * 512:(fc + 1) * 512]))
                T.wait(GWI[slot], nwld)
                for fb in range(4):
                    f = fc * 4 + fb
                    pp = pr[f % 4]
                    eng, cnt = free[f % 4]
                    T.wait(trk[eng], cnt)
                    T.wait(Sc, n_lnfm)
                    for k in range(8):
                        nt = T.op(lambda t, k=k, fb=fb, slot=slot, pp=pp,
                                  wt=wt: t.matmul(
                            pp[:, 0:wt],
                            wi_sb[:, slot * 4096 + k * 512 + fb * 128:
                                  slot * 4096 + k * 512 + fb * 128 + 128],
                            ln2_fm[:, k * 512: k * 512 + wt],
                            start=(k == 0), stop=(k == 7)))
                    Sc.wait(T, nt)
                    nse = Sc.op(lambda s, f=f, e=e, pp=pp, wt=wt:
                                s.activation(
                        h_fm[:, f * 512: f * 512 + wt], pp[:, 0:wt],
                        AF.Gelu,
                        bias=bic[:, e * 32 + f: e * 32 + f + 1]))
                    free[f % 4] = ("s", nse)
                wlast_wi[slot] = nt

            V.wait(T, n_rmm_prev)
            nvh = 0
            for fc in range(8):
                slot = fc % 3
                GWG[slot].wait(T, wlast_wg[slot])
                nwld = GWG[slot].op(lambda g, e=e, fc=fc, slot=slot:
                                    g.dma_start(
                    out=wg_sb[:, slot * 4096:(slot + 1) * 4096].rearrange(
                        "p (k f) -> p k f", k=8),
                    in_=scr_wg[e].rearrange("(k p) f -> p k f", p=128)[
                        :, :, fc * 512:(fc + 1) * 512]))
                T.wait(GWG[slot], nwld)
                for fb in range(4):
                    f = fc * 4 + fb
                    fs = f % 2
                    pp = pr[f % 4]
                    eng, cnt = free[f % 4]
                    T.wait(trk[eng], cnt)
                    for k in range(8):
                        nt = T.op(lambda t, k=k, fb=fb, slot=slot, pp=pp,
                                  wt=wt: t.matmul(
                            pp[:, 0:wt],
                            wg_sb[:, slot * 4096 + k * 512 + fb * 128:
                                  slot * 4096 + k * 512 + fb * 128 + 128],
                            ln2_fm[:, k * 512: k * 512 + wt],
                            start=(k == 0), stop=(k == 7)))
                    Sc.wait(T, nt)
                    Sc.wait(V, sg_use[fs])
                    nse = Sc.op(lambda s, f=f, e=e, pp=pp, fs=fs, wt=wt:
                                s.activation(
                        sgt[:, fs * 512: fs * 512 + wt], pp[:, 0:wt],
                        AF.Sigmoid,
                        bias=bgc[:, e * 32 + f: e * 32 + f + 1]))
                    free[f % 4] = ("s", nse)
                    V.wait(Sc, nse)
                    nvh = V.op(lambda v, f=f, fs=fs, wt=wt: v.tensor_mul(
                        h_fm[:, f * 512: f * 512 + wt],
                        h_fm[:, f * 512: f * 512 + wt],
                        sgt[:, fs * 512: fs * 512 + wt]))
                    sg_use[fs] = nvh
                wlast_wg[slot] = nt
            n_h = nvh
            n_h_prev = nvh

            G.wait(T, n_rmm_prev)
            G.wait(G, G.n)
            nrow = G.op(lambda g, e=e: g.dma_start(
                out=rowb[:].rearrange("o (a h) -> o a h", a=3),
                in_=io["rows"][:, e, :].unsqueeze(0)))
            T.wait(G, nrow)
            for which, dst in ((0, gb_sb), (1, bb_sb)):
                for half in range(2):
                    bank = 6 + which
                    eng, cnt = free[bank]
                    T.wait(trk[eng], cnt)
                    nt = T.op(lambda t, which=which, half=half, bank=bank:
                              t.matmul(pr[bank][:, 0:512], ones[:],
                                       rowb[0:1, 1024 + which * 1024
                                            + half * 512: 1024
                                            + which * 1024 + half * 512
                                            + 512],
                                       start=True, stop=True))
                    V.wait(T, nt)
                    nv = V.op(lambda v, dst=dst, half=half, bank=bank:
                              v.tensor_copy(dst[:, half * 512:
                                                half * 512 + 512],
                                            pr[bank][:, 0:512]))
                    free[bank] = ("v", nv)

            T.wait(V, n_h)
            for b_ in range(8):
                eng, cnt = free[b_]
                T.wait(trk[eng], cnt)
            for fg in range(8):
                slot = fg % 2
                GWO[slot].wait(T, wlast_wo[slot])
                nwld = GWO[slot].op(lambda g, e=e, fg=fg, slot=slot:
                                    g.dma_start(
                    out=wo_sb[:, slot * 4096:(slot + 1) * 4096].rearrange(
                        "p (g o) -> p g o", g=4),
                    in_=scr_wo[e][fg * 512:(fg + 1) * 512, :].rearrange(
                        "(g p) o -> p g o", p=128)))
                T.wait(GWO[slot], nwld)
                for tc in range(ntc_w):
                    for fb in range(4):
                        f = fg * 4 + fb
                        for half in range(2):
                            nt = T.op(lambda t, f=f, tc=tc, fb=fb,
                                      half=half, slot=slot: t.matmul(
                                pr[tc * 2 + half][:, 0:512],
                                h_fm[:, f * 512 + tc * 128:
                                     f * 512 + tc * 128 + 128],
                                wo_sb[:, slot * 4096 + fb * 1024
                                      + half * 512: slot * 4096
                                      + fb * 1024 + half * 512 + 512],
                                start=(fg == 0 and fb == 0), stop=False))
                wlast_wo[slot] = nt
            for tc in range(ntc_w):
                for half in range(2):
                    nt = T.op(lambda t, tc=tc, half=half, e=e: t.matmul(
                        pr[tc * 2 + half][:, 0:512], ones[:],
                        rowb[0:1, half * 512: half * 512 + 512],
                        start=False, stop=True))
            n_rmm = nt
            n_rmm_prev = nt
            for tc in range(ntc_w):
                V.wait(T, n_rmm)
                rsl = tc % 2
                rs = r_sb[:, rsl * 1024: rsl * 1024 + 1024]
                V.wait(V, rlast[rsl])
                for half in range(2):
                    nv1 = V.op(lambda v, tc=tc, half=half, rs=rs, lnf=lnf:
                               v.tensor_add(
                        rs[:, half * 512: half * 512 + 512],
                        pr[tc * 2 + half][:, 0:512],
                        lnf[:, tc * 1024 + half * 512:
                            tc * 1024 + half * 512 + 512]))
                    free[tc * 2 + half] = ("v", nv1)
                n_lnuse_q[q_] = nv1
                V.wait(V, nv1)
                n1 = V.op(lambda v, rs=rs: v.reduce_sum(
                    cols[:, 8:9], rs, axis=AX.X, negate=True))
                V.wait(V, n1)
                n2 = V.op(lambda v: v.tensor_scalar_mul(cols[:, 8:9],
                                                        cols[:, 8:9],
                                                        1.0 / H))
                V.wait(V, n2)
                Sc.wait(V, n2)
                Sc.wait(V, sq_use)
                n4 = Sc.op(lambda s, rs=rs: s.activation(
                    sqscr[:], rs, AF.Square, bias=cols[:, 8:9],
                    accum_out=cols[:, 9:10]))
                V.wait(Sc, n4)
                n5 = V.op(lambda v: v.tensor_scalar_mul(cols[:, 10:11],
                                                        cols[:, 9:10],
                                                        1.0 / H))
                Sc.wait(V, n5)
                n6 = Sc.op(lambda s: s.activation(cols[:, 11:12],
                                                  cols[:, 10:11], AF.Ln,
                                                  bias=cols[:, 15:16]))
                Sc.wait(Sc, n6)
                n7 = Sc.op(lambda s: s.activation(cols[:, 10:11],
                                                  cols[:, 11:12], AF.Exp,
                                                  scale=-0.5))
                V.wait(Sc, n7)
                n9 = V.op(lambda v, tc=tc, e=e, p_=p_: v.tensor_mul(
                    cols[:, 12:13],
                    grec_q[p_][:, (toff // 128 + tc) * 16 + e:
                               (toff // 128 + tc) * 16 + e + 1],
                    cols[:, 10:11]))
                V.wait(V, n9)
                nta = V.op(lambda v, rs=rs: v.tensor_scalar(
                    sqscr[:], rs, cols[:, 8:9], cols[:, 12:13],
                    ALU.add, ALU.mult))
                rlast[rsl] = nta
                V.wait(V, nta)
                ntb = V.op(lambda v: v.tensor_mul(sqscr[:], sqscr[:],
                                                  gb_sb[:]))
                sq_use = ntb
                V.wait(V, ntb)
                ysl = y_out[:, tc * 1024: tc * 1024 + 1024]
                V.wait(GL[1], n_ysc_slot[tc])
                ntc2 = V.op(lambda v, ysl=ysl: v.tensor_copy(ysl, sqscr[:]))
                sq_use = ntc2
                V.wait(V, ntc2)
                ntd = V.op(lambda v, tc=tc, e=e, ysl=ysl, p_=p_:
                           v.scalar_tensor_tensor(
                    ysl, bb_sb[:],
                    grec_q[p_][:, (toff // 128 + tc) * 16 + e:
                               (toff // 128 + tc) * 16 + e + 1],
                    ysl, ALU.mult, ALU.add))
                n_grec_read_exp[e] = ntd
                GL[1].wait(V, ntd)
                GL[1].wait(Q, ctx.nacc0)
                GL[1].wait(GL[1], n_ysc_prev)
                n_ysc = GL[1].op(lambda g, tc=tc, e=e, ysl=ysl, toff=toff,
                                 p_=p_: g.indirect_dma_start(
                    out=x2_acc,
                    out_offset=bass.IndirectOffsetOnAxis(
                        ap=gtok_q[p_][:, toff // 128 + tc:
                                      toff // 128 + tc + 1], axis=0),
                    in_=ysl,
                    in_offset=None,
                    compute_op=ALU.add))
                n_ysc_prev = n_ysc
                n_ysc_slot[tc] = n_ysc
            n_ysc_exp_end[e] = n_ysc_prev
    GL[1].wait(GL[1], n_ysc_prev)
    Q.wait(GL[1], n_ysc_prev)
    ctx.barrier()
    if not fused:
        nfin = Q.op(lambda q: q.dma_start(out=io["x2"],
                                          in_=x2_acc[0:TTOK, :]))
        Q.wait(Q, nfin)


def build_phase_b_sparse(n_tt=8):
    nc = bass.Bass("TRN2", target_bir_lowering=False)
    Sx = n_tt * 512
    E, F = 8, 4096
    CAP = ((int(Sx * 2 / 8 * 3.0) + 127) // 128) * 128
    NSC = CAP // 128
    io = {
        "x1": nc.dram_tensor("x1", [Sx, H], F32, kind="ExternalInput").ap(),
        "Wi": nc.dram_tensor("Wi", [E, H, F], F32, kind="ExternalInput").ap(),
        "Wg": nc.dram_tensor("Wg", [E, H, F], F32, kind="ExternalInput").ap(),
        "Wo": nc.dram_tensor("Wo", [E, F, H], F32, kind="ExternalInput").ap(),
        "Wr": nc.dram_tensor("Wr", [H, E], F32, kind="ExternalInput").ap(),
        "bic": nc.dram_tensor("bic", [128, 256], F32, kind="ExternalInput").ap(),
        "bgc": nc.dram_tensor("bgc", [128, 256], F32, kind="ExternalInput").ap(),
        "rows": nc.dram_tensor("rows", [3, E, 1024], F32,
                               kind="ExternalInput").ap(),
        "br": nc.dram_tensor("br", [1, 8], F32, kind="ExternalInput").ap(),
        "ident": nc.dram_tensor("ident", [128, 128], BF16,
                                kind="ExternalInput").ap(),
        "idf32": nc.dram_tensor("idf32", [128, 128], F32,
                                kind="ExternalInput").ap(),
        "ones1": nc.dram_tensor("ones1", [1, 128], BF16,
                                kind="ExternalInput").ap(),
        "iota_c": nc.dram_tensor("iota_c", [128, n_tt * 4], F32,
                                 kind="ExternalInput").ap(),
        "ecolv": nc.dram_tensor("ecolv", [8, 1], F32,
                                kind="ExternalInput").ap(),
        "x2": nc.dram_tensor("x2", [Sx, H], F32, kind="ExternalOutput").ap(),
        "scr_wi": nc.dram_tensor("scr_wi", [E, H, F], BF16, kind="Internal").ap(),
        "scr_wg": nc.dram_tensor("scr_wg", [E, H, F], BF16, kind="Internal").ap(),
        "scr_wo": nc.dram_tensor("scr_wo", [E, F, H], BF16, kind="Internal").ap(),
        "scr_ln": nc.dram_tensor("scr_ln", [Sx + 128, H], BF16,
                                 kind="Internal").ap(),
        "scr_lnf": nc.dram_tensor("scr_lnf", [Sx + 128, H], F32,
                                  kind="Internal").ap(),
        "scr_rec": nc.dram_tensor("scr_rec", [E * CAP + 128, 16], F32,
                                  kind="Internal").ap(),
        "scr_accx": nc.dram_tensor("scr_accx", [Sx + 128, H], F32,
                                   kind="Internal").ap(),
        "zrecv": nc.dram_tensor("zrecv", [128, 16], F32,
                                kind="ExternalInput").ap(),
        "dumpc": nc.dram_tensor("dumpc", [128, 1], F32,
                                kind="ExternalInput").ap(),
    }
    es = ExitStack()
    with es:
        sb = lambda name, w, dt=F32, p=128: es.enter_context(
            nc.sbuf_tensor(name, [p, w], dt))
        buf = {
            "x1t": sb("x1t", 4096),
            "ln2_sb": sb("ln2_sb", 4096),
            "ln2_fm": sb("ln2_fm", 4096, BF16),
            "lnbf": sb("lnbf", 1024, BF16),
            "h_fm": sb("h_fm", 16384, BF16),
            "sgt": sb("sgt", 1024, BF16),
            "wi_sb": sb("wi_sb", 12288, BF16),
            "wg_sb": sb("wg_sb", 12288, BF16),
            "wo_sb": sb("wo_sb", 8192, BF16),
            "acc": sb("accb", 16),
            "r_sb": sb("r_sb", 2048),
            "gb_sb": sb("gb_sb", 1024),
            "bb_sb": sb("bb_sb", 1024),
            "ctile": sb("ctile", 64),
            "cols": sb("cols", 16),
            "sqscr": sb("sqscr", 1024),
            "bic": sb("bicsb", 256),
            "bgc": sb("bgcsb", 256),
            "rowb": sb("rowb", 3072, BF16, 1),
            "brf": sb("brf", 8, F32, 1),
            "brb": sb("brb", 8, BF16, 1),
            "wr_bf": sb("wr_bfsb", 64, BF16),
            "ident": sb("identsb", 128, BF16),
            "idf32": sb("idf32sb", 128),
            "ones": sb("onessb", 128, BF16, 1),
            "rec_sb": sb("rec_sb", n_tt * 64),
            "islots": sb("islots", n_tt * 32, mybir.dt.int32),
            "iota_c": sb("iota_csb", n_tt * 4),
            "onecol": sb("onecol", 1, F32, 8),
            "ecol": sb("ecolsb", 1, F32, 8),
            "grec": sb("grec", NSC * 32),
            "gtok": sb("gtok", NSC * 2, mybir.dt.int32),
            "zrec": sb("zrec", 16),
            "y_out": sb("y_out", 4096),
            "gln": sb("gln", 4096, BF16),
            "dumpc": sb("dumpcsb", 1),
            "dmsk": sb("dmsk", 8),
            "dslf": sb("dslf", 8),
            "pr": [es.enter_context(nc.psum_tensor(f"pr{i}", [128, 512], F32))
                   for i in range(8)],
        }
        ctx = Ctx(nc, es)
        ctx.GWI = (ctx.queue("gwi0", engine="q"), ctx.queue("gwi1", engine="q"),
                   ctx.queue("gwi2", engine="q"))
        ctx.GWG = (ctx.queue("gwg0", engine="q"), ctx.queue("gwg1", engine="q"),
                   ctx.queue("gwg2", engine="q"))
        ctx.GWO = (ctx.queue("gwo0", engine="q"), ctx.queue("gwo1", engine="q"))
        ctx.GL = (ctx.queue("gl0"), ctx.queue("gl1"))
        blk = es.enter_context(nc.Block())
        ctx.run_passes(blk, lambda c: emit_phase_b_sparse(c, io, buf, n_tt))
    return nc


def host_prep_b_sparse(inputs, n_tt=8):
    d = host_prep_b(inputs)
    Sx = n_tt * 512
    CAP = ((int(Sx * 2 / 8 * 3.0) + 127) // 128) * 128
    d["idf32"] = np.eye(128, dtype=np.float32)
    d["iota_c"] = np.ascontiguousarray(
        (np.arange(n_tt * 4)[None, :] * 128
         + np.arange(128)[:, None]).astype(np.float32))
    d["ecolv"] = (np.arange(8, dtype=np.float32) * CAP).reshape(8, 1)
    z = np.zeros((128, 16), np.float32)
    z[:, 8] = Sx + np.arange(128)
    d["zrecv"] = z
    d["dumpc"] = (8 * CAP + np.arange(128, dtype=np.float32)).reshape(128, 1)
    return d



def emit_phase_c(ctx, io, buf, n_ch=8, fused=False):
    T, V, Sc, Q, G, P = ctx.T, ctx.V, ctx.S, ctx.Q, ctx.G, ctx.P
    G0, G1 = ctx.GW
    GW = [G0, G1]
    trk = {"v": V, "s": Sc, "t": T, "p": P}

    x2_in, x3_out = io["x2"], io["x3"]
    scr_win = io["scr_win"]

    (x2t, lnbf, ln3_fm, win_sb, wout_sb, u_sb, zs_bf, yg_bf, upad, sigma,
     state, a_sb, dcol, wcv, cols, sqscr, x3t, ident) = (buf[k] for k in (
        "x2t", "lnbf", "ln3_fm", "win_sb", "wout_sb", "u_sb", "zs_bf",
        "yg_bf", "upad", "sigma", "state", "a_sb", "dcol", "wcv", "cols",
        "sqscr", "x3t", "ident"))
    pr = buf["pr"]
    free = {i: ("v", 0) for i in range(8)}

    if not fused:
        G.op(lambda g: g.dma_start(out=scr_win, in_=io["W_in"]))
    nc1 = Q.op(lambda q: q.dma_start(out=ident[:], in_=io["ident"]))
    nc1 = Q.op(lambda q: q.dma_start(
        out=a_sb[:].rearrange("p (c s) -> p c s", c=16),
        in_=io["A_logT"].rearrange("(c p) s -> p c s", p=128)))
    nc1 = Q.op(lambda q: q.dma_start(out=dcol[:], in_=io["D_param"]))
    nc1 = Q.op(lambda q: q.dma_start(out=wcv[:], in_=io["conv_w"]))
    nc2 = Q.op(lambda q: q.dma_start(out=buf["cbc"][:], in_=io["conv_b"]))
    nG = G.op(lambda g: g.dma_start(
        out=wout_sb[:].rearrange("p (k o) -> p k o", k=16),
        in_=io["W_out"].rearrange("(k p) o -> p k o", p=128)))
    n_g_setup = nG
    Sc.wait(Q, nc2)
    na1 = Sc.op(lambda s: s.activation(a_sb[:], a_sb[:], AF.Exp))
    V.wait(Sc, na1)
    na2 = V.op(lambda v: v.tensor_scalar_mul(a_sb[:], a_sb[:], -1.0))
    neps = V.op(lambda v: v.memset(cols[:, 14:15], 1e-5))
    nst0 = V.op(lambda v: v.memset(state[:], 0.0))
    nut0 = V.op(lambda v: v.memset(buf["utail"][:], 0.0))
    Sc.wait(V, neps)
    T.wait(Q, nc2)
    T.wait(G, nG)
    cbc = buf["cbc"]
    utail = buf["utail"]

    n_state_w = {}
    n_u_c = [0] * 16
    n_z_c = [0] * 16
    nacc_c = [0] * 16
    n_conv_q = [0, 0]
    last_store = 0
    wlast = [0, 0]
    nsig_used = [0, 0, 0, 0]
    nsig_ext = [0, 0, 0, 0]
    nacc_p = 0
    n_conv_prev = 0
    n_gate_c = [0] * 16
    n_oproj_prev = 0
    n_oproj_mm_prev = 0
    n_xz_mm_prev = 0
    for ch in range(n_ch):
        Q.wait(V, n_oproj_prev)
        Q.wait(Q, Q.n)
        nload = Q.op(lambda q, ch=ch: q.dma_start(
            out=x2t[:].rearrange("p (tc h) -> p tc h", tc=4),
            in_=x2_in[ch * 512:(ch + 1) * 512, :].rearrange(
                "(tc p) h -> p tc h", p=128)))
        V.wait(Q, nload)
        V.wait(V, V.n)
        Sc.wait(T, n_xz_mm_prev)
        V.wait(T, n_oproj_mm_prev)

        nse = 0
        for tc in range(4):
            xi = x2t[:, tc * 1024:(tc + 1) * 1024]
            n1 = V.op(lambda v, xi=xi: v.reduce_sum(
                cols[:, 0:1], xi, axis=AX.X, negate=True))
            V.wait(V, n1)
            n2 = V.op(lambda v: v.tensor_scalar_mul(cols[:, 0:1],
                                                    cols[:, 0:1], 1.0 / H))
            V.wait(V, n2)
            n3 = V.op(lambda v, xi=xi: v.tensor_scalar_add(
                sqscr[:], xi, cols[:, 0:1]))
            Sc.wait(V, n3)
            Sc.wait(Q, last_store)
            n4 = Sc.op(lambda s: s.activation(
                x3t[:, 0:1024], sqscr[:], AF.Square, accum_out=cols[:, 1:2]))
            V.wait(Sc, n4)
            n5 = V.op(lambda v: v.tensor_scalar_mul(cols[:, 2:3],
                                                    cols[:, 1:2], 1.0 / H))
            Sc.wait(V, n5)
            n6 = Sc.op(lambda s: s.activation(cols[:, 3:4], cols[:, 2:3],
                                              AF.Ln, bias=cols[:, 14:15]))
            Sc.wait(Sc, n6)
            n7 = Sc.op(lambda s: s.activation(cols[:, 2:3], cols[:, 3:4],
                                              AF.Exp, scale=-0.5))
            V.wait(Sc, n7)
            n8 = V.op(lambda v: v.tensor_scalar_mul(lnbf[:], sqscr[:],
                                                    cols[:, 2:3]))
            T.wait(V, n8)
            for k in range(8):
                p0 = pr[k % 2]
                eng, cnt = free[k % 2]
                T.wait(trk[eng], cnt)
                nt = T.op(lambda t, k=k, p0=p0: t.transpose(
                    p0[:].bitcast(BF16)[:, 0:128], lnbf[:, ts(k)], ident[:]))
                Sc.wait(T, nt)
                nse = Sc.op(lambda s, k=k, tc=tc, p0=p0: s.copy(
                    ln3_fm[:, k * 512 + tc * 128: k * 512 + tc * 128 + 128],
                    p0[:].bitcast(BF16)[:, 0:128]))
                free[k % 2] = ("s", nse)
        n_lnfm = nse

        G.wait(G, n_g_setup)
        GW[0].wait(G, n_g_setup)
        GW[1].wait(G, n_g_setup)
        for mg in range(8):
            slot = mg % 2
            GW[slot].wait(T, wlast[slot])
            nwld = GW[slot].op(lambda g, mg=mg, slot=slot: g.dma_start(
                out=win_sb[:, slot * 4096:(slot + 1) * 4096].rearrange(
                    "p (k f) -> p k f", k=8),
                in_=scr_win.rearrange("(k p) f -> p k f", p=128)[
                    :, :, mg * 512:(mg + 1) * 512]))
            T.wait(GW[slot], nwld)
            for mb in range(4):
                m = mg * 4 + mb
                pp = pr[2 + m % 2]
                eng, cnt = free[2 + m % 2]
                T.wait(trk[eng], cnt)
                T.wait(Sc, n_lnfm)
                for k in range(8):
                    nt = T.op(lambda t, k=k, mb=mb, slot=slot, pp=pp:
                              t.matmul(
                        pp[:, 0:512],
                        win_sb[:, slot * 4096 + k * 512 + mb * 128:
                               slot * 4096 + k * 512 + mb * 128 + 128],
                        ln3_fm[:, ts(k, 512)],
                        start=(k == 0), stop=(k == 7)))
                if m < 16:
                    Sc.wait(T, nt)
                    Sc.wait(V, n_gate_c[m])
                    nse = Sc.op(lambda s, m=m, pp=pp: s.copy(
                        u_sb[:, ts(m, 512)], pp[:, 0:512]))
                    n_u_c[m] = nse
                    free[2 + m % 2] = ("s", nse)
                else:
                    Sc.wait(T, nt)
                    Sc.wait(V, n_gate_c[m - 16])
                    nse = Sc.op(lambda s, m=m, pp=pp: s.activation(
                        zs_bf[:, ts(m - 16, 512)], pp[:, 0:512], AF.Silu))
                    n_z_c[m - 16] = nse
                    free[2 + m % 2] = ("s", nse)
            wlast[slot] = nt
        n_xz = nse
        n_xz_mm_prev = nt

        for c in range(16):
            up = upad[:, (c % 2) * 515:(c % 2) * 515 + 515]
            P.wait(Sc, n_u_c[c])
            P.wait(V, n_conv_q[c % 2])
            P.wait(P, P.n)
            pc1 = P.op(lambda p, c=c, up=up: p.tensor_copy(
                up[:, 0:3], utail[:, c * 4: c * 4 + 3]))
            pc2 = P.op(lambda p, c=c, up=up: p.tensor_copy(
                up[:, 3:515], u_sb[:, ts(c, 512)]))
            P.wait(P, pc2)
            nutw = P.op(lambda p, c=c, up=up: p.tensor_copy(
                utail[:, c * 4: c * 4 + 3], up[:, 512:515]))
            V.wait(P, pc2)
            m1 = V.op(lambda v, c=c, up=up: v.tensor_scalar(
                u_sb[:, ts(c, 512)], up[:, 0:512],
                wcv[:, c * 4: c * 4 + 1], cbc[:, c: c + 1],
                ALU.mult, ALU.add))
            V.wait(V, m1)
            for kk in range(1, 4):
                mk_ = V.op(lambda v, c=c, kk=kk, up=up:
                           v.scalar_tensor_tensor(
                    u_sb[:, ts(c, 512)], up[:, kk: kk + 512],
                    wcv[:, c * 4 + kk: c * 4 + kk + 1],
                    u_sb[:, ts(c, 512)], ALU.mult, ALU.add))
                V.wait(V, mk_)
            n_conv = mk_
            n_conv_q[c % 2] = mk_
            for s_ in range(16):
                sl = s_ % 4
                ab = a_sb[:, c * 16 + s_: c * 16 + s_ + 1]
                V.wait(P, nsig_used[sl])
                V.wait(Sc, nsig_ext[sl])
                V.wait(Sc, n_state_w.get((c, s_), 0))
                nscan = V.op(lambda v, c=c, s_=s_, sl=sl, ab=ab:
                             v.tensor_tensor_scan(
                    sigma[:, sl * 512:(sl + 1) * 512],
                    ab.broadcast_to([128, 512]), u_sb[:, ts(c, 512)],
                    state[:, c * 16 + s_: c * 16 + s_ + 1],
                    ALU.mult, ALU.add))

                Sc.wait(V, nscan)
                next_ = Sc.op(lambda s, c=c, s_=s_, sl=sl: s.copy(
                    state[:, c * 16 + s_: c * 16 + s_ + 1],
                    sigma[:, sl * 512 + 511: sl * 512 + 512]))
                n_state_w[(c, s_)] = next_
                nsig_ext[sl] = next_
                P.wait(V, nscan)
                if s_ == 0:
                    P.wait(V, n_gate_c[c])
                    P.wait(P, nacc_p)
                    nacc_p = P.op(lambda p, c=c, sl=sl: p.tensor_copy(
                        buf["y_acc"][:, ts(c, 512)],
                        sigma[:, sl * 512:(sl + 1) * 512]))
                else:
                    P.wait(P, nacc_p)
                    nacc_p = P.op(lambda p, c=c, sl=sl: p.tensor_add(
                        buf["y_acc"][:, ts(c, 512)],
                        buf["y_acc"][:, ts(c, 512)],
                        sigma[:, sl * 512:(sl + 1) * 512]))
                nsig_used[sl] = nacc_p
            nacc_c[c] = nacc_p
        for c in range(16):
            V.wait(P, nacc_c[c])
            V.wait(Sc, n_z_c[c])
            ng1 = V.op(lambda v, c=c: v.scalar_tensor_tensor(
                buf["y_acc"][:, ts(c, 512)], u_sb[:, ts(c, 512)],
                dcol[:, c: c + 1], buf["y_acc"][:, ts(c, 512)],
                ALU.mult, ALU.add))
            V.wait(V, ng1)
            ng2 = V.op(lambda v, c=c: v.tensor_mul(
                yg_bf[:, ts(c, 512)], buf["y_acc"][:, ts(c, 512)],
                zs_bf[:, ts(c, 512)]))
            n_gate_c[c] = ng2
        n_gate = ng2

        T.wait(V, n_gate)
        for tc in range(4):
            for half in range(2):
                bank = 4 + (tc * 2 + half) % 4
                pp = pr[bank]
                eng, cnt = free[bank]
                T.wait(trk[eng], cnt)
                for k in range(16):
                    nt = T.op(lambda t, k=k, tc=tc, half=half, pp=pp:
                              t.matmul(
                        pp[:, 0:512],
                        yg_bf[:, k * 512 + tc * 128: k * 512 + tc * 128 + 128],
                        wout_sb[:, k * 1024 + half * 512:
                                k * 1024 + half * 512 + 512],
                        start=(k == 0), stop=(k == 15)))
                V.wait(T, nt)
                V.wait(Q, last_store)
                nv = V.op(lambda v, tc=tc, half=half, pp=pp: v.tensor_add(
                    x3t[:, tc * 1024 + half * 512:
                        tc * 1024 + half * 512 + 512],
                    pp[:, 0:512],
                    x2t[:, tc * 1024 + half * 512:
                        tc * 1024 + half * 512 + 512]))
                free[bank] = ("v", nv)
        n_oproj_prev = nv
        n_oproj_mm_prev = nt
        Q.wait(V, nv)
        last_store = Q.op(lambda q, ch=ch: q.dma_start(
            out=x3_out[ch * 512:(ch + 1) * 512, :].rearrange(
                "(tc p) h -> p tc h", p=128),
            in_=x3t[:].rearrange("p (tc h) -> p tc h", tc=4)))

    Q.wait(Q, last_store)


def build_phase_c(n_ch=8):
    nc = bass.Bass("TRN2", target_bir_lowering=False)
    Sx = n_ch * 512
    io = {
        "x2": nc.dram_tensor("x2", [Sx, H], F32, kind="ExternalInput").ap(),
        "W_in": nc.dram_tensor("W_in", [H, 4096], F32, kind="ExternalInput").ap(),
        "W_out": nc.dram_tensor("W_out", [2048, H], F32, kind="ExternalInput").ap(),
        "A_logT": nc.dram_tensor("A_logT", [2048, 16], F32, kind="ExternalInput").ap(),
        "D_param": nc.dram_tensor("D_param", [128, 16], F32, kind="ExternalInput").ap(),
        "conv_w": nc.dram_tensor("conv_w", [128, 64], F32, kind="ExternalInput").ap(),
        "conv_b": nc.dram_tensor("conv_b", [128, 16], F32, kind="ExternalInput").ap(),
        "ident": nc.dram_tensor("ident", [128, 128], BF16, kind="ExternalInput").ap(),
        "x3": nc.dram_tensor("x3", [Sx, H], F32, kind="ExternalOutput").ap(),
        "scr_win": nc.dram_tensor("scr_win", [H, 4096], BF16, kind="Internal").ap(),
    }
    es = ExitStack()
    with es:
        sb = lambda name, w, dt=F32, p=128: es.enter_context(
            nc.sbuf_tensor(name, [p, w], dt))
        buf = {
            "x2t": sb("x2t", 4096),
            "lnbf": sb("lnbf", 1024, BF16),
            "ln3_fm": sb("ln3_fm", 4096, BF16),
            "win_sb": sb("win_sb", 8192, BF16),
            "wout_sb": sb("wout_sb", 16384, BF16),
            "u_sb": sb("u_sb", 8192),
            "zs_bf": sb("zs_bf", 8192, BF16),
            "yg_bf": sb("yg_bf", 8192, BF16),
            "y_acc": sb("y_acc", 8192),
            "upad": sb("upad", 1030),
            "sigma": sb("sigma", 2048),
            "state": sb("state", 256),
            "utail": sb("utail", 64),
            "a_sb": sb("a_sb", 256),
            "dcol": sb("dcol", 16),
            "wcv": sb("wcv", 64),
            "cbc": sb("cbc", 16),
            "cols": sb("colsc", 16),
            "sqscr": sb("sqscrc", 1024),
            "x3t": sb("x3t", 4096),
            "ident": sb("identsb", 128, BF16),
            "pr": [es.enter_context(nc.psum_tensor(f"pr{i}", [128, 512], F32))
                   for i in range(8)],
        }
        ctx = Ctx(nc, es)
        ctx.GW = (ctx.queue("gw0", engine="q"), ctx.queue("gw1", engine="q"))
        blk = es.enter_context(nc.Block())
        ctx.run_passes(blk, lambda c: emit_phase_c(c, io, buf, n_ch))
    return nc


def host_prep_c(inputs):
    return {
        "W_in": np.ascontiguousarray(inputs["L1_attn_W_in"], np.float32),
        "W_out": np.ascontiguousarray(inputs["L1_attn_W_out"], np.float32),
        "A_logT": np.ascontiguousarray(
            np.asarray(inputs["L1_attn_A_log"], np.float32).T),
        "D_param": np.ascontiguousarray(
            np.asarray(inputs["L1_attn_D_param"], np.float32).reshape(16, 128).T),
        "conv_w": np.ascontiguousarray(
            np.asarray(inputs["L1_attn_conv_w"], np.float32).reshape(
                16, 128, 4).transpose(1, 0, 2).reshape(128, 64)),
        "conv_b": np.ascontiguousarray(
            np.asarray(inputs["L1_attn_conv_b"], np.float32).reshape(16, 128).T),
        "ident": np.eye(128, dtype=ml_dtypes.bfloat16),
    }




def emit_phase_d(ctx, io, buf, n_tt=8, fused=False):
    T, V, Sc, Q, G = ctx.T, ctx.V, ctx.S, ctx.Q, ctx.G
    GW1, GW2 = ctx.GW1, ctx.GW2
    trk = {"v": V, "s": Sc, "t": T}
    E, D = 8, 2048

    x3_in, x4_out = io["x3"], io["x4"]
    scr_w1, scr_w2 = io["scr_w1"], io["scr_w2"]

    (x3t, lnbf, ln4_fm, w_sb, h_fm, acc, ctile, cols, sqscr, b1c, rowb,
     brf, brb, wr_bf, ident, ones) = (buf[k] for k in (
        "x3t", "lnbf", "ln4_fm", "w_sb", "h_fm", "acc", "ctile", "cols",
        "sqscr", "b1c", "rowb", "brf", "brb", "wr_bf", "ident",
        "ones"))
    w2_sb = buf["w2_sb"]
    pr = buf["pr"]
    free = {i: ("v", 0) for i in range(8)}

    if not fused:
        for e in range(E):
            G.op(lambda g, e=e: g.dma_start(out=scr_w1[e], in_=io["W1"][e]))
            G.op(lambda g, e=e: g.dma_start(out=scr_w2[e], in_=io["W2"][e]))

    nc1 = Q.op(lambda q: q.dma_start(out=ident[:], in_=io["ident"]))
    nc1 = Q.op(lambda q: q.dma_start(out=ones[:], in_=io["ones1"]))
    nc1 = Q.op(lambda q: q.dma_start(out=b1c[:], in_=io["b1c"]))
    nc1 = Q.op(lambda q: q.dma_start(out=brf[:], in_=io["br2"]))
    nG = G.op(lambda g: g.dma_start(
        out=wr_bf[:].rearrange("p (k e) -> p k e", k=8),
        in_=io["Wr2"].rearrange("(k p) e -> p k e", p=128)))
    n_g_setup = nG
    V.wait(Q, nc1)
    nbr = V.op(lambda v: v.tensor_copy(brb[:], brf[:]))
    neps = V.op(lambda v: v.memset(cols[:, 14:15], 1e-5))
    Sc.wait(V, neps)
    T.wait(Q, nc1)
    T.wait(G, nG)
    T.wait(V, nbr)

    last_store = 0
    for tt in range(n_tt):
        ctx.barrier()
        nload = Q.op(lambda q, tt=tt: q.dma_start(
            out=x3t[:].rearrange("p (tc h) -> p tc h", tc=4),
            in_=x3_in[tt * 512:(tt + 1) * 512, :].rearrange(
                "(tc p) h -> p tc h", p=128)))
        V.wait(Q, nload)

        nse = 0
        for tc in range(4):
            xi = x3t[:, tc * 1024:(tc + 1) * 1024]
            n1 = V.op(lambda v, xi=xi: v.reduce_sum(
                cols[:, 0:1], xi, axis=AX.X, negate=True))
            V.wait(V, n1)
            n2 = V.op(lambda v: v.tensor_scalar_mul(cols[:, 0:1],
                                                    cols[:, 0:1], 1.0 / H))
            V.wait(V, n2)
            n3 = V.op(lambda v, xi=xi: v.tensor_scalar_add(
                sqscr[:], xi, cols[:, 0:1]))
            Sc.wait(V, n3)
            n4 = Sc.op(lambda s: s.activation(
                buf["sq2"][:], sqscr[:], AF.Square, accum_out=cols[:, 1:2]))
            V.wait(Sc, n4)
            n5 = V.op(lambda v: v.tensor_scalar_mul(cols[:, 2:3],
                                                    cols[:, 1:2], 1.0 / H))
            Sc.wait(V, n5)
            n6 = Sc.op(lambda s: s.activation(cols[:, 3:4], cols[:, 2:3],
                                              AF.Ln, bias=cols[:, 14:15]))
            Sc.wait(Sc, n6)
            n7 = Sc.op(lambda s: s.activation(cols[:, 2:3], cols[:, 3:4],
                                              AF.Exp, scale=-0.5))
            V.wait(Sc, n7)
            n8 = V.op(lambda v: v.tensor_scalar_mul(lnbf[:], sqscr[:],
                                                    cols[:, 2:3]))
            T.wait(V, n8)
            for k in range(8):
                p0 = pr[k % 2]
                eng, cnt = free[k % 2]
                T.wait(trk[eng], cnt)
                nt = T.op(lambda t, k=k, p0=p0: t.transpose(
                    p0[:].bitcast(BF16)[:, 0:128], lnbf[:, ts(k)], ident[:]))
                Sc.wait(T, nt)
                nse = Sc.op(lambda s, k=k, tc=tc, p0=p0: s.copy(
                    ln4_fm[:, k * 512 + tc * 128: k * 512 + tc * 128 + 128],
                    p0[:].bitcast(BF16)[:, 0:128]))
                free[k % 2] = ("s", nse)
        n_lnfm = nse

        prev_nC = 0
        for tc in range(4):
            eng, cnt = free[4]
            T.wait(trk[eng], cnt)
            T.wait(Sc, n_lnfm)
            for k in range(8):
                nt = T.op(lambda t, k=k, tc=tc: t.matmul(
                    pr[4][:, 0:8],
                    ln4_fm[:, k * 512 + tc * 128: k * 512 + tc * 128 + 128],
                    wr_bf[:, k * 8: k * 8 + 8], start=(k == 0), stop=False))
            nt = T.op(lambda t: t.matmul(
                pr[4][:, 0:8], ones[:], brb[:], start=False, stop=True))
            V.wait(T, nt)
            V.wait(Sc, prev_nC)
            nA = V.op(lambda v: v.reduce_max(cols[:, 4:5], pr[4][:, 0:8],
                                             axis=AX.X, negate=True))
            V.wait(V, nA)
            nB = V.op(lambda v: v.tensor_scalar_add(ctile[:, 48:56],
                                                    pr[4][:, 0:8],
                                                    cols[:, 4:5]))
            free[4] = ("v", nB)
            Sc.wait(V, nB)
            nC = Sc.op(lambda s: s.activation(ctile[:, 40:48], ctile[:, 48:56],
                                              AF.Exp))
            prev_nC = nC
            V.wait(Sc, nC)
            nS = V.op(lambda v: v.reduce_sum(cols[:, 7:8], ctile[:, 40:48],
                                             axis=AX.X))
            V.wait(V, nS)
            nJ = V.op(lambda v: v.reciprocal(cols[:, 7:8], cols[:, 7:8]))
            nD = V.op(lambda v: v.reduce_max(cols[:, 5:6], ctile[:, 40:48],
                                             axis=AX.X))
            V.wait(V, nD)
            nE = V.op(lambda v: v.tensor_single_scalar(
                ctile[:, 32:40], ctile[:, 40:48], cols[:, 5:6], ALU.is_equal))
            V.wait(V, nE)
            nF = V.op(lambda v: v.scalar_tensor_tensor(
                ctile[:, 32:40], ctile[:, 32:40], NEG, ctile[:, 40:48],
                ALU.mult, ALU.add))
            V.wait(V, nF)
            nGm = V.op(lambda v: v.reduce_max(cols[:, 6:7], ctile[:, 32:40],
                                              axis=AX.X))
            V.wait(V, nGm)
            nH = V.op(lambda v: v.tensor_single_scalar(
                ctile[:, 32:40], ctile[:, 40:48], cols[:, 6:7], ALU.is_ge))
            V.wait(V, nH)
            nK = V.op(lambda v: v.tensor_mul(ctile[:, 32:40], ctile[:, 32:40],
                                             ctile[:, 40:48]))
            V.wait(V, nK)
            nL = V.op(lambda v, tc=tc: v.tensor_scalar_mul(
                ctile[:, tc * 8:(tc + 1) * 8], ctile[:, 32:40], cols[:, 7:8]))
            nM = V.op(lambda v, tc=tc: v.tensor_copy(
                acc[:, tc * 1024:(tc + 1) * 1024],
                x3t[:, tc * 1024:(tc + 1) * 1024]))
        n_accinit = nM

        wlast_w1 = [0, 0]
        wlast_w2 = [0, 0]
        n_ymm_prev = 0
        for e in range(E):
            G.wait(G, n_g_setup)
            G.wait(T, n_ymm_prev)
            G.wait(G, G.n)
            nrow = G.op(lambda g, e=e: g.dma_start(
                out=rowb[:], in_=io["rows2"][e: e + 1, :]))
            T.wait(G, nrow)

            Sc.wait(T, n_ymm_prev)
            for fc in range(4):
                slot = fc % 2
                GW1[slot].wait(T, wlast_w1[slot])
                nwld = GW1[slot].op(lambda g, e=e, fc=fc, slot=slot:
                                    g.dma_start(
                    out=w_sb[:, slot * 4096:(slot + 1) * 4096].rearrange(
                        "p (k f) -> p k f", k=8),
                    in_=scr_w1[e].rearrange("(k p) f -> p k f", p=128)[
                        :, :, fc * 512:(fc + 1) * 512]))
                T.wait(GW1[slot], nwld)
                for fb in range(4):
                    f = fc * 4 + fb
                    pp = pr[f % 4]
                    eng, cnt = free[f % 4]
                    T.wait(trk[eng], cnt)
                    for k in range(8):
                        nt = T.op(lambda t, k=k, fb=fb, slot=slot, pp=pp:
                                  t.matmul(
                            pp[:, 0:512],
                            w_sb[:, slot * 4096 + k * 512 + fb * 128:
                                 slot * 4096 + k * 512 + fb * 128 + 128],
                            ln4_fm[:, ts(k, 512)],
                            start=(k == 0), stop=(k == 7)))
                    Sc.wait(T, nt)
                    nse = Sc.op(lambda s, f=f, e=e, pp=pp: s.activation(
                        h_fm[:, ts(f, 512)], pp[:, 0:512], AF.Relu,
                        bias=b1c[:, e * 16 + f: e * 16 + f + 1]))
                    free[f % 4] = ("s", nse)
                wlast_w1[slot] = nt
            n_h = nse

            T.wait(Sc, n_h)
            for b_ in range(8):
                eng, cnt = free[b_]
                T.wait(trk[eng], cnt)
            for dg in range(4):
                slot = dg % 2
                GW2[slot].wait(T, wlast_w2[slot])
                nwld = GW2[slot].op(lambda g, e=e, dg=dg, slot=slot:
                                    g.dma_start(
                    out=w2_sb[:, slot * 4096:(slot + 1) * 4096].rearrange(
                        "p (g o) -> p g o", g=4),
                    in_=scr_w2[e][dg * 512:(dg + 1) * 512, :].rearrange(
                        "(g p) o -> p g o", p=128)))
                T.wait(GW2[slot], nwld)
                for tc in range(4):
                    for fb in range(4):
                        f = dg * 4 + fb
                        for half in range(2):
                            nt = T.op(lambda t, f=f, tc=tc, fb=fb, half=half,
                                      slot=slot: t.matmul(
                                pr[tc * 2 + half][:, 0:512],
                                h_fm[:, f * 512 + tc * 128:
                                     f * 512 + tc * 128 + 128],
                                w2_sb[:, slot * 4096 + fb * 1024 + half * 512:
                                      slot * 4096 + fb * 1024 + half * 512
                                      + 512],
                                start=(dg == 0 and fb == 0), stop=False))
                wlast_w2[slot] = nt
            for tc in range(4):
                for half in range(2):
                    nt = T.op(lambda t, tc=tc, half=half: t.matmul(
                        pr[tc * 2 + half][:, 0:512], ones[:],
                        rowb[0:1, half * 512: half * 512 + 512],
                        start=False, stop=True))
            n_ymm = nt
            n_ymm_prev = nt
            V.wait(V, n_accinit)
            for tc in range(4):
                for half in range(2):
                    V.wait(T, n_ymm)
                    nv = V.op(lambda v, tc=tc, half=half, e=e:
                              v.scalar_tensor_tensor(
                        acc[:, tc * 1024 + half * 512:
                            tc * 1024 + half * 512 + 512],
                        pr[tc * 2 + half][:, 0:512],
                        ctile[:, tc * 8 + e: tc * 8 + e + 1],
                        acc[:, tc * 1024 + half * 512:
                            tc * 1024 + half * 512 + 512],
                        ALU.mult, ALU.add))
                    free[tc * 2 + half] = ("v", nv)

        ctx.barrier()
        last_store = Q.op(lambda q, tt=tt: q.dma_start(
            out=x4_out[tt * 512:(tt + 1) * 512, :].rearrange(
                "(tc p) h -> p tc h", p=128),
            in_=acc[:].rearrange("p (tc h) -> p tc h", tc=4)))

    Q.wait(Q, last_store)


def build_phase_d(n_tt=8):
    nc = bass.Bass("TRN2", target_bir_lowering=False)
    Sx = n_tt * 512
    E, D = 8, 2048
    io = {
        "x3": nc.dram_tensor("x3", [Sx, H], F32, kind="ExternalInput").ap(),
        "W1": nc.dram_tensor("W1", [E, H, D], F32, kind="ExternalInput").ap(),
        "W2": nc.dram_tensor("W2", [E, D, H], F32, kind="ExternalInput").ap(),
        "Wr2": nc.dram_tensor("Wr2", [H, E], F32, kind="ExternalInput").ap(),
        "b1c": nc.dram_tensor("b1c", [128, 128], F32, kind="ExternalInput").ap(),
        "rows2": nc.dram_tensor("rows2", [E, 1024], F32, kind="ExternalInput").ap(),
        "br2": nc.dram_tensor("br2", [1, 8], F32, kind="ExternalInput").ap(),
        "ident": nc.dram_tensor("ident", [128, 128], BF16, kind="ExternalInput").ap(),
        "ones1": nc.dram_tensor("ones1", [1, 128], BF16, kind="ExternalInput").ap(),
        "x4": nc.dram_tensor("x4", [Sx, H], F32, kind="ExternalOutput").ap(),
        "scr_w1": nc.dram_tensor("scr_w1", [E, H, D], BF16, kind="Internal").ap(),
        "scr_w2": nc.dram_tensor("scr_w2", [E, D, H], BF16, kind="Internal").ap(),
    }
    es = ExitStack()
    with es:
        sb = lambda name, w, dt=F32, p=128: es.enter_context(
            nc.sbuf_tensor(name, [p, w], dt))
        buf = {
            "x3t": sb("x3t", 4096),
            "lnbf": sb("lnbf", 1024, BF16),
            "ln4_fm": sb("ln4_fm", 4096, BF16),
            "w_sb": sb("w_sb", 8192, BF16),
            "w2_sb": sb("w2_sb", 8192, BF16),
            "h_fm": sb("h_fm", 8192, BF16),
            "acc": sb("acc", 4096),
            "ctile": sb("ctile", 64),
            "cols": sb("colsd", 16),
            "sqscr": sb("sqscrd", 1024),
            "sq2": sb("sq2d", 1024),
            "b1c": sb("b1csb", 128),
            "rowb": sb("rowbd", 1024, BF16, 1),
            "brf": sb("brfd", 8, F32, 1),
            "brb": sb("brbd", 8, BF16, 1),
            "wr_bf": sb("wr_bfd", 64, BF16),
            "ident": sb("identsb", 128, BF16),
            "ones": sb("onessb", 128, BF16, 1),
            "pr": [es.enter_context(nc.psum_tensor(f"pr{i}", [128, 512], F32))
                   for i in range(8)],
        }
        ctx = Ctx(nc, es)
        ctx.GW1 = (ctx.queue("gd10", engine="q"), ctx.queue("gd11", engine="q"))
        ctx.GW2 = (ctx.queue("gd20", engine="q"), ctx.queue("gd21", engine="q"))
        blk = es.enter_context(nc.Block())
        ctx.run_passes(blk, lambda c: emit_phase_d(c, io, buf, n_tt))
    return nc


def host_prep_d(inputs):
    b1 = np.asarray(inputs["L1_ffn_b1"], np.float32)
    return {
        "W1": np.ascontiguousarray(inputs["L1_ffn_W1"], np.float32),
        "W2": np.ascontiguousarray(inputs["L1_ffn_W2"], np.float32),
        "Wr2": np.ascontiguousarray(inputs["L1_ffn_Wr"], np.float32),
        "b1c": np.ascontiguousarray(
            b1.reshape(8, 16, 128).transpose(2, 0, 1).reshape(128, 128)),
        "rows2": np.ascontiguousarray(inputs["L1_ffn_b2"], np.float32),
        "br2": np.ascontiguousarray(
            np.asarray(inputs["L1_ffn_br"], np.float32).reshape(1, 8)),
        "ident": np.eye(128, dtype=ml_dtypes.bfloat16),
        "ones1": np.ones((1, 128), dtype=ml_dtypes.bfloat16),
    }


def emit_phase_d_sparse(ctx, io, buf, n_tt=8, fused=False):
    T, V, Sc, Q, G = ctx.T, ctx.V, ctx.S, ctx.Q, ctx.G
    GW1, GW2, GL = ctx.GW1, ctx.GW2, ctx.GL
    trk = {"v": V, "s": Sc, "t": T}
    E, D = 8, 2048
    TTOK = n_tt * 512
    NTC = n_tt * 4
    CAP = ((int(TTOK * 2 / 8 * 3.0) + 127) // 128) * 128
    NSC = CAP // 128
    SUBT = [min(512, CAP - s) for s in range(0, CAP, 512)]

    x3_in = io["x3"]
    x4_acc = io["scr_accx2"]
    scr_w1, scr_w2 = io["scr_w1"], io["scr_w2"]
    scr_ln, scr_rec = io["scr_lnd"], io["scr_recd"]

    (x3t, lnbf, ln4_fm, w_sb, w2_sb, h_fm, ctile, cols, sqscr, b1c, rowb,
     brf, brb, wr_bf, ident, ones) = (buf[k] for k in (
        "x3t", "lnbf", "ln4_fm", "w_sb", "w2_sb", "h_fm", "ctile", "cols",
        "sqscr", "b1c", "rowb", "brf", "brb", "wr_bf", "ident", "ones"))
    pr = buf["pr"]
    rec_sb, islots, iota_c = buf["rec_sb"], buf["islots"], buf["iota_c"]
    onecol, ecol, idf32 = buf["onecol"], buf["ecol"], buf["idf32"]
    grec, gtok, zrec, y_out = (buf["grec"], buf["gtok"], buf["zrec"],
                               buf["y_out"])
    gbf = buf["gln"][:]
    ctx.n_gb_last = 0
    cfm = h_fm[:].bitcast(F32)[0:8, 0:TTOK]
    msk = w_sb[:].bitcast(F32)[0:8, 0:TTOK]
    msk2 = w2_sb[:].bitcast(F32)[0:8, 0:TTOK]
    free = {i: ("v", 0) for i in range(8)}

    if not fused:
        for e in range(E):
            G.op(lambda g, e=e: g.dma_start(out=scr_w1[e], in_=io["W1"][e]))
            G.op(lambda g, e=e: g.dma_start(out=scr_w2[e], in_=io["W2"][e]))

    nc1 = Q.op(lambda q: q.dma_start(out=ident[:], in_=io["ident"]))
    nc1 = Q.op(lambda q: q.dma_start(out=idf32[:], in_=io["idf32"]))
    nc1 = Q.op(lambda q: q.dma_start(out=ones[:], in_=io["ones1"]))
    nc1 = Q.op(lambda q: q.dma_start(out=b1c[:], in_=io["b1c"]))
    nc1 = Q.op(lambda q: q.dma_start(out=brf[:], in_=io["br2"]))
    nc1 = Q.op(lambda q: q.dma_start(out=iota_c[:], in_=io["iota_c"]))
    nc1 = Q.op(lambda q: q.dma_start(out=ecol[:], in_=io["ecolv"]))
    nc1 = Q.op(lambda q: q.dma_start(out=zrec[:], in_=io["zrecv"]))
    nG = G.op(lambda g: g.dma_start(
        out=wr_bf[:].rearrange("p (k e) -> p k e", k=8),
        in_=io["Wr2"].rearrange("(k p) e -> p k e", p=128)))
    n_g_setup = nG
    G.wait(G, nG)
    V.wait(Q, nc1)
    nbr = V.op(lambda v: v.tensor_copy(brb[:], brf[:]))
    neps = V.op(lambda v: v.memset(cols[:, 14:15], 1e-5))
    noc = V.op(lambda v: v.memset(onecol[:], 1.0))
    noc = V.op(lambda v: v.memset(rec_sb[:], 0.0))
    Sc.wait(V, neps)
    T.wait(Q, nc1)
    T.wait(G, nG)
    T.wait(V, nbr)
    G.wait(Q, nc1)
    nzf = V.op(lambda v: v.memset(lnbf[:], 0.0))
    Q.wait(V, nzf)
    nzfst = Q.op(lambda q: q.dma_start(out=scr_ln[TTOK:TTOK + 128, :],
                                       in_=lnbf[:]))
    Q.wait(Q, nc1)
    nzrec = 0
    for z in range(E * CAP // 128):
        nzrec = Q.op(lambda q, z=z: q.dma_start(
            out=scr_rec[z * 128:(z + 1) * 128, :], in_=zrec[:]))

    last_cfm = 0
    n_lnst = nzfst
    nR = 0
    for tt in range(n_tt):
        ctx.barrier()
        nload = Q.op(lambda q, tt=tt: q.dma_start(
            out=x3t[:].rearrange("p (tc h) -> p tc h", tc=4),
            in_=x3_in[tt * 512:(tt + 1) * 512, :].rearrange(
                "(tc p) h -> p tc h", p=128)))
        V.wait(Q, nload)
        Q.wait(Q, nload)
        nacc0 = Q.op(lambda q, tt=tt: q.dma_start(
            out=x4_acc[tt * 512:(tt + 1) * 512, :].rearrange(
                "(tc p) h -> p tc h", p=128),
            in_=x3t[:].rearrange("p (tc h) -> p tc h", tc=4)))
        ctx.nacc0d = nacc0

        for tc in range(4):
            gtc = tt * 4 + tc
            xi = x3t[:, tc * 1024:(tc + 1) * 1024]
            n1 = V.op(lambda v, xi=xi: v.reduce_sum(
                cols[:, 0:1], xi, axis=AX.X, negate=True))
            V.wait(V, n1)
            n2 = V.op(lambda v: v.tensor_scalar_mul(cols[:, 0:1],
                                                    cols[:, 0:1], 1.0 / H))
            V.wait(V, n2)
            n3 = V.op(lambda v, xi=xi: v.tensor_scalar_add(
                sqscr[:], xi, cols[:, 0:1]))
            Sc.wait(V, n3)
            n4 = Sc.op(lambda s: s.activation(
                buf["sq2"][:], sqscr[:], AF.Square, accum_out=cols[:, 1:2]))
            V.wait(Sc, n4)
            n5 = V.op(lambda v: v.tensor_scalar_mul(cols[:, 2:3],
                                                    cols[:, 1:2], 1.0 / H))
            Sc.wait(V, n5)
            n6 = Sc.op(lambda s: s.activation(cols[:, 3:4], cols[:, 2:3],
                                              AF.Ln, bias=cols[:, 14:15]))
            Sc.wait(Sc, n6)
            n7 = Sc.op(lambda s: s.activation(cols[:, 2:3], cols[:, 3:4],
                                              AF.Exp, scale=-0.5))
            V.wait(Sc, n7)
            V.wait(Q, n_lnst)
            n8 = V.op(lambda v: v.tensor_scalar_mul(lnbf[:], sqscr[:],
                                                    cols[:, 2:3]))
            Q.wait(V, n8)
            n_lnst = Q.op(lambda q, gtc=gtc: q.dma_start(
                out=scr_ln[gtc * 128:(gtc + 1) * 128, :], in_=lnbf[:]))
            T.wait(V, n8)
            for k in range(8):
                p0 = pr[k % 2]
                eng, cnt = free[k % 2]
                T.wait(trk[eng], cnt)
                nt = T.op(lambda t, k=k, p0=p0: t.transpose(
                    p0[:].bitcast(BF16)[:, 0:128], lnbf[:, ts(k)], ident[:]))
                Sc.wait(T, nt)
                nse = Sc.op(lambda s, k=k, tc=tc, p0=p0: s.copy(
                    ln4_fm[:, k * 512 + tc * 128: k * 512 + tc * 128 + 128],
                    p0[:].bitcast(BF16)[:, 0:128]))
                free[k % 2] = ("s", nse)
            n_lnfm = nse
            eng, cnt = free[4]
            T.wait(trk[eng], cnt)
            T.wait(Sc, n_lnfm)
            for k in range(8):
                nt = T.op(lambda t, k=k, tc=tc: t.matmul(
                    pr[4][:, 0:8],
                    ln4_fm[:, k * 512 + tc * 128: k * 512 + tc * 128 + 128],
                    wr_bf[:, k * 8: k * 8 + 8], start=(k == 0), stop=False))
            nt = T.op(lambda t: t.matmul(
                pr[4][:, 0:8], ones[:], brb[:], start=False, stop=True))
            V.wait(T, nt)
            nA = V.op(lambda v: v.reduce_max(cols[:, 4:5], pr[4][:, 0:8],
                                             axis=AX.X, negate=True))
            V.wait(V, nA)
            nB = V.op(lambda v: v.tensor_scalar_add(ctile[:, 48:56],
                                                    pr[4][:, 0:8],
                                                    cols[:, 4:5]))
            free[4] = ("v", nB)
            Sc.wait(V, nB)
            nC = Sc.op(lambda s: s.activation(ctile[:, 40:48],
                                              ctile[:, 48:56], AF.Exp))
            V.wait(Sc, nC)
            nS = V.op(lambda v: v.reduce_sum(cols[:, 7:8], ctile[:, 40:48],
                                             axis=AX.X))
            V.wait(V, nS)
            nJ = V.op(lambda v: v.reciprocal(cols[:, 7:8], cols[:, 7:8]))
            nD = V.op(lambda v: v.reduce_max(cols[:, 5:6], ctile[:, 40:48],
                                             axis=AX.X))
            V.wait(V, nD)
            nE = V.op(lambda v: v.tensor_single_scalar(
                ctile[:, 32:40], ctile[:, 40:48], cols[:, 5:6],
                ALU.is_equal))
            V.wait(V, nE)
            nF = V.op(lambda v: v.scalar_tensor_tensor(
                ctile[:, 32:40], ctile[:, 32:40], NEG, ctile[:, 40:48],
                ALU.mult, ALU.add))
            V.wait(V, nF)
            nGm = V.op(lambda v: v.reduce_max(cols[:, 6:7], ctile[:, 32:40],
                                              axis=AX.X))
            V.wait(V, nGm)
            nH = V.op(lambda v: v.tensor_single_scalar(
                ctile[:, 32:40], ctile[:, 40:48], cols[:, 6:7], ALU.is_ge))
            V.wait(V, nH)
            nK = V.op(lambda v: v.tensor_mul(ctile[:, 32:40],
                                             ctile[:, 32:40],
                                             ctile[:, 40:48]))
            V.wait(V, nK)
            nL = V.op(lambda v: v.tensor_scalar_mul(
                ctile[:, 0:8], ctile[:, 32:40], cols[:, 7:8]))
            V.wait(V, nL)
            nR = V.op(lambda v, gtc=gtc: v.tensor_copy(
                rec_sb[:, gtc * 16: gtc * 16 + 8], ctile[:, 0:8]))
            nR = V.op(lambda v, gtc=gtc: v.tensor_copy(
                rec_sb[:, gtc * 16 + 8: gtc * 16 + 9],
                iota_c[:, gtc: gtc + 1]))
            nCb = V.op(lambda v: v.tensor_copy(
                ctile[:].bitcast(BF16)[:, 112:120], ctile[:, 0:8]))
            eng, cnt = free[5]
            T.wait(trk[eng], cnt)
            T.wait(V, nCb)
            ntc2 = T.op(lambda t: t.transpose(
                pr[5][:].bitcast(BF16)[0:8, 0:128],
                ctile[:].bitcast(BF16)[:, 112:120], ident[:]))
            V.wait(T, ntc2)
            V.wait(V, last_cfm)
            ncf = V.op(lambda v, gtc=gtc: v.tensor_copy(
                cfm[0:8, gtc * 128:(gtc + 1) * 128],
                pr[5][:].bitcast(BF16)[0:8, 0:128]))
            last_cfm = ncf
            free[5] = ("v", ncf)

    ctx.barrier()
    nm = V.op(lambda v: v.tensor_single_scalar(msk, cfm, 0.0, ALU.is_gt))
    V.wait(V, nm)
    nm = V.op(lambda v: v.tensor_tensor_scan(
        cfm, onecol[:].broadcast_to([8, TTOK]), msk, 0.0,
        ALU.mult, ALU.add))
    V.wait(V, nm)
    nm = V.op(lambda v: v.tensor_scalar_add(cfm, cfm, -1.0))
    V.wait(V, nm)
    nm = V.op(lambda v: v.tensor_single_scalar(msk2, cfm, float(CAP) - 0.5,
                                               ALU.is_lt))
    V.wait(V, nm)
    nm = V.op(lambda v: v.tensor_mul(msk, msk, msk2))
    V.wait(V, nm)
    nm = V.op(lambda v: v.tensor_scalar(cfm, cfm, ecol[:, 0:1], None,
                                        ALU.add))
    V.wait(V, nm)
    nm = V.op(lambda v: v.tensor_mul(cfm, cfm, msk))
    V.wait(V, nm)
    nm = V.op(lambda v: v.scalar_tensor_tensor(
        msk2, msk, -1.0, onecol[:].broadcast_to([8, TTOK]),
        ALU.mult, ALU.add))
    V.wait(V, nm)
    nm = V.op(lambda v: v.scalar_tensor_tensor(
        cfm, msk2, 1e6, cfm, ALU.mult, ALU.add))
    V.wait(V, nm)
    nslt = 0
    nsc = 0
    bcreg = [None]
    def _bc(g):
        if bcreg[0] is None:
            bcreg[0] = g.to_reg(E * CAP - 1)
        return bcreg[0]
    G.wait(Q, nzrec)
    G.wait(V, nR)
    T.wait(V, nm)
    for gtc in range(NTC):
        eng, cnt = free[gtc % 2]
        T.wait(trk[eng], cnt)
        ntx = T.op(lambda t, gtc=gtc: t.transpose(
            pr[gtc % 2][:, 0:8], cfm[0:8, gtc * 128:(gtc + 1) * 128],
            idf32[0:8, 0:8]))
        V.wait(T, ntx)
        V.wait(V, nslt)
        nslt = V.op(lambda v, gtc=gtc: v.tensor_copy(
            islots[:, gtc * 8:(gtc + 1) * 8], pr[gtc % 2][:, 0:8]))
        free[gtc % 2] = ("v", nslt)
        G.wait(V, nslt)
        for e in range(E):
            nsc = G.op(lambda g, gtc=gtc, e=e: g.indirect_dma_start(
                out=scr_rec,
                out_offset=bass.IndirectOffsetOnAxis(
                    ap=islots[:, gtc * 8 + e: gtc * 8 + e + 1], axis=0),
                in_=rec_sb[:, gtc * 16:(gtc + 1) * 16],
                in_offset=None,
                bounds_check=_bc(g), oob_is_err=False))

    ctx.barrier()
    for qq in (GW1[0], GW1[1], GW2[0], GW2[1]):
        qq.wait(G, n_g_setup)
    wlast_w1 = [0, 0]
    wlast_w2 = [0, 0]
    n_ymm_prev = 0
    n_ysc_prev = 0
    n_ysc_slot = [0, 0, 0, 0]
    n_gtrans = 0
    NTILES = len(SUBT)
    tiles = [(e_, ti_) for e_ in range(E) for ti_ in range(NTILES)]
    grec_q = [grec[:, 0:NSC * 16], grec[:, NSC * 16:NSC * 32]]
    gtok_q = [gtok[:, 0:NSC], gtok[:, NSC:2 * NSC]]
    ntk_q = [0, 0]
    n_grec_read_exp = {}
    n_ysc_exp_end = {}
    pend_ngb = {}

    def emit_grec(e2):
        p = e2 % 2
        GL[0].wait(GL[0], GL[0].n)
        GL[0].wait(G, nsc)
        GL[0].wait(V, n_grec_read_exp.get(e2 - 2, 0))
        ngrec = GL[0].op(lambda g, e2=e2, p=p: g.dma_start(
            out=grec_q[p].rearrange("p (s r) -> p s r", s=NSC),
            in_=scr_rec[e2 * CAP:(e2 + 1) * CAP, :].rearrange(
                "(s p) r -> p s r", p=128)))
        V.wait(GL[0], ngrec)
        V.wait(GL[1], n_ysc_exp_end.get(e2 - 2, 0))
        ntk_q[p] = V.op(lambda v, p=p: v.tensor_copy(
            gtok_q[p],
            grec_q[p].rearrange("p (s r) -> p s r", s=NSC)[:, :, 8]))

    def emit_gathers(gi2):
        e2, ti2 = tiles[gi2]
        p = e2 % 2
        GL[0].wait(GL[0], GL[0].n)
        GL[0].wait(V, ntk_q[p])
        GL[0].wait(T, n_gtrans)
        ngb = 0
        for s in range(SUBT[ti2] // 128):
            sc_ = ti2 * 4 + s
            ngb = GL[0].op(lambda g, sc_=sc_, s=s, p=p: g.indirect_dma_start(
                out=gbf[:, s * 1024:(s + 1) * 1024],
                out_offset=None,
                in_=scr_ln,
                in_offset=bass.IndirectOffsetOnAxis(
                    ap=gtok_q[p][:, sc_: sc_ + 1], axis=0)))
        pend_ngb[gi2] = ngb

    G.wait(G, n_g_setup)
    emit_grec(0)
    emit_gathers(0)
    for gi, (e, ti) in enumerate(tiles):
        wt = SUBT[ti]
        toff = ti * 512
        ntc_w = wt // 128
        p_ = e % 2
        if ti == 0:
            G.wait(T, n_ymm_prev)
            G.wait(G, G.n)
            nrow = G.op(lambda g, e=e: g.dma_start(
                out=rowb[:], in_=io["rows2"][e: e + 1, :]))
            T.wait(G, nrow)
        if True:
            T.wait(GL[0], pend_ngb[gi])
            Sc.wait(T, n_ymm_prev)
            nse = 0
            for s in range(ntc_w):
                for k in range(8):
                    p0 = pr[k % 4]
                    eng, cnt = free[k % 4]
                    T.wait(trk[eng], cnt)
                    nt = T.op(lambda t, k=k, s=s, p0=p0: t.transpose(
                        p0[:].bitcast(BF16)[:, 0:128],
                        gbf[:, s * 1024 + k * 128: s * 1024 + k * 128
                            + 128],
                        ident[:]))
                    n_gtrans = nt
                    Sc.wait(T, nt)
                    nse = Sc.op(lambda s_, k=k, s=s, p0=p0: s_.copy(
                        ln4_fm[:, k * 512 + s * 128: k * 512 + s * 128
                               + 128],
                        p0[:].bitcast(BF16)[:, 0:128]))
                    free[k % 4] = ("s", nse)
            n_lnfm = nse
            if gi + 1 < len(tiles):
                if tiles[gi + 1][1] == 0:
                    emit_grec(e + 1)
                emit_gathers(gi + 1)

            for fc in range(4):
                slot = fc % 2
                GW1[slot].wait(T, wlast_w1[slot])
                nwld = GW1[slot].op(lambda g, e=e, fc=fc, slot=slot:
                                    g.dma_start(
                    out=w_sb[:, slot * 4096:(slot + 1) * 4096].rearrange(
                        "p (k f) -> p k f", k=8),
                    in_=scr_w1[e].rearrange("(k p) f -> p k f", p=128)[
                        :, :, fc * 512:(fc + 1) * 512]))
                T.wait(GW1[slot], nwld)
                for fb in range(4):
                    f = fc * 4 + fb
                    pp = pr[f % 4]
                    eng, cnt = free[f % 4]
                    T.wait(trk[eng], cnt)
                    T.wait(Sc, n_lnfm)
                    for k in range(8):
                        nt = T.op(lambda t, k=k, fb=fb, slot=slot, pp=pp,
                                  wt=wt: t.matmul(
                            pp[:, 0:wt],
                            w_sb[:, slot * 4096 + k * 512 + fb * 128:
                                 slot * 4096 + k * 512 + fb * 128 + 128],
                            ln4_fm[:, k * 512: k * 512 + wt],
                            start=(k == 0), stop=(k == 7)))
                    Sc.wait(T, nt)
                    nse = Sc.op(lambda s, f=f, e=e, pp=pp, wt=wt:
                                s.activation(
                        h_fm[:, f * 512: f * 512 + wt], pp[:, 0:wt],
                        AF.Relu,
                        bias=b1c[:, e * 16 + f: e * 16 + f + 1]))
                    free[f % 4] = ("s", nse)
                wlast_w1[slot] = nt
            n_h = nse

            T.wait(Sc, n_h)
            for b_ in range(8):
                eng, cnt = free[b_]
                T.wait(trk[eng], cnt)
            for dg in range(4):
                slot = dg % 2
                GW2[slot].wait(T, wlast_w2[slot])
                nwld = GW2[slot].op(lambda g, e=e, dg=dg, slot=slot:
                                    g.dma_start(
                    out=w2_sb[:, slot * 4096:(slot + 1) * 4096].rearrange(
                        "p (g o) -> p g o", g=4),
                    in_=scr_w2[e][dg * 512:(dg + 1) * 512, :].rearrange(
                        "(g p) o -> p g o", p=128)))
                T.wait(GW2[slot], nwld)
                for tc in range(ntc_w):
                    for fb in range(4):
                        f = dg * 4 + fb
                        for half in range(2):
                            nt = T.op(lambda t, f=f, tc=tc, fb=fb,
                                      half=half, slot=slot: t.matmul(
                                pr[tc * 2 + half][:, 0:512],
                                h_fm[:, f * 512 + tc * 128:
                                     f * 512 + tc * 128 + 128],
                                w2_sb[:, slot * 4096 + fb * 1024
                                      + half * 512: slot * 4096
                                      + fb * 1024 + half * 512 + 512],
                                start=(dg == 0 and fb == 0), stop=False))
                wlast_w2[slot] = nt
            for tc in range(ntc_w):
                for half in range(2):
                    nt = T.op(lambda t, tc=tc, half=half: t.matmul(
                        pr[tc * 2 + half][:, 0:512], ones[:],
                        rowb[0:1, half * 512: half * 512 + 512],
                        start=False, stop=True))
            n_ymm = nt
            n_ymm_prev = nt
            for tc in range(ntc_w):
                V.wait(T, n_ymm)
                ysl = y_out[:, tc * 1024: tc * 1024 + 1024]
                V.wait(GL[1], n_ysc_slot[tc])
                for half in range(2):
                    nv = V.op(lambda v, tc=tc, half=half, e=e, ysl=ysl,
                              toff=toff, p_=p_: v.tensor_scalar_mul(
                        ysl[:, half * 512: half * 512 + 512],
                        pr[tc * 2 + half][:, 0:512],
                        grec_q[p_][:, (toff // 128 + tc) * 16 + e:
                                   (toff // 128 + tc) * 16 + e + 1]))
                    free[tc * 2 + half] = ("v", nv)
                n_grec_read_exp[e] = nv
                GL[1].wait(V, nv)
                GL[1].wait(Q, ctx.nacc0d)
                GL[1].wait(GL[1], n_ysc_prev)
                n_ysc = GL[1].op(lambda g, tc=tc, e=e, ysl=ysl, toff=toff,
                                 p_=p_: g.indirect_dma_start(
                    out=x4_acc,
                    out_offset=bass.IndirectOffsetOnAxis(
                        ap=gtok_q[p_][:, toff // 128 + tc:
                                      toff // 128 + tc + 1], axis=0),
                    in_=ysl,
                    in_offset=None,
                    compute_op=ALU.add))
                n_ysc_prev = n_ysc
                n_ysc_slot[tc] = n_ysc
            if ti == NTILES - 1:
                n_ysc_exp_end[e] = n_ysc_prev
    GL[1].wait(GL[1], n_ysc_prev)
    Q.wait(GL[1], n_ysc_prev)
    ctx.barrier()
    nfin = Q.op(lambda q: q.dma_start(out=io["x4"],
                                      in_=x4_acc[0:TTOK, :]))
    Q.wait(Q, nfin)


def build_phase_d_sparse(n_tt=8):
    nc = bass.Bass("TRN2", target_bir_lowering=False)
    Sx = n_tt * 512
    E, D = 8, 2048
    CAP = ((int(Sx * 2 / 8 * 3.0) + 127) // 128) * 128
    NSC = CAP // 128
    io = {
        "x3": nc.dram_tensor("x3", [Sx, H], F32, kind="ExternalInput").ap(),
        "W1": nc.dram_tensor("W1", [E, H, D], F32, kind="ExternalInput").ap(),
        "W2": nc.dram_tensor("W2", [E, D, H], F32, kind="ExternalInput").ap(),
        "Wr2": nc.dram_tensor("Wr2", [H, E], F32, kind="ExternalInput").ap(),
        "b1c": nc.dram_tensor("b1c", [128, 128], F32,
                              kind="ExternalInput").ap(),
        "rows2": nc.dram_tensor("rows2", [E, 1024], F32,
                                kind="ExternalInput").ap(),
        "br2": nc.dram_tensor("br2", [1, 8], F32, kind="ExternalInput").ap(),
        "ident": nc.dram_tensor("ident", [128, 128], BF16,
                                kind="ExternalInput").ap(),
        "idf32": nc.dram_tensor("idf32", [128, 128], F32,
                                kind="ExternalInput").ap(),
        "ones1": nc.dram_tensor("ones1", [1, 128], BF16,
                                kind="ExternalInput").ap(),
        "iota_c": nc.dram_tensor("iota_c", [128, n_tt * 4], F32,
                                 kind="ExternalInput").ap(),
        "ecolv": nc.dram_tensor("ecolv", [8, 1], F32,
                                kind="ExternalInput").ap(),
        "zrecv": nc.dram_tensor("zrecv", [128, 16], F32,
                                kind="ExternalInput").ap(),
        "x4": nc.dram_tensor("x4", [Sx, H], F32, kind="ExternalOutput").ap(),
        "scr_w1": nc.dram_tensor("scr_w1", [E, H, D], BF16,
                                 kind="Internal").ap(),
        "scr_w2": nc.dram_tensor("scr_w2", [E, D, H], BF16,
                                 kind="Internal").ap(),
        "scr_lnd": nc.dram_tensor("scr_lnd", [Sx + 128, H], BF16,
                                  kind="Internal").ap(),
        "scr_recd": nc.dram_tensor("scr_recd", [E * CAP + 128, 16], F32,
                                   kind="Internal").ap(),
        "scr_accx2": nc.dram_tensor("scr_accx2", [Sx + 128, H], F32,
                                    kind="Internal").ap(),
    }
    es = ExitStack()
    with es:
        sb = lambda name, w, dt=F32, p=128: es.enter_context(
            nc.sbuf_tensor(name, [p, w], dt))
        buf = {
            "x3t": sb("x3t", 4096),
            "lnbf": sb("lnbf", 1024, BF16),
            "ln4_fm": sb("ln4_fm", 4096, BF16),
            "w_sb": sb("w_sb", 8192, BF16),
            "w2_sb": sb("w2_sb", 8192, BF16),
            "h_fm": sb("h_fm", 8192, BF16),
            "ctile": sb("ctile", 64),
            "cols": sb("colsd", 16),
            "sqscr": sb("sqscrd", 1024),
            "sq2": sb("sq2d", 1024),
            "b1c": sb("b1csb", 128),
            "rowb": sb("rowbd", 1024, BF16, 1),
            "brf": sb("brfd", 8, F32, 1),
            "brb": sb("brbd", 8, BF16, 1),
            "wr_bf": sb("wr_bfd", 64, BF16),
            "ident": sb("identsb", 128, BF16),
            "idf32": sb("idf32sb", 128),
            "ones": sb("onessb", 128, BF16, 1),
            "rec_sb": sb("rec_sbd", n_tt * 64),
            "islots": sb("islotsd", n_tt * 32, mybir.dt.int32),
            "iota_c": sb("iota_csbd", n_tt * 4),
            "onecol": sb("onecold", 1, F32, 8),
            "ecol": sb("ecolsbd", 1, F32, 8),
            "grec": sb("grecd", NSC * 32),
            "gtok": sb("gtokd", NSC * 2, mybir.dt.int32),
            "zrec": sb("zrecd", 16),
            "y_out": sb("y_outd", 4096),
            "gln": sb("glnd", 4096, BF16),
            "pr": [es.enter_context(nc.psum_tensor(f"pr{i}", [128, 512], F32))
                   for i in range(8)],
        }
        ctx = Ctx(nc, es)
        ctx.GW1 = (ctx.queue("gd10", engine="q"), ctx.queue("gd11", engine="q"))
        ctx.GW2 = (ctx.queue("gd20", engine="q"), ctx.queue("gd21", engine="q"))
        ctx.GL = (ctx.queue("gl0"), ctx.queue("gl1"))
        blk = es.enter_context(nc.Block())
        ctx.run_passes(blk, lambda c: emit_phase_d_sparse(c, io, buf, n_tt))
    return nc


def host_prep_d_sparse(inputs, n_tt=8):
    d = host_prep_d(inputs)
    Sx = n_tt * 512
    CAP = ((int(Sx * 2 / 8 * 3.0) + 127) // 128) * 128
    d["idf32"] = np.eye(128, dtype=np.float32)
    d["iota_c"] = np.ascontiguousarray(
        (np.arange(n_tt * 4)[None, :] * 128
         + np.arange(128)[:, None]).astype(np.float32))
    d["ecolv"] = (np.arange(8, dtype=np.float32) * CAP).reshape(8, 1)
    z = np.zeros((128, 16), np.float32)
    z[:, 8] = Sx + np.arange(128)
    d["zrecv"] = z
    return d


def _build_masks():
    kp = np.arange(128)[:, None]
    qp = np.arange(128)[None, :]
    return np.where(kp <= qp, 0.0, 1e30).astype(np.float32)


def host_prep_a(inputs):
    wq = np.asarray(inputs["L0_attn_Wq"], np.float32)
    wk = np.asarray(inputs["L0_attn_Wk"], np.float32)
    wv = np.asarray(inputs["L0_attn_Wv"], np.float32)
    wo = np.asarray(inputs["L0_attn_Wo"], np.float32)
    wq_t = np.empty((H, 1024), np.float32)
    wo_t = np.empty((1024, H), np.float32)
    for jslot, h in enumerate(HEAD_AT_SLOT):
        wq_t[:, jslot * 64:(jslot + 1) * 64] = wq[h * 64:(h + 1) * 64, :].T
        wo_t[jslot * 64:(jslot + 1) * 64, :] = wo[:, h * 64:(h + 1) * 64].T
    return {
        "wq_t": np.ascontiguousarray(wq_t),
        "wk_t": np.ascontiguousarray(wk.T),
        "wv_t": np.ascontiguousarray(wv.T),
        "wo_t": np.ascontiguousarray(wo_t),
        "ident": np.eye(128, dtype=ml_dtypes.bfloat16),
        "masks": _build_masks(),
        "ones_c": np.ones((128, 1), dtype=ml_dtypes.bfloat16),
        "negones": np.full((1, 128), -1.0, dtype=ml_dtypes.bfloat16),
    }




def build_fused():
    nc = bass.Bass("TRN2", target_bir_lowering=False)
    E, F, D = 8, 4096, 2048
    dram = lambda name, shape, dt=F32, kind="ExternalInput": nc.dram_tensor(
        name, shape, dt, kind=kind).ap()
    ident_ap = dram("ident", [128, 128], BF16)
    ones_ap = dram("ones1", [1, 128], BF16)
    x1s = dram("x1s", [S, H], F32, "Internal")
    x3s = dram("x3s", [S, H], F32, "Internal")
    CAP = ((int(S * 2 / 8 * 3.0) + 127) // 128) * 128
    idf32_ap = dram("idf32", [128, 128])
    iota_ap = dram("iota_c", [128, 32])
    ecol_ap = dram("ecolv", [8, 1])
    zrec_ap = dram("zrecv", [128, 16])
    accx_ap = dram("scr_accx", [S + 128, H], F32, "Internal")
    ioA = {
        "x": dram("x", [S, H]),
        "wq_t": dram("wq_t", [H, 1024]), "wk_t": dram("wk_t", [H, 256]),
        "wv_t": dram("wv_t", [H, 256]), "wo_t": dram("wo_t", [H, 1024]),
        "ident": ident_ap, "masks": dram("masks", [128, 128]),
        "ones_c": dram("ones_c", [128, 1], BF16),
        "negones": dram("negones", [1, 128], BF16),
        "x1": x1s,
    }
    ioB = {
        "x1": x1s,
        "Wi": dram("Wi", [E, H, F]), "Wg": dram("Wg", [E, H, F]),
        "Wo": dram("Wo", [E, F, H]), "Wr": dram("Wr", [H, E]),
        "bic": dram("bic", [128, 256]), "bgc": dram("bgc", [128, 256]),
        "rows": dram("rows", [3, E, 1024]), "br": dram("br", [1, 8]),
        "ident": ident_ap, "ones1": ones_ap,
        "scr_wi": dram("scr_wi", [E, H, F], BF16, "Internal"),
        "scr_wg": dram("scr_wg", [E, H, F], BF16, "Internal"),
        "scr_wo": dram("scr_wo", [E, F, H], BF16, "Internal"),
        "idf32": idf32_ap, "iota_c": iota_ap, "ecolv": ecol_ap,
        "zrecv": zrec_ap, "dumpc": dram("dumpc", [128, 1]),
        "scr_ln": dram("scr_ln", [S + 128, H], BF16, "Internal"),
        "scr_lnf": dram("scr_lnf", [S + 128, H], F32, "Internal"),
        "scr_rec": dram("scr_rec", [E * CAP + 128, 16], F32, "Internal"),
        "scr_accx": accx_ap,
    }
    ioC = {
        "x2": accx_ap, "x3": x3s,
        "W_in": dram("W_in", [H, 4096]), "W_out": dram("W_out", [2048, H]),
        "A_logT": dram("A_logT", [2048, 16]),
        "D_param": dram("D_param", [128, 16]),
        "conv_w": dram("conv_w", [128, 64]), "conv_b": dram("conv_b", [128, 16]),
        "ident": ident_ap,
        "scr_win": dram("scr_win", [H, 4096], BF16, "Internal"),
    }
    ioD = {
        "x3": x3s,
        "x4": dram("x4", [S, H], F32, "ExternalOutput"),
        "W1": dram("W1", [E, H, D]), "W2": dram("W2", [E, D, H]),
        "Wr2": dram("Wr2", [H, E]), "b1c": dram("b1c", [128, 128]),
        "rows2": dram("rows2", [E, 1024]), "br2": dram("br2", [1, 8]),
        "ident": ident_ap, "ones1": ones_ap,
        "idf32": idf32_ap, "iota_c": iota_ap, "ecolv": ecol_ap,
        "zrecv": zrec_ap,
        "scr_lnd": dram("scr_lnd", [S + 128, H], BF16, "Internal"),
        "scr_recd": dram("scr_recd", [E * CAP + 128, 16], F32, "Internal"),
        "scr_accx2": dram("scr_accx2", [S + 128, H], F32, "Internal"),
    }

    es = ExitStack()
    with es:
        ctx = Ctx(nc, es)
        ctx.GW = (ctx.queue("gw0", engine="q"), ctx.queue("gw1", engine="q"))
        ctx.GWI = (ctx.queue("gwi0", engine="q"), ctx.queue("gwi1", engine="q"),
                   ctx.queue("gwi2", engine="q"))
        ctx.GWG = (ctx.queue("gwg0", engine="q"), ctx.queue("gwg1", engine="q"),
                   ctx.queue("gwg2", engine="q"))
        ctx.GWO = (ctx.queue("gwo0", engine="q"), ctx.queue("gwo1", engine="q"))
        ctx.GW1 = (ctx.queue("gd10", engine="q"), ctx.queue("gd11", engine="q"))
        ctx.GW2 = (ctx.queue("gd20", engine="q"), ctx.queue("gd21", engine="q"))
        ctx.GL = (ctx.queue("gl0"), ctx.queue("gl1"))

        def mk_bufs(pfx, spec, esx):
            sb = lambda name, w, dt=F32, p=128: esx.enter_context(
                nc.sbuf_tensor(pfx + name, [p, w], dt))
            ps = lambda name, w, dt=F32: esx.enter_context(
                nc.psum_tensor(pfx + name, [128, w], dt))
            out = {}
            for key, kind, args in spec:
                if kind == "sb":
                    out[key] = sb(key, *args)
                elif kind == "ps":
                    out[key] = ps(key, *args)
                elif kind == "sblist":
                    out[key] = [sb(key + str(i), *args) for i in range(2)]
                elif kind == "pslist":
                    out[key] = [ps(key + str(i), *args) for i in range(args[0] and 8)]
            return out

        esA = ExitStack()
        bufA = {}
        sbA = lambda name, w, dt=F32, p=128: esA.enter_context(
            nc.sbuf_tensor("a_" + name, [p, w], dt))
        psA_ = lambda name, w, dt=F32: esA.enter_context(
            nc.psum_tensor("a_" + name, [128, w], dt))
        bufA = {
            "xin": sbA("xin", 2048), "x1t": sbA("x1t", 1024),
            "ccen": sbA("ccen", 1024), "sqscr": sbA("sqscr", 1024),
            "cols": sbA("cols", 24),
            "ln_bf": sbA("ln_bf", 1024, BF16), "ln_fm": sbA("ln_fm", 1024, BF16),
            "kfm": [sbA("kfm0", NT * 128, BF16), sbA("kfm1", NT * 128, BF16)],
            "vtl": sbA("vtl", NT * 260, BF16), "qfm": sbA("qfm", 1024, BF16),
            "ssb": sbA("ssb", 2 * NT * 128), "psb": sbA("psb", 2 * NT * 128, BF16),
            "mbc_sb": sbA("mbc_sb", 640),
            "ones_col": sbA("ones_colsb", 1, BF16),
            "negones": sbA("negonessb", 128, BF16, 1),
            "ksq": sbA("ksq", 256, BF16), "qsq": sbA("qsq", 1024, BF16),
            "qn2_sb": sbA("qn2_sb", 2048, F32, 1),
            "lq_sb": sbA("lq_sb", 2048, F32, 1),
            "crow_sb": sbA("crow_sb", 2048, BF16, 1),
            "rsc": sbA("rsc", 16, F32, 1),
            "ones_r": sbA("ones_r", 128, BF16, 1), "att": sbA("att", 1024),
            "attbf": sbA("attbf", 1024, BF16), "attfm": sbA("attfm", 1024, BF16),
            "wq_bf": sbA("wq_bf", 8192, BF16), "wk_bf": sbA("wk_bf", 2048, BF16),
            "wv_bf": sbA("wv_bf", 2048, BF16), "wo_bf": sbA("wo_bf", 8192, BF16),
            "ident": sbA("identsb", 128, BF16), "masks": sbA("maskssb", 128),
            "psA": psA_("psA", 512), "psB": psA_("psB", 512),
            "ptA": psA_("ptA", 512), "ptB": psA_("ptB", 512),
            "pv": psA_("pv", 512), "pk": psA_("pk", 512),
            "pq1": psA_("pq1", 512), "pq2": psA_("pq2", 512),
        }

        def emit_all(c):
            T, V, Sc, Q, G = c.T, c.V, c.S, c.Q, c.G
            for name, dst in (("wq_t", bufA["wq_bf"]), ("wk_t", bufA["wk_bf"]),
                              ("wv_t", bufA["wv_bf"]), ("wo_t", bufA["wo_bf"])):
                c.nW_a = G.op(lambda g, name=name, dst=dst: g.dma_start(
                    out=dst[:].rearrange("p (k o) -> p k o", k=8),
                    in_=ioA[name].rearrange("(k p) o -> p k o", p=128)))
            for e in range(E):
                G.op(lambda g, e=e: g.dma_start(out=ioB["scr_wi"][e],
                                                in_=ioB["Wi"][e]))
                G.op(lambda g, e=e: g.dma_start(out=ioB["scr_wg"][e],
                                                in_=ioB["Wg"][e]))
                G.op(lambda g, e=e: g.dma_start(out=ioB["scr_wo"][e],
                                                in_=ioB["Wo"][e]))
            G.op(lambda g: g.dma_start(out=ioC["scr_win"], in_=ioC["W_in"]))
            for e in range(E):
                G.op(lambda g, e=e: g.dma_start(out=ioD["scr_w1"][e],
                                                in_=ioD["W1"][e]))
                G.op(lambda g, e=e: g.dma_start(out=ioD["scr_w2"][e],
                                                in_=ioD["W2"][e]))
            emit_phase_a(c, ioA, bufA, NT, fused=True)
            c.barrier()
            emit_phase_b_sparse(c, ioB, bufB, 8, fused=True)
            c.barrier()
            emit_phase_c(c, ioC, bufC, 8, fused=True)
            c.barrier()
            emit_phase_d_sparse(c, ioD, bufD, 8, fused=True)

        esA.close()
        esB = ExitStack()
        sbB = lambda name, w, dt=F32, p=128: esB.enter_context(
            nc.sbuf_tensor("b_" + name, [p, w], dt))
        NSC = CAP // 128
        bufB = {
            "x1t": sbB("x1t", 4096), "ln2_sb": sbB("ln2_sb", 4096),
            "ln2_fm": sbB("ln2_fm", 4096, BF16), "lnbf": sbB("lnbf", 1024, BF16),
            "h_fm": sbB("h_fm", 16384, BF16),
            "sgt": sbB("sgt", 1024, BF16), "wi_sb": sbB("wi_sb", 12288, BF16),
            "wg_sb": sbB("wg_sb", 12288, BF16),
            "wo_sb": sbB("wo_sb", 8192, BF16),
            "acc": sbB("acc", 16), "r_sb": sbB("r_sb", 2048),
            "gb_sb": sbB("gb_sb", 1024), "bb_sb": sbB("bb_sb", 1024),
            "ctile": sbB("ctile", 64), "cols": sbB("cols", 16),
            "sqscr": sbB("sqscr", 1024), "bic": sbB("bicsb", 256),
            "bgc": sbB("bgcsb", 256),
            "rowb": sbB("rowb", 3072, BF16, 1), "brf": sbB("brf", 8, F32, 1),
            "brb": sbB("brb", 8, BF16, 1), "wr_bf": sbB("wr_bfsb", 64, BF16),
            "ident": sbB("identsb", 128, BF16), "ones": sbB("onessb", 128, BF16, 1),
            "idf32": sbB("idf32sb", 128),
            "rec_sb": sbB("rec_sb", 512),
            "islots": sbB("islots", 256, mybir.dt.int32),
            "iota_c": sbB("iota_csb", 32),
            "onecol": sbB("onecol", 1, F32, 8),
            "ecol": sbB("ecolsb", 1, F32, 8),
            "grec": sbB("grec", NSC * 32),
            "gtok": sbB("gtok", NSC * 2, mybir.dt.int32),
            "zrec": sbB("zrec", 16),
            "y_out": sbB("y_out", 4096),
            "gln": sbB("gln", 4096, BF16),
            "dumpc": sbB("dumpcsb", 1),
            "dmsk": sbB("dmsk", 8),
            "dslf": sbB("dslf", 8),
            "pr": [esB.enter_context(nc.psum_tensor(f"b_pr{i}", [128, 512], F32))
                   for i in range(8)],
        }
        esB.close()
        esC = ExitStack()
        sbC = lambda name, w, dt=F32, p=128: esC.enter_context(
            nc.sbuf_tensor("c_" + name, [p, w], dt))
        bufC = {
            "x2t": sbC("x2t", 4096), "lnbf": sbC("lnbf", 1024, BF16),
            "ln3_fm": sbC("ln3_fm", 4096, BF16), "win_sb": sbC("win_sb", 8192, BF16),
            "wout_sb": sbC("wout_sb", 16384, BF16), "u_sb": sbC("u_sb", 8192),
            "zs_bf": sbC("zs_bf", 8192, BF16), "yg_bf": sbC("yg_bf", 8192, BF16),
            "y_acc": sbC("y_acc", 8192), "upad": sbC("upad", 1030),
            "sigma": sbC("sigma", 2048), "state": sbC("state", 256),
            "utail": sbC("utail", 64), "a_sb": sbC("a_sb", 256),
            "dcol": sbC("dcol", 16), "wcv": sbC("wcv", 64),
            "cbc": sbC("cbc", 16), "cols": sbC("colsc", 16),
            "sqscr": sbC("sqscrc", 1024),
            "x3t": sbC("x3t", 4096), "ident": sbC("identsb", 128, BF16),
            "pr": [esC.enter_context(nc.psum_tensor(f"c_pr{i}", [128, 512], F32))
                   for i in range(8)],
        }
        esC.close()
        esD = ExitStack()
        sbD = lambda name, w, dt=F32, p=128: esD.enter_context(
            nc.sbuf_tensor("d_" + name, [p, w], dt))
        bufD = {
            "x3t": sbD("x3t", 4096), "lnbf": sbD("lnbf", 1024, BF16),
            "ln4_fm": sbD("ln4_fm", 4096, BF16), "w_sb": sbD("w_sb", 8192, BF16),
            "w2_sb": sbD("w2_sb", 8192, BF16),
            "h_fm": sbD("h_fm", 8192, BF16),
            "ctile": sbD("ctile", 64), "cols": sbD("colsd", 16),
            "sqscr": sbD("sqscrd", 1024), "sq2": sbD("sq2d", 1024),
            "b1c": sbD("b1csb", 128),
            "rowb": sbD("rowbd", 1024, BF16, 1), "brf": sbD("brfd", 8, F32, 1),
            "brb": sbD("brbd", 8, BF16, 1), "wr_bf": sbD("wr_bfd", 64, BF16),
            "ident": sbD("identsb", 128, BF16), "ones": sbD("onessb", 128, BF16, 1),
            "idf32": sbD("idf32sb", 128),
            "rec_sb": sbD("rec_sbd", 512),
            "islots": sbD("islotsd", 256, mybir.dt.int32),
            "iota_c": sbD("iota_csbd", 32),
            "onecol": sbD("onecold", 1, F32, 8),
            "ecol": sbD("ecolsbd", 1, F32, 8),
            "grec": sbD("grecd", NSC * 32),
            "gtok": sbD("gtokd", NSC * 2, mybir.dt.int32),
            "zrec": sbD("zrecd", 16),
            "y_out": sbD("y_outd", 4096),
            "gln": sbD("glnd", 4096, BF16),
            "pr": [esD.enter_context(nc.psum_tensor(f"d_pr{i}", [128, 512], F32))
                   for i in range(8)],
        }
        esD.close()
        ioD["scr_w1"] = dram("scr_w1", [E, H, D], BF16, "Internal")
        ioD["scr_w2"] = dram("scr_w2", [E, D, H], BF16, "Internal")

        blk = es.enter_context(nc.Block())
        ctx.run_passes(blk, emit_all)
    return nc


def kernel(**inputs):
    x = np.ascontiguousarray(np.asarray(inputs["L0_attn_hidden_states"],
                                        np.float32))
    cores = list(range(NCORES))
    shared = {}
    shared.update(host_prep_a(inputs))
    shared.update(host_prep_b_sparse(inputs, 8))
    shared.update(host_prep_c(inputs))
    shared.update(host_prep_d_sparse(inputs, 8))
    nc = build_fused()
    in_maps = [dict(shared, x=np.ascontiguousarray(x[c])) for c in cores]
    res = run_bass_kernel_spmd(nc, in_maps, core_ids=cores)
    return np.stack([r["x4"] for r in res.results])


def kernel_unfused(**inputs):

    x = np.ascontiguousarray(np.asarray(inputs["L0_attn_hidden_states"],
                                        np.float32))
    cores = list(range(NCORES))

    shared_a = host_prep_a(inputs)
    nc_a = build_phase_a()
    res = run_bass_kernel_spmd(
        nc_a, [dict(shared_a, x=np.ascontiguousarray(x[c])) for c in cores],
        core_ids=cores)
    xs = [r["x1"] for r in res.results]

    shared_b = host_prep_b(inputs)
    nc_b = build_phase_b()
    res = run_bass_kernel_spmd(
        nc_b, [dict(shared_b, x1=xs[c]) for c in cores], core_ids=cores)
    xs = [r["x2"] for r in res.results]

    shared_c = host_prep_c(inputs)
    nc_c = build_phase_c()
    res = run_bass_kernel_spmd(
        nc_c, [dict(shared_c, x2=xs[c]) for c in cores], core_ids=cores)
    xs = [r["x3"] for r in res.results]

    shared_d = host_prep_d(inputs)
    nc_d = build_phase_d()
    res = run_bass_kernel_spmd(
        nc_d, [dict(shared_d, x3=xs[c]) for c in cores], core_ids=cores)
    return np.stack([r["x4"] for r in res.results])
